# version 08 (final)
# baseline (speedup 1.0000x reference)
import math
from contextlib import ExitStack

import numpy as np
import concourse.bass as bass
import concourse.mybir as mybir
from concourse.bass_utils import run_bass_kernel_spmd

F32 = mybir.dt.float32
BF16 = mybir.dt.bfloat16
AF = mybir.ActivationFunctionType
ALU = mybir.AluOpType
AX = mybir.AxisListType

D = 1024
KC = D // 128
NCORES = 8
LN_EPS = 1e-5


class Tok:
    __slots__ = ("sem", "val")
    def __init__(self, sem, val):
        self.sem = sem
        self.val = val


class Builder:
    ENGINES = ("t", "v", "s", "g", "d")

    def __init__(self, nc):
        self.nc = nc
        self.es = ExitStack()
        self.q = {e: [] for e in self.ENGINES}
        self.sem = {}
        self.cnt = {e: 0 for e in ("t", "v", "s", "g")}
        self.slots = {}
        for e in ("v", "s", "g", "t"):
            self.sem[e] = self.es.enter_context(nc.semaphore(f"seq_{e}"))
        self._waited = {e: {} for e in self.ENGINES}
        self._bump = {}

    def _emit_wait(self, e, sem, val, key):
        if val <= 0:
            return
        w = self._waited[e]
        if w.get(key, -1) >= val:
            return
        w[key] = val
        self.q[e].append(lambda eng, sem=sem, val=val: eng.wait_ge(sem, val))

    def wait(self, e, tok):
        if tok is None:
            return
        val = tok.val
        bump = self._bump.get(id(tok.sem))
        if bump is not None:
            val = bump[1] * 16
        self._emit_wait(e, tok.sem, val, id(tok.sem))

    def waits(self, e, toks):
        best = {}
        for tok in toks:
            if tok is None:
                continue
            k = id(tok.sem)
            if k not in best or best[k].val < tok.val:
                best[k] = tok
        for tok in best.values():
            self.wait(e, tok)

    def op(self, e, fn, waits=(), inc=True):
        self.waits(e, waits)
        if e != "t":
            self._emit_wait(e, self.sem[e], self.cnt[e], f"self_{e}")
        if inc:
            self.cnt[e] += 1
            sem = self.sem[e]
            self.q[e].append(lambda eng, fn=fn, sem=sem: fn(eng).then_inc(sem, 1))
            return Tok(sem, self.cnt[e])
        self.q[e].append(lambda eng, fn=fn: fn(eng))
        return Tok(self.sem[e], self.cnt[e])

    def mm(self, psum, lhsT, rhs, start, stop, waits=(), inc=False):
        return self.op(
            "t", lambda t, a=psum, b=lhsT, c=rhs, s1=start, s2=stop:
                t.matmul(a, b, c, start=s1, stop=s2),
            waits=waits, inc=inc)

    SLOT_ALIAS = {}
    for _n in ("gnl", "bia", "mixl", "b1l", "b2l", "gbl", "bol", "mu",
               "bqkv", "toepst"):
        SLOT_ALIAS[_n] = "cfg"
    for _n in ("c0", "c1", "c2", "idf", "mu2", "re", "mrep", "oht"):
        SLOT_ALIAS[_n] = "cfgg"
    for _i in range(4):
        SLOT_ALIAS[f"lnx{_i}"] = f"a{_i}"
        SLOT_ALIAS[f"yst{_i}"] = f"a{_i}"
        SLOT_ALIAS[f"rox{_i}"] = f"b{_i}"
        SLOT_ALIAS[f"kst{_i}"] = f"b{_i}"
    for _i in range(2):
        SLOT_ALIAS[f"qst{_i}"] = f"c{_i}"
        SLOT_ALIAS[f"pst{_i}"] = f"c{_i}"
        SLOT_ALIAS[f"oast{_i}"] = f"c{_i}"
        SLOT_ALIAS[f"hst{_i}"] = f"c{_i}"
    for _n in ("lnst", "mixst", "ctxst"):
        SLOT_ALIAS[_n] = "st5"
    for _n in ("lnrow", "arow", "nrmrow", "rwst"):
        SLOT_ALIAS[_n] = "row"
    for _n in ("lnb", "arep", "nrmrep", "rwrep"):
        SLOT_ALIAS[_n] = "repl"
    for _n in ("mixld", "lnxa", "nrmld", "dwl"):
        SLOT_ALIAS[_n] = "ld0"
    for _n in ("nrmld2", "dwl2"):
        SLOT_ALIAS[_n] = "ld1"
    SLOT_ALIAS["kh"] = "kt0"
    SLOT_ALIAS["vh"] = "vt0"

    def slot_sem(self, slot):
        slot = self.SLOT_ALIAS.get(slot, slot)
        if slot not in self.slots:
            self.slots[slot] = [self.es.enter_context(self.nc.semaphore(f"sl_{slot}")), 0]
            if slot in ("cfg", "cfgg"):
                self._bump[id(self.slots[slot][0])] = self.slots[slot]
        return self.slots[slot]

    def dma(self, qn, out, in_, waits=(), slot=None, **kw):
        assert slot is not None
        self.waits(qn, waits)
        ss = self.slot_sem(slot)
        ss[1] += 1
        self.q[qn].append(
            lambda eng, out=out, in_=in_, sem=ss[0], kw=kw:
                eng.dma_start(out, in_, **kw).then_inc(sem, 16))
        return Tok(ss[0], ss[1] * 16)

    def tok(self, e):
        return Tok(self.sem[e], self.cnt[e])

    def barrier(self):
        toks = [Tok(self.sem[e], self.cnt[e]) for e in ("t", "v", "s", "g")]
        toks += [Tok(ss[0], ss[1] * 16) for ss in self.slots.values()]
        for e in self.ENGINES:
            for tk in toks:
                self.wait(e, tk)

    def finalize(self):
        nc = self.nc
        with nc.Block() as blk:
            @blk.gpsimd
            def _(eng):
                for fn in self.q["g"]:
                    fn(eng)
            @blk.tensor
            def _(eng):
                for fn in self.q["t"]:
                    fn(eng)
            @blk.vector
            def _(eng):
                for fn in self.q["v"]:
                    fn(eng)
            @blk.scalar
            def _(eng):
                for fn in self.q["s"]:
                    fn(eng)
            @blk.sync
            def _(eng):
                for fn in self.q["d"]:
                    fn(eng)
        self.es.close()


class Ctx:
    def __init__(self, nc, T):
        self.nc = nc
        self.T = T
        self.W = nc.alloc_sbuf_tensor("W", [128, 65536], BF16)
        self.Wf = self.W.bitcast(F32)
        self.RT0 = nc.alloc_sbuf_tensor("RT0", [128, 8192], BF16)
        self.RT1 = nc.alloc_sbuf_tensor("RT1", [128, 8192], BF16)
        self.F1 = nc.alloc_sbuf_tensor("F1", [128, 4096], F32)
        self.H1 = nc.alloc_sbuf_tensor("H1", [128, 4224], BF16)
        self.H2 = nc.alloc_sbuf_tensor("H2", [128, 4224], BF16)
        self.SM = nc.alloc_sbuf_tensor("SM", [128, 2048], F32)
        self.SMb = self.SM.bitcast(BF16)
        self.PS = nc.alloc_psum_tensor("PS", [128, 4096], F32)
        self.PSb = self.PS.bitcast(BF16)
        self.rows_dram = nc.dram_tensor("rows_dram", [8, 4096], F32,
                                        kind="Internal").ap()
        self.ones_col = None
        self.ones_row = None
        self._cc = {}

    def const_col(self, bld, value):
        if value not in self._cc:
            i = len(self._cc)
            assert i < 60
            sl = self.SM[:, 1900 + i: 1901 + i]
            tok = bld.op("g", lambda g, sl=sl, value=value: g.memset(sl, value))
            self._cc[value] = (sl, tok)
        return self._cc[value]

    def bank(self, b, w=512):
        return self.PS[:, b * 512: b * 512 + w]

    def dram(self, name, shape, dtype):
        if not hasattr(self, "_dr"):
            self._dr = {}
        if name not in self._dr:
            self._dr[name] = self.nc.dram_tensor(name, list(shape), dtype,
                                                 kind="Internal").ap()
        return self._dr[name]


def ttiles(T, n=512):
    return [(i, min(n, T - i)) for i in range(0, T, n)]



def emit_consts(bld, cx, ones_hbm, ident_hbm):
    oc = cx.SMb[:, 4094:4095]
    orow = cx.SMb[0:1, 3968:4096]
    ident = cx.SMb[:, 3840:3968]
    t0 = bld.dma("g", oc, bass.AP(ones_hbm.tensor, 0, [[0, 128], [1, 1]]), slot="c0")
    bld.dma("g", orow, ones_hbm, waits=[t0], slot="c1")
    bld.dma("g", ident, ident_hbm, slot="c2")
    cx.ones_col = oc
    cx.ones_row = orow
    cx.ident = ident
    bld.barrier()


def emit_weight_load(bld, cx, wcol, hbm_ap, din, dout):
    nkc = din // 128
    W = cx.W
    tok = None
    for k0 in range(0, nkc, 8):
        kn = min(8, nkc - k0)
        tok = bld.dma(
            "g", bass.AP(W, wcol + k0 * dout, [[W.shape[1], 128], [dout, kn], [1, dout]]),
            hbm_ap[k0 * 128:(k0 + kn) * 128, :].rearrange("(kc p) j -> p kc j", p=128),
            slot=f"w{wcol}")
    def sl(kc, mc, mw=128):
        return W[:, wcol + kc * dout + mc * mw: wcol + kc * dout + mc * mw + mw]
    return sl, tok


def emit_ln(bld, cx, x_hbm, xn_hbm, gb=None, res_hbm=None):
    T = cx.T
    emit_ln._radd = None
    Wf = cx.Wf
    MB = Wf[:, 12288:16384]
    RB = Wf[:, 16384:20480]
    stat_s = Wf[0:1, 12288:16384]
    stat_q = Wf[0:1, 16384:20480]
    r_mean = Wf[0:1, 20480:24576]
    r_rstd = Wf[0:1, 24576:28672]
    xc, xb, sq = cx.F1, cx.H1, cx.H2
    tts = ttiles(T)

    slot_cast = {}
    slot_mm = {}
    evac = {}
    for ti, (t0, tw) in enumerate(tts):
        bs, bq = 2 * (ti % 2), 2 * (ti % 2) + 1
        mmtok = None
        for c in range(KC):
            sl = c % 4
            ld = bld.dma("d", xc[:, sl * 512: sl * 512 + tw],
                         x_hbm[c * 128:(c + 1) * 128, t0:t0 + tw],
                         waits=[slot_cast.get(sl)], slot=f"lnx{sl}")
            cast = bld.op("s", lambda s, sl=sl, tw=tw: s.copy(
                xb[:, sl * 512: sl * 512 + tw], xc[:, sl * 512: sl * 512 + tw]),
                waits=[ld, slot_mm.get(sl)])
            sqt = bld.op("s", lambda s, sl=sl, tw=tw: s.activation(
                sq[:, sl * 512: sl * 512 + tw], xb[:, sl * 512: sl * 512 + tw],
                AF.Square))
            slot_cast[sl] = sqt
            w0 = [cast] + ([evac.get(ti % 2)] if c == 0 and evac.get(ti % 2) else [])
            bld.mm(cx.PS[0:1, bs * 512: bs * 512 + tw], cx.ones_col,
                   xb[:, sl * 512: sl * 512 + tw],
                   start=(c == 0), stop=(c == KC - 1), waits=w0)
            mmtok = bld.mm(cx.PS[0:1, bq * 512: bq * 512 + tw], cx.ones_col,
                           sq[:, sl * 512: sl * 512 + tw],
                           start=(c == 0), stop=(c == KC - 1), waits=[sqt],
                           inc=True)
            slot_mm[sl] = mmtok
        e1 = bld.op("v", lambda v, t0=t0, tw=tw, bs=bs: v.tensor_copy(
            stat_s[:, t0:t0 + tw], cx.PS[0:1, bs * 512: bs * 512 + tw]),
            waits=[mmtok])
        e2 = bld.op("v", lambda v, t0=t0, tw=tw, bq=bq: v.tensor_copy(
            stat_q[:, t0:t0 + tw], cx.PS[0:1, bq * 512: bq * 512 + tw]))
        evac[ti % 2] = e2
    bld.op("v", lambda v: v.tensor_scalar_mul(r_mean[:, 0:T], stat_s[:, 0:T], 1.0 / D))
    bld.op("v", lambda v: v.tensor_scalar_mul(stat_q[:, 0:T], stat_q[:, 0:T], 1.0 / D))
    bld.op("v", lambda v: v.tensor_mul(stat_s[:, 0:T], r_mean[:, 0:T], r_mean[:, 0:T]))
    vv = bld.op("v", lambda v: v.tensor_sub(stat_q[:, 0:T], stat_q[:, 0:T], stat_s[:, 0:T]))
    epsc, epst = cx.const_col(bld, LN_EPS)
    lg = bld.op("s", lambda s: s.activation(r_rstd[:, 0:T], stat_q[:, 0:T], AF.Ln,
                                            bias=epsc[0:1, 0:1]), waits=[vv, epst])
    rst = bld.op("s", lambda s: s.activation(r_rstd[:, 0:T], r_rstd[:, 0:T], AF.Exp,
                                             scale=-0.5))
    d1 = bld.dma("d", cx.rows_dram[0:1, 0:T], r_mean[:, 0:T], waits=[bld.tok("v")], slot="lnrow")
    d2 = bld.dma("d", cx.rows_dram[1:2, 0:T], r_rstd[:, 0:T], waits=[rst], slot="lnrow")
    rd = cx.rows_dram.tensor
    b1 = bld.dma("d", MB[:, 0:T], bass.AP(rd, 0, [[0, 128], [1, T]]), waits=[d1, d2], slot="lnb")
    b2 = bld.dma("d", RB[:, 0:T], bass.AP(rd, 4096, [[0, 128], [1, T]]), slot="lnb")
    a1 = a2 = st = rl = None
    H2f = cx.H2.bitcast(F32)
    for c in range(KC):
        ld = bld.dma("d", xc[:, 0:T], x_hbm[c * 128:(c + 1) * 128, :],
                     waits=[a1 if a1 is not None else bld.tok("s")], slot="lnxa")
        a1 = bld.op("v", lambda v: v.tensor_sub(xc[:, 0:T], xc[:, 0:T], MB[:, 0:T]),
                    waits=[ld, b1, b2])
        if gb is None:
            a2 = bld.op("v", lambda v: v.tensor_tensor(
                xb[:, 0:T], xc[:, 0:T], RB[:, 0:T], ALU.mult),
                waits=[st] if st is not None else ())
            st = bld.dma("d", xn_hbm[c * 128:(c + 1) * 128, :], xb[:, 0:T],
                         waits=[a2], slot="lnst")
        else:
            gcol, bcol, gtok = gb
            a15 = bld.op("v", lambda v: v.tensor_tensor(
                xc[:, 0:T], xc[:, 0:T], RB[:, 0:T], ALU.mult))
            for hh in range(2):
                h0 = hh * (T // 2)
                hw = T // 2
                zt = H2f[:, 0:hw]
                a2 = bld.op("v", lambda v, h0=h0, hw=hw, zt=zt, c=c: v.tensor_scalar(
                    zt, xc[:, h0:h0 + hw], gcol[:, c:c + 1], bcol[:, c:c + 1],
                    ALU.mult, ALU.add), waits=[gtok, st, a15])
                if res_hbm is not None:
                    rl = bld.dma("d", cx.H1.bitcast(F32)[:, 0:hw],
                                 res_hbm[c * 128:(c + 1) * 128, h0:h0 + hw],
                                 waits=[getattr(emit_ln, "_radd", None)], slot="lnres")
                    a2 = bld.op("v", lambda v, zt=zt, hw=hw: v.tensor_add(
                        zt, zt, cx.H1.bitcast(F32)[:, 0:hw]), waits=[rl])
                    emit_ln._radd = a2
                st = bld.dma("d", xn_hbm[c * 128:(c + 1) * 128, h0:h0 + hw], zt,
                             waits=[a2], slot="lnst")
    bld.barrier()


def emit_proj(bld, cx, w_sl, w_tok, rhs_hbm, dout, epilogue, rt, extra_streams=()):
    T = cx.T
    nmc = dout // 128
    nb = 8 if nmc >= 8 else max(2, nmc)
    bank_free = [None] * nb
    tts = ttiles(T)
    half = [rt[:, 0:4096], rt[:, 4096:8192]]
    rhs_toks = [None, None]
    for ti, (t0, tw) in enumerate(tts):
        hb = ti % 2
        war = [bank_free[i] for i in range(nb)] if ti >= 2 else []
        rhs_toks[hb] = bld.dma(
            "d", bass.AP(rt, hb * 4096, [[8192, 128], [512, KC], [1, tw]]),
            rhs_hbm[:, t0:t0 + tw].rearrange("(kc p) t -> p kc t", p=128),
            waits=[w for w in war if w is not None], slot=f"rta{hb}")
        extras = []
        for (eh,) in extra_streams:
            extras.append(bld.dma(
                "d", bass.AP(cx.RT1, hb * 4096, [[8192, 128], [512, KC], [1, tw]]),
                eh[:, t0:t0 + tw].rearrange("(kc p) t -> p kc t", p=128),
                waits=[w for w in war if w is not None], slot=f"rtb{hb}"))
        for mc in range(nmc):
            bi = mc % nb
            w0 = [w_tok, rhs_toks[hb]] + extras
            if bank_free[bi] is not None:
                w0.append(bank_free[bi])
            pslice = cx.bank(bi, tw)
            for kc in range(KC):
                tok = bld.mm(pslice, w_sl(kc, mc),
                             rt[:, hb * 4096 + kc * 512: hb * 4096 + kc * 512 + tw],
                             start=(kc == 0), stop=(kc == KC - 1),
                             waits=w0 if kc == 0 else (), inc=(kc == KC - 1))
            bank_free[bi] = epilogue(mc, t0, tw, pslice, tok, hb)
    bld.barrier()


def emit_l1ffn(bld, cx, x_hbm, xout_hbm, wts, scr):
    T = cx.T
    emit_ln(bld, cx, x_hbm, scr["xn"])

    mu_r, mu_k = cx.SM[:, 0:8], cx.SM[:, 8:16]
    mtoks = []
    for c in range(KC):
        mtoks.append(bld.dma("d", mu_r[:, c:c + 1], wts["mu_r"][c * 128:(c + 1) * 128, 0:1], slot="mu"))
        mtoks.append(bld.dma("d", mu_k[:, c:c + 1], wts["mu_k"][c * 128:(c + 1) * 128, 0:1], slot="mu"))
    xnc, out_b, t1 = cx.H1, cx.H2, cx.F1
    a3 = st = None
    for c in range(KC):
        z = bld.op("g", lambda g: g.memset(xnc[:, 0:1], 0.0),
                   waits=[a3] if a3 is not None else ())
        ld = bld.dma("d", xnc[:, 1:T + 1], scr["xn"][c * 128:(c + 1) * 128, :],
                     waits=[a3] if a3 is not None else (), slot="mixld")
        for mu, dst in ((mu_r, scr["xr"]), (mu_k, scr["xk"])):
            a1 = bld.op("v", lambda v, mu=mu, c=c: v.tensor_scalar(
                t1[:, 0:T], xnc[:, 0:T], mu[:, c:c + 1], None, ALU.mult),
                waits=[ld, z] + mtoks)
            a2 = bld.op("v", lambda v: v.tensor_sub(t1[:, 0:T], xnc[:, 0:T], t1[:, 0:T]))
            a3 = bld.op("v", lambda v, mu=mu, c=c: v.scalar_tensor_tensor(
                out_b[:, 0:T], xnc[:, 1:T + 1], mu[:, c:c + 1], t1[:, 0:T],
                ALU.mult, ALU.add), waits=[st] if st is not None else ())
            st = bld.dma("d", dst[c * 128:(c + 1) * 128, :], out_b[:, 0:T], waits=[a3], slot="mixst")
    bld.barrier()

    wk_sl, wk_tok = emit_weight_load(bld, cx, 0, wts["WkT"], D, D)
    kstage = cx.H1
    kst_tok = {}

    def ep_k(mc, t0, tw, pslice, tok, hb):
        slot = (mc % 4)
        sl = kstage[:, slot * 1024: slot * 1024 + tw]
        sl2 = kstage[:, slot * 1024 + 512: slot * 1024 + 512 + tw]
        war = [kst_tok[slot]] if slot in kst_tok else []
        a1 = bld.op("v", lambda v: v.tensor_scalar_max(sl, pslice, 0.0),
                    waits=[tok] + war)
        a2 = bld.op("v", lambda v: v.tensor_tensor(sl2, sl, sl, ALU.mult))
        kst_tok[slot] = bld.dma("d", scr["kact"][mc * 128:(mc + 1) * 128, t0:t0 + tw],
                                sl2, waits=[a2], slot=f"kst{slot}")
        return a1
    emit_proj(bld, cx, wk_sl, wk_tok, scr["xk"], D, ep_k, cx.RT0)

    wr_sl, wr_tok = emit_weight_load(bld, cx, 8192, wts["WrT"], D, D)
    wv_sl, wv_tok = emit_weight_load(bld, cx, 16384, wts["WvT"], D, D)
    H2f = cx.H2.bitcast(F32)
    tts = ttiles(T)
    bank_free = [None] * 8
    rhs_toks = [None, None]
    ka_toks = [None, None]
    prev = {}
    for ti, (t0, tw) in enumerate(tts):
        hb = ti % 2
        war = [b for b in bank_free if b is not None] if ti >= 2 else []
        rhs_toks[hb] = bld.dma(
            "d", bass.AP(cx.RT0, hb * 4096, [[8192, 128], [512, KC], [1, tw]]),
            scr["xr"][:, t0:t0 + tw].rearrange("(kc p) t -> p kc t", p=128), waits=war,
            slot=f"rta{hb}")
        ka_toks[hb] = bld.dma(
            "d", bass.AP(cx.RT1, hb * 4096, [[8192, 128], [512, KC], [1, tw]]),
            scr["kact"][:, t0:t0 + tw].rearrange("(kc p) t -> p kc t", p=128), waits=war,
            slot=f"rtb{hb}")
        for mc in range(KC):
            br, bo = (2 * mc) % 8, (2 * mc + 1) % 8
            sx = (2 * ti + mc) % 4
            sr = mc % 2
            xl = bld.dma("d", cx.F1[:, sx * 1024: sx * 1024 + tw],
                         x_hbm[mc * 128:(mc + 1) * 128, t0:t0 + tw],
                         waits=[prev.get(("x", sx))], slot=f"rox{sx}")
            w0 = [wr_tok, rhs_toks[hb]]
            if bank_free[br] is not None:
                w0.append(bank_free[br])
            for kc in range(KC):
                rtk = bld.mm(cx.bank(br, tw), wr_sl(kc, mc),
                             cx.RT0[:, hb * 4096 + kc * 512: hb * 4096 + kc * 512 + tw],
                             start=(kc == 0), stop=(kc == KC - 1),
                             waits=w0 if kc == 0 else (), inc=(kc == KC - 1))
            w1 = [wv_tok, ka_toks[hb]]
            if bank_free[bo] is not None:
                w1.append(bank_free[bo])
            for kc in range(KC):
                otk = bld.mm(cx.bank(bo, tw), wv_sl(kc, mc),
                             cx.RT1[:, hb * 4096 + kc * 512: hb * 4096 + kc * 512 + tw],
                             start=(kc == 0), stop=(kc == KC - 1),
                             waits=w1 if kc == 0 else (), inc=(kc == KC - 1))
            rbuf = H2f[:, 1024 + sr * 512: 1024 + sr * 512 + tw]
            stage = H2f[:, sr * 512: sr * 512 + tw]
            s1 = bld.op("s", lambda s, rbuf=rbuf, br=br, tw=tw: s.activation(
                rbuf, cx.bank(br, tw), AF.Sigmoid),
                waits=[rtk, prev.get(("r", sr))])
            m1 = bld.op("v", lambda v, stage=stage, rbuf=rbuf, bo=bo, tw=tw:
                        v.tensor_tensor(stage, rbuf, cx.bank(bo, tw), ALU.mult),
                        waits=[otk, s1, prev.get(("st", sr))])
            m2 = bld.op("v", lambda v, stage=stage, sx=sx, tw=tw: v.tensor_add(
                stage, stage, cx.F1[:, sx * 1024: sx * 1024 + tw]), waits=[xl])
            stok = bld.dma("d", xout_hbm[mc * 128:(mc + 1) * 128, t0:t0 + tw],
                           stage, waits=[m2], slot=f"yst{sr}")
            prev[("r", sr)] = m1
            prev[("st", sr)] = stok
            prev[("x", sx)] = m2
            bank_free[br] = m1
            bank_free[bo] = m1
    bld.barrier()


def load_bias_cols(bld, cx, smcol, hbm_ap, slot):
    toks = []
    for c in range(KC):
        toks.append(bld.dma("d", cx.SM[:, smcol + c: smcol + c + 1],
                            hbm_ap[c * 128:(c + 1) * 128, 0:1], slot=slot))
    return toks[-1]


def emit_l0aa(bld, cx, x_hbm, xout_hbm, wts, scr):
    T = cx.T
    emit_ln(bld, cx, x_hbm, scr["xn"])

    wi_sl, wi_tok = emit_weight_load(bld, cx, 0, wts["WiT"], D, D)
    wg_sl, wg_tok = emit_weight_load(bld, cx, 8192, wts["WgT"], D, D)
    bi_tok = load_bias_cols(bld, cx, 0, wts["bi"], "bia")
    bg_tok = load_bias_cols(bld, cx, 8, wts["bg"], "bia")
    H2f = cx.H2.bitcast(F32)
    tts = ttiles(T)
    rhs_war = {}
    hstores = {}
    scan_cst = {}
    evac_done = {}
    sc_prev = None
    sc_prev_w = 512
    for mc in range(KC):
        for ti, (t0, tw) in enumerate(tts):
            hb = ti % 2
            bu, bg_ = 2 * hb, 2 * hb + 1
            rhs_tok = bld.dma(
                "d", bass.AP(cx.RT0, hb * 4096, [[8192, 128], [512, KC], [1, tw]]),
                scr["xn"][:, t0:t0 + tw].rearrange("(kc p) t -> p kc t", p=128),
                waits=[rhs_war.get(hb)], slot=f"rta{hb}")
            w0 = [wi_tok, wg_tok, bi_tok, bg_tok, rhs_tok] + list(evac_done.get(hb, ()))
            for kc in range(KC):
                bld.mm(cx.bank(bu, tw), wi_sl(kc, mc),
                       cx.RT0[:, hb * 4096 + kc * 512: hb * 4096 + kc * 512 + tw],
                       start=(kc == 0), stop=(kc == KC - 1),
                       waits=w0 if kc == 0 else ())
                gtk = bld.mm(cx.bank(bg_, tw), wg_sl(kc, mc),
                             cx.RT0[:, hb * 4096 + kc * 512: hb * 4096 + kc * 512 + tw],
                             start=(kc == 0), stop=(kc == KC - 1),
                             inc=(kc == KC - 1))
            rhs_war[hb] = gtk
            gt = H2f[:, 512:512 + tw]
            ut = H2f[:, 0:tw]
            bt = H2f[:, 1024:1024 + tw]
            sg = bld.op("s", lambda s, gt=gt, bg_=bg_, mc=mc, tw=tw: s.activation(
                gt, cx.bank(bg_, tw), AF.Sigmoid, bias=cx.SM[:, 8 + mc: 9 + mc]),
                waits=[gtk, sc_prev])
            uu = bld.op("v", lambda v, ut=ut, bu=bu, mc=mc, tw=tw: v.tensor_scalar_add(
                ut, cx.bank(bu, tw), cx.SM[:, 0 + mc: 1 + mc]), waits=[gtk])
            m1 = bld.op("v", lambda v, bt=bt, gt=gt, ut=ut: v.tensor_tensor(
                bt, gt, ut, ALU.mult), waits=[sg])
            m2 = bld.op("v", lambda v, bt=bt, ut=ut: v.tensor_sub(bt, ut, bt))
            evac_done[hb] = (sg, uu)
            hsl = cx.F1[:, hb * 512: hb * 512 + tw]
            if sc_prev is None:
                sc = bld.op("v", lambda v, hsl=hsl, gt=gt, bt=bt: v.tensor_tensor_scan(
                    hsl, gt, bt, 0.0, ALU.mult, ALU.add),
                    waits=[scan_cst.get(hb)])
            else:
                carry = cx.F1[:, (1 - hb) * 512 + sc_prev_w - 1: (1 - hb) * 512 + sc_prev_w]
                sc = bld.op("v", lambda v, hsl=hsl, gt=gt, bt=bt, carry=carry:
                            v.tensor_tensor_scan(hsl, gt, bt, carry, ALU.mult, ALU.add),
                            waits=[scan_cst.get(hb)])
            sc_prev, sc_prev_w = sc, tw
            hbuf = cx.H1[:, hb * 512: hb * 512 + tw]
            cstt = bld.op("s", lambda s, hbuf=hbuf, hsl=hsl: s.copy(hbuf, hsl),
                          waits=[sc, hstores.get(hb)])
            scan_cst[hb] = cstt
            hstores[hb] = bld.dma(
                "d", scr["b1"][mc * 128:(mc + 1) * 128, t0:t0 + tw], hbuf,
                waits=[cstt], slot=f"hst{hb}")
    bld.barrier()

    wo_sl, wo_tok = emit_weight_load(bld, cx, 16384, wts["WoT"], D, D)
    bo_tok = load_bias_cols(bld, cx, 16, wts["bo"], "bia")
    prev = {}

    def ep_o(mc, t0, tw, pslice, tok, hb):
        sx = (mc % 2) + 2 * hb
        xl = bld.dma("d", cx.F1[:, sx * 1024: sx * 1024 + tw],
                     x_hbm[mc * 128:(mc + 1) * 128, t0:t0 + tw],
                     waits=[prev.get(("x", sx))], slot=f"rox{sx}")
        stage = cx.H2.bitcast(F32)[:, sx * 512: sx * 512 + tw]
        a1 = bld.op("v", lambda v, stage=stage, tw=tw: v.tensor_scalar_add(
            stage, pslice, cx.SM[:, 16 + mc: 17 + mc]),
            waits=[tok, bo_tok, prev.get(("st", sx))])
        a2 = bld.op("v", lambda v, stage=stage, sx=sx, tw=tw: v.tensor_add(
            stage, stage, cx.F1[:, sx * 1024: sx * 1024 + tw]), waits=[xl])
        stok = bld.dma("d", xout_hbm[mc * 128:(mc + 1) * 128, t0:t0 + tw],
                       stage, waits=[a2], slot=f"yst{sx}")
        prev[("x", sx)] = a2
        prev[("st", sx)] = stok
        return a1
    emit_proj(bld, cx, wo_sl, wo_tok, scr["b1"], D, ep_o, cx.RT0)


def make_res_epilogue(bld, cx, x_hbm, xout_hbm, bias_smcol=None, bias_tok=None):
    prev = {}

    def ep(mc, t0, tw, pslice, tok, hb):
        sx = (mc % 2) + 2 * hb
        xl = bld.dma("d", cx.F1[:, sx * 1024: sx * 1024 + tw],
                     x_hbm[mc * 128:(mc + 1) * 128, t0:t0 + tw],
                     waits=[prev.get(("x", sx))], slot=f"rox{sx}")
        stage = cx.H2.bitcast(F32)[:, sx * 512: sx * 512 + tw]
        if bias_smcol is not None:
            a1 = bld.op("v", lambda v, stage=stage, mc=mc: v.tensor_scalar_add(
                stage, pslice, cx.SM[:, bias_smcol + mc: bias_smcol + mc + 1]),
                waits=[tok, bias_tok, prev.get(("st", sx))])
        else:
            a1 = bld.op("v", lambda v, stage=stage: v.tensor_copy(stage, pslice),
                        waits=[tok, prev.get(("st", sx))])
        a2 = bld.op("v", lambda v, stage=stage, sx=sx, tw=tw: v.tensor_add(
            stage, stage, cx.F1[:, sx * 1024: sx * 1024 + tw]), waits=[xl])
        stok = bld.dma("d", xout_hbm[mc * 128:(mc + 1) * 128, t0:t0 + tw],
                       stage, waits=[a2], slot=f"yst{sx}")
        prev[("x", sx)] = a2
        prev[("st", sx)] = stok
        return a1
    return ep


def emit_qkv_pass(bld, cx, rhs_hbm, wT_hbm, b_hbm, dst_hbm, rope_sign, cs):
    T = cx.T
    w_sl, w_tok = emit_weight_load(bld, cx, 0, wT_hbm, D, D)
    b_tok = load_bias_cols(bld, cx, 24, b_hbm, "bqkv") if b_hbm is not None else None
    RTf = cx.RT1.bitcast(F32)
    H2f = cx.H2.bitcast(F32)
    tts = ttiles(T)
    bank_free = [None] * 8
    rhs_war = {}
    cs_war = None
    st_war = {}
    rope = rope_sign is not None
    for ti, (t0, tw) in enumerate(tts):
        hb = ti % 2
        rhs_tok = bld.dma(
            "d", bass.AP(cx.RT0, hb * 4096, [[8192, 128], [512, KC], [1, tw]]),
            rhs_hbm[:, t0:t0 + tw].rearrange("(kc p) t -> p kc t", p=128),
            waits=[rhs_war.get(hb)], slot=f"rta{hb}")
        ctoks = []
        if rope:
            ctoks.append(bld.dma(
                "d", bass.AP(RTf, 0, [[RTf.shape[1], 128], [512, 4], [1, tw]]),
                cs[0][:, t0:t0 + tw].rearrange("(j p) t -> p j t", p=128),
                waits=[cs_war], slot="cs0"))
            ctoks.append(bld.dma(
                "d", bass.AP(RTf, 2048, [[RTf.shape[1], 128], [512, 4], [1, tw]]),
                cs[1][:, t0:t0 + tw].rearrange("(j p) t -> p j t", p=128),
                waits=[cs_war], slot="cs1"))
        mm_last = None
        for mc in range(KC):
            w0 = [w_tok, b_tok, rhs_tok, bank_free[mc]]
            for kc in range(KC):
                tok = bld.mm(cx.bank(mc, tw), w_sl(kc, mc),
                             cx.RT0[:, hb * 4096 + kc * 512: hb * 4096 + kc * 512 + tw],
                             start=(kc == 0), stop=(kc == KC - 1),
                             waits=w0 if kc == 0 else (), inc=(kc == KC - 1))
            mm_last = tok
            rhs_war[hb] = tok
            if not rope:
                sl = mc % 2
                if b_hbm is not None:
                    ev = bld.op("v", lambda v, sl=sl, mc=mc, tw=tw: v.tensor_scalar(
                        cx.H1[:, sl * 512: sl * 512 + tw], cx.bank(mc, tw),
                        cx.SM[:, 24 + mc: 25 + mc], None, ALU.add),
                        waits=[tok, st_war.get(sl)])
                else:
                    ev = bld.op("v", lambda v, sl=sl, mc=mc, tw=tw: v.tensor_copy(
                        cx.H1[:, sl * 512: sl * 512 + tw], cx.bank(mc, tw)),
                        waits=[tok, st_war.get(sl)])
                st_war[sl] = bld.dma(
                    "d", dst_hbm[mc * 128:(mc + 1) * 128, t0:t0 + tw],
                    cx.H1[:, sl * 512: sl * 512 + tw], waits=[ev], slot=f"qst{sl}")
                bank_free[mc] = ev
            else:
                ev = bld.op("v", lambda v, mc=mc, tw=tw: v.tensor_scalar(
                    cx.F1[:, mc * 512: mc * 512 + tw], cx.bank(mc, tw),
                    cx.SM[:, 24 + mc: 25 + mc], None, ALU.add), waits=[tok])
                bank_free[mc] = ev
        if rope:
            for mc in range(KC):
                j4 = mc % 4
                par = mc + 4 if mc < 4 else mc - 4
                cos_t = RTf[:, j4 * 512: j4 * 512 + tw]
                sin_t = RTf[:, 2048 + j4 * 512: 2048 + j4 * 512 + tw]
                tmp1 = H2f[:, 0:tw]
                tmp2 = H2f[:, 512:512 + tw]
                sl = mc % 2
                m1 = bld.op("v", lambda v, tmp1=tmp1, mc=mc, cos_t=cos_t, tw=tw:
                            v.tensor_tensor(tmp1, cx.F1[:, mc * 512: mc * 512 + tw],
                                            cos_t, ALU.mult), waits=ctoks)
                m2 = bld.op("v", lambda v, tmp2=tmp2, par=par, sin_t=sin_t, tw=tw:
                            v.tensor_tensor(tmp2, cx.F1[:, par * 512: par * 512 + tw],
                                            sin_t, ALU.mult))
                if mc < 4:
                    m3 = bld.op("v", lambda v, sl=sl, tmp1=tmp1, tmp2=tmp2, tw=tw:
                                v.tensor_sub(cx.H1[:, sl * 512: sl * 512 + tw],
                                             tmp1, tmp2), waits=[st_war.get(sl)])
                else:
                    m3 = bld.op("v", lambda v, sl=sl, tmp1=tmp1, tmp2=tmp2, tw=tw:
                                v.tensor_add(cx.H1[:, sl * 512: sl * 512 + tw],
                                             tmp1, tmp2), waits=[st_war.get(sl)])
                st_war[sl] = bld.dma(
                    "d", dst_hbm[mc * 128:(mc + 1) * 128, t0:t0 + tw],
                    cx.H1[:, sl * 512: sl * 512 + tw], waits=[m3], slot=f"qst{sl}")
            cs_war = m3
    bld.barrier()


def emit_l1ab(bld, cx, x_hbm, xout_hbm, wts, scr, consts):
    T = cx.T
    WIN = 512
    emit_ln(bld, cx, x_hbm, scr["xn"])
    emit_qkv_pass(bld, cx, scr["xn"], wts["WqT"], wts["bq"], scr["xr"], +1,
                  (consts["cos"], consts["sin"]))
    emit_qkv_pass(bld, cx, scr["xn"], wts["WkT"], wts["bk"], scr["xk"], +1,
                  (consts["cos"], consts["sin"]))
    emit_qkv_pass(bld, cx, scr["xn"], wts["WvT"], wts["bv"], scr["kact"], None, None)

    mrow = consts["mask"]
    mtoks = []
    for idx in range(8):
        d0 = idx * 128 - 512
        off = 4096 + d0 - 511
        mtoks.append(bld.dma(
            "g", cx.RT0[:, idx * 1024:(idx + 1) * 1024],
            bass.AP(mrow.tensor, off, [[1, 128], [1, 1024]]), slot="mrep"))
    mtok = mtoks[-1]
    sc40, sc40t = cx.const_col(bld, -40.0)

    H2 = cx.H2
    nqb = T // WIN
    onesv = bld.op("g", lambda g: g.memset(
        bass.AP(cx.H2, 1024 + 64, [[cx.H2.shape[1], 128], [65, 16]]), 1.0))
    set_war = {}
    bank_war = {}
    for h in range(16):
        for qb in range(nqb):
            q0 = qb * WIN
            kv0 = max(0, q0 - WIN)
            kw = q0 + WIN - kv0
            nkc = kw // 128
            st_ = qb % 2
            base = st_ * 1536
            Qt = cx.H1[0:64, base + 1024: base + 1536]
            Kc = cx.H1[0:64, base: base + 1024]
            Vc = cx.H1[64:128, base: base + 1024]
            VT = None
            vt0 = 1024 + st_ * 520
            qtok = bld.dma("d", Qt[:, 0:WIN], scr["xr"][h * 64:(h + 1) * 64, q0:q0 + WIN],
                           waits=[set_war.get(("q", st_))], slot=f"qt{st_}")
            ktok = bld.dma("d", Kc[:, 0:kw], scr["xk"][h * 64:(h + 1) * 64, kv0:kv0 + kw],
                           waits=[set_war.get(("q", st_))], slot=f"kt{st_}")
            vtok = bld.dma("d", Vc[:, 0:kw], scr["kact"][h * 64:(h + 1) * 64, kv0:kv0 + kw],
                           waits=[set_war.get(("v", st_))], slot=f"vt{st_}")
            evt = None
            for j in range(nkc):
                pb = cx.PSb[:, (6 + j % 2) * 1024: (6 + j % 2) * 1024 + 64]
                trw = [vtok, bank_war.get(("tr", j % 2))]
                tr = bld.op(
                    "t", lambda t, pb=pb, j=j, Vc=Vc: t.transpose(
                        pb, Vc[:, j * 128:(j + 1) * 128], cx.ident[64:128, 64:128]),
                    waits=trw, inc=True)
                ev = bld.op("v", lambda v, pb=pb, j=j, vt0=vt0: v.tensor_copy(
                    H2[:, vt0 + j * 65: vt0 + j * 65 + 64], pb),
                    waits=[tr, set_war.get(("vt", st_)), onesv])
                bank_war[("tr", j % 2)] = ev
                evt = ev
            lastpv = None
            par = qb % 2
            pvc = 2048 + par * 512
            pend = []

            def pv_emit(ent, par=par, pvc=pvc, evt=evt, nkc=nkc, vt0=vt0):
                tkc, mkt, E2m = ent
                w = [mkt, evt, bank_war.get(("pv", par))] if tkc == 0 else [mkt]
                pv = bld.mm(cx.PS[0:65, pvc:pvc + WIN],
                            H2[:, vt0 + tkc * 65: vt0 + tkc * 65 + 65], E2m,
                            start=(tkc == 0), stop=(tkc == nkc - 1),
                            waits=w, inc=True)
                bank_war[("E2", tkc % 4)] = pv
                return pv

            for tkc in range(nkc):
                sE = tkc % 4
                idx = (kv0 + tkc * 128 - q0 + 512) // 128
                Et = cx.RT1[:, sE * 512: sE * 512 + WIN]
                E2 = cx.RT1[:, 2048 + sE * 512: 2048 + sE * 512 + WIN]
                smm = bld.mm(cx.bank(sE, WIN), Kc[:, tkc * 128:(tkc + 1) * 128],
                             Qt[:, 0:WIN], start=True, stop=True,
                             waits=[qtok, ktok, bank_war.get(("S", sE))], inc=True)
                ex = bld.op("s", lambda s, Et=Et, sE=sE: s.activation(
                    Et, cx.bank(sE, WIN), AF.Exp, bias=sc40, scale=0.125),
                    waits=[smm, sc40t, bank_war.get(("E", sE))])
                bank_war[("S", sE)] = ex
                mask_ap = bass.AP(cx.RT0, idx * 1024 + 511, [[8192, 128], [-1, WIN]])
                mk = bld.op("v", lambda v, E2=E2, Et=Et, mask_ap=mask_ap:
                            v.tensor_tensor(E2, Et, mask_ap, ALU.mult),
                            waits=[ex, mtok, bank_war.get(("E2", sE))])
                bank_war[("E", sE)] = mk
                pend.append((tkc, mk, E2))
                if len(pend) > 2:
                    lastpv = pv_emit(pend.pop(0))
            qt_war = bld.tok("t")
            while pend:
                lastpv = pv_emit(pend.pop(0))
            set_war[("q", st_)] = qt_war
            set_war[("vt", st_)] = lastpv
            set_war[("v", st_)] = bld.tok("t")
            srow = cx.SM[64:65, 1024:1024 + WIN]
            rc = bld.op("v", lambda v, srow=srow, pvc=pvc: v.reciprocal(
                srow, cx.PS[64:65, pvc:pvc + WIN]),
                waits=[lastpv, bank_war.get(("srow",))])
            rd = bld.dma("d", cx.rows_dram[2:3, 0:WIN], srow,
                         waits=[rc, bank_war.get(("rb",))], slot="arow")
            bank_war[("srow",)] = rd
            rep = cx.F1[0:64, 0:WIN]
            rb = bld.dma("d", rep, bass.AP(cx.rows_dram.tensor, 2 * 4096, [[0, 64], [1, WIN]]),
                         waits=[rd, bank_war.get(("rep",))], slot="arep")
            bank_war[("rb",)] = rb
            stg = H2[0:64, 3104: 3104 + WIN]
            cn = bld.op("v", lambda v, stg=stg, rep=rep, pvc=pvc: v.tensor_tensor(
                stg, cx.PS[0:64, pvc:pvc + WIN], rep, ALU.mult),
                waits=[rb, bank_war.get(("stg",))])
            bank_war[("pv", par)] = cn
            bank_war[("rep",)] = cn
            bank_war[("stg",)] = bld.dma(
                "d", scr["b1"][h * 64:(h + 1) * 64, q0:q0 + WIN], stg,
                waits=[cn], slot="ctxst")
    bld.barrier()

    wo_sl, wo_tok = emit_weight_load(bld, cx, 0, wts["WoT"], D, D)
    bo_tok = load_bias_cols(bld, cx, 24, wts["bo"], "bqkv")
    ep = make_res_epilogue(bld, cx, x_hbm, xout_hbm, 24, bo_tok)
    emit_proj(bld, cx, wo_sl, wo_tok, scr["b1"], D, ep, cx.RT1)


def emit_l1aa(bld, cx, x_hbm, xout_hbm, wts, scr, cconsts):
    T = cx.T
    emit_ln(bld, cx, x_hbm, scr["xn"])
    emit_qkv_pass(bld, cx, scr["xn"], wts["Wq"], None, scr["xr"], None, None)
    emit_qkv_pass(bld, cx, scr["xn"], wts["Wk"], None, scr["xk"], None, None)
    emit_qkv_pass(bld, cx, scr["xn"], wts["Wv"], None, scr["kact"], None, None)

    toep_dram = cx.dram("toep_dram", [16, 8192], BF16)
    re_sb = cx.SMb[0:32, 2560:2576]
    rtok = bld.dma("g", re_sb, wts["rel_emb"], slot="re")
    oh = cconsts["onehotT"]
    oht = cx.H1[0:32, 0:512]
    erow = cx.SMb[0:16, 2048:2560]
    mmt = etok = stok = None
    for i in range(16):
        otok = bld.dma("g", oht, oh[:, i * 512:(i + 1) * 512],
                       waits=[mmt], slot="oht")
        mmt = bld.mm(cx.PS[0:16, 0:512], re_sb, oht, start=True, stop=True,
                     waits=[rtok, otok, etok], inc=True)
        etok = bld.op("s", lambda s: s.activation(erow, cx.PS[0:16, 0:512], AF.Exp),
                      waits=[mmt, stok])
        stok = bld.dma("d", toep_dram[:, i * 512:(i + 1) * 512], erow,
                       waits=[etok], slot="toepst")
    bld.barrier()

    sc40, sc40t = cx.const_col(bld, -40.0)
    H2 = cx.H2
    ntq = T // 512
    nkc = T // 128
    U = 8064
    set_war = {}
    bank_war = {}
    onesv = bld.op("g", lambda g: g.memset(
        bass.AP(cx.H2, 1024 + 64, [[cx.H2.shape[1], 128], [65, nkc]]), 1.0))
    F1b = cx.F1.bitcast(BF16)
    for h in range(16):
        hs = h % 2
        KV = cx.H1 if hs == 0 else F1b
        Kh = KV[0:64, 0:T]
        Vh = KV[64:128, 0:T]
        ktok = bld.dma("d", Kh, scr["xk"][h * 64:(h + 1) * 64, :],
                       waits=[set_war.get(("kh", hs))], slot=f"kt{hs}")
        vtok = bld.dma("d", Vh, scr["kact"][h * 64:(h + 1) * 64, :],
                       waits=[set_war.get(("kh", hs))], slot=f"vt{hs}")
        reptok = bld.dma("g", cx.RT1[:, 0:U],
                         bass.AP(toep_dram.tensor, h * 8192, [[1, 128], [1, U]]),
                         waits=[set_war.get("rep")], slot="reph")
        evt = None
        for j in range(nkc):
            pb = cx.PSb[:, (6 + j % 2) * 1024: (6 + j % 2) * 1024 + 64]
            tr = bld.op("t", lambda t, pb=pb, j=j, Kh=Kh, Vh=Vh: t.transpose(
                pb, Vh[:, j * 128:(j + 1) * 128], cx.ident[64:128, 64:128]),
                waits=[vtok, bank_war.get(("tr", j % 2))], inc=True)
            evt = bld.op("v", lambda v, pb=pb, j=j: v.tensor_copy(
                H2[:, 1024 + j * 65: 1024 + j * 65 + 64], pb),
                waits=[tr, set_war.get("vt"), onesv])
            bank_war[("tr", j % 2)] = evt
        for tqt in range(ntq):
            q0 = tqt * 512
            par = tqt % 2
            pvc = 2048 + par * 512
            Qt = H2[0:64, 3136:3648]
            qtok = bld.dma("d", Qt, scr["xr"][h * 64:(h + 1) * 64, q0:q0 + 512],
                           waits=[set_war.get("qt")], slot="qt0")
            lastpv = None
            pend = []

            def pv_emit(ent, par=par, pvc=pvc, evt=evt, nkc=nkc):
                tkc, mkt, E2m = ent
                w = [mkt, evt, bank_war.get(("pv", par))] if tkc == 0 else [mkt]
                pv = bld.mm(cx.PS[0:65, pvc:pvc + 512],
                            H2[:, 1024 + tkc * 65: 1024 + tkc * 65 + 65], E2m,
                            start=(tkc == 0), stop=(tkc == nkc - 1),
                            waits=w, inc=True)
                bank_war[("E2", tkc % 4)] = pv
                return pv

            for tkc in range(nkc):
                sE = tkc % 4
                Et = cx.RT0[:, sE * 512: sE * 512 + 512]
                E2 = cx.RT0[:, 2048 + sE * 512: 2048 + sE * 512 + 512]
                smm = bld.mm(cx.bank(sE, 512), Kh[:, tkc * 128:(tkc + 1) * 128],
                             Qt, start=True, stop=True,
                             waits=[qtok, ktok, bank_war.get(("S", sE))], inc=True)
                ex = bld.op("s", lambda s, Et=Et, sE=sE: s.activation(
                    Et, cx.bank(sE, 512), AF.Exp, bias=sc40),
                    waits=[smm, sc40t, bank_war.get(("E", sE))])
                bank_war[("S", sE)] = ex
                C = 4095 + tkc * 128 - q0
                eap = bass.AP(cx.RT1, C, [[8192, 128], [-1, 512]])
                mk = bld.op("v", lambda v, E2=E2, Et=Et, eap=eap:
                            v.tensor_tensor(E2, Et, eap, ALU.mult),
                            waits=[ex, reptok, bank_war.get(("E2", sE))])
                bank_war[("E", sE)] = mk
                pend.append((tkc, mk, E2))
                if len(pend) > 2:
                    lastpv = pv_emit(pend.pop(0))
            qt_war = bld.tok("t")
            while pend:
                lastpv = pv_emit(pend.pop(0))
            set_war["qt"] = qt_war
            srow = cx.SM[64:65, 1024:1536]
            rc = bld.op("v", lambda v, srow=srow, pvc=pvc: v.reciprocal(
                srow, cx.PS[64:65, pvc:pvc + 512]),
                waits=[lastpv, bank_war.get(("srow",))])
            rd = bld.dma("d", cx.rows_dram[2:3, 0:512], srow,
                         waits=[rc, bank_war.get(("rb",))], slot="arow")
            bank_war[("srow",)] = rd
            rep = cx.H2.bitcast(F32)[0:64, 0:512]
            rb = bld.dma("d", rep,
                         bass.AP(cx.rows_dram.tensor, 2 * 4096, [[0, 64], [1, 512]]),
                         waits=[rd, bank_war.get(("rep",))], slot="arep")
            bank_war[("rb",)] = rb
            stg = H2[0:64, 3648:4160]
            cn = bld.op("v", lambda v, stg=stg, rep=rep, pvc=pvc: v.tensor_tensor(
                stg, cx.PS[0:64, pvc:pvc + 512], rep, ALU.mult),
                waits=[rb, bank_war.get(("stg",))])
            bank_war[("pv", par)] = cn
            bank_war[("rep",)] = cn
            bank_war[("stg",)] = bld.dma(
                "d", scr["b1"][h * 64:(h + 1) * 64, q0:q0 + 512], stg,
                waits=[cn], slot="ctxst")
        set_war[("kh", hs)] = bld.tok("t")
        set_war["vt"] = bank_war.get(("pv", (ntq - 1) % 2))
        set_war["rep"] = bld.tok("v")
    bld.barrier()

    wo_sl, wo_tok = emit_weight_load(bld, cx, 0, wts["Wo"], D, D)
    ep = make_res_epilogue(bld, cx, x_hbm, xout_hbm)
    emit_proj(bld, cx, wo_sl, wo_tok, scr["b1"], D, ep, cx.RT1)


def emit_l0ffn(bld, cx, x_hbm, xout_hbm, wts, scr):
    T = cx.T
    nc = cx.nc
    emit_ln(bld, cx, x_hbm, scr["xn"])
    acc_dram = cx.dram("acc_dram", [D, cx.T], F32)
    rw_dram = cx.dram("rw_dram", [8, 4096], F32)
    tts = ttiles(T)

    wr_sl, wr_tok = emit_weight_load(bld, cx, 0, wts["WrT"], D, 8)
    brc = cx.SM[0:8, 72:73]
    brt = bld.dma("d", brc, wts["br"][0:8, 0:1], slot="bia")
    idf = cx.SM[:, 1536:1664]
    idt = bld.dma("g", idf, wts["identf"], slot="idf")
    lgrow = cx.F1[0:8, 0:T]
    ev = None
    for ti, (t0, tw) in enumerate(tts):
        hb = ti % 2
        rhs_tok = bld.dma(
            "d", bass.AP(cx.RT0, hb * 4096, [[8192, 128], [512, KC], [1, tw]]),
            scr["xn"][:, t0:t0 + tw].rearrange("(kc p) t -> p kc t", p=128),
            waits=[Tok(bld.sem["t"], bld.cnt["t"]) if ti >= 2 else None],
            slot=f"rta{hb}")
        for kc in range(KC):
            mmt = bld.mm(cx.PS[0:8, 0:tw], wr_sl(kc, 0, 8),
                         cx.RT0[:, hb * 4096 + kc * 512: hb * 4096 + kc * 512 + tw],
                         start=(kc == 0), stop=(kc == KC - 1),
                         waits=[wr_tok, rhs_tok, ev] if kc == 0 else (),
                         inc=(kc == KC - 1))
        ev = bld.op("v", lambda v, t0=t0, tw=tw: v.tensor_scalar(
            lgrow[:, t0:t0 + tw], cx.PS[0:8, 0:tw], brc, None, ALU.add),
            waits=[mmt, brt])
    H2f = cx.H2.bitcast(F32)
    lg = H2f[:, 0:8]
    m1 = H2f[:, 8:9]
    m2 = H2f[:, 9:10]
    e1 = H2f[:, 10:18]
    e2 = H2f[:, 18:26]
    w1 = H2f[:, 26:27]
    t5 = H2f[:, 27:35]
    rwt = H2f[:, 35:43]
    prev_tb = prev_ev2 = None
    for i in range(T // 128):
        tf = bld.op("t", lambda t, i=i: t.transpose(
            cx.PS[0:128, 512:520], lgrow[:, i * 128:(i + 1) * 128], idf[0:8, 0:8]),
            waits=[ev, idt, prev_ev2], inc=True)
        evl = bld.op("v", lambda v: v.tensor_copy(lg, cx.PS[0:128, 512:520]),
                     waits=[tf, prev_tb])
        bld.op("v", lambda v: v.reduce_max(m1, lg, axis=AX.X))
        bld.op("v", lambda v: v.tensor_scalar(e1, lg, m1, None, ALU.is_equal))
        bld.op("v", lambda v: v.scalar_tensor_tensor(e2, e1, -1e30, lg, ALU.mult, ALU.add))
        bld.op("v", lambda v: v.reduce_max(m2, e2, axis=AX.X))
        bld.op("v", lambda v: v.tensor_scalar(e2, e2, m2, None, ALU.is_equal))
        vv = bld.op("v", lambda v: v.tensor_sub(w1, m1, m2))
        sg = bld.op("s", lambda s: s.activation(w1, w1, AF.Sigmoid), waits=[vv])
        bld.op("v", lambda v: v.tensor_scalar(rwt, e1, w1, None, ALU.mult), waits=[sg])
        bld.op("v", lambda v: v.tensor_scalar(t5, e2, w1, None, ALU.mult))
        bld.op("v", lambda v: v.tensor_sub(t5, e2, t5))
        ev2 = bld.op("v", lambda v: v.tensor_add(rwt, rwt, t5))
        prev_ev2 = ev2
        tb = bld.op("t", lambda t, i=i: t.transpose(
            cx.PS[0:8, 520:648], rwt, idf), waits=[ev2], inc=True)
        prev_tb = bld.op("v", lambda v, i=i: v.tensor_copy(
            lgrow[:, i * 128:(i + 1) * 128], cx.PS[0:8, 520:648]), waits=[tb])
    rwst = bld.dma("d", rw_dram[0:8, 0:T], lgrow, waits=[prev_tb], slot="rwst")
    acc_in = []
    for c in range(KC):
        acc_in.append(bld.dma("g", acc_dram[c * 128:(c + 1) * 128, :],
                              scr["xn"][c * 128:(c + 1) * 128, :], slot="acci"))
    bld.barrier()

    def ht_slice(c, tw):
        if c < 16:
            return cx.RT1[:, c * 512: c * 512 + tw]
        if c < 24:
            return cx.H1[:, (c - 16) * 512: (c - 16) * 512 + tw]
        return cx.H2[:, (c - 24) * 512: (c - 24) * 512 + tw]

    acc_last = [None, None]
    rhs_war = {}
    bank_war = {}
    rep_war = None
    for e in range(8):
        w1_sl, w1_tok = emit_weight_load(bld, cx, 0, wts["W1T"][e], D, 4096)
        w2_sl, w2_tok = emit_weight_load(bld, cx, 32768, wts["W2T"][e], 4096, 1024)
        b1c = cx.SM[:, 32:64]
        b2c = cx.SM[:, 64:72]
        b1t = bld.dma("d", b1c, wts["b1"][e].rearrange("(f p) one -> p (f one)", p=128),
                      waits=[bld.tok("v"), bld.tok("s")], slot="b1l",
                      allow_slow_non_contiguous=True)
        b2t = bld.dma("d", b2c, wts["b2"][e].rearrange("(f p) one -> p (f one)", p=128),
                      slot="b2l", allow_slow_non_contiguous=True)
        acc_first_wait = list(acc_last) if isinstance(acc_last, list) else [acc_last]
        for ti, (t0, tw) in enumerate(tts):
            hb = ti % 2
            rhs_tok = bld.dma(
                "d", bass.AP(cx.RT0, hb * 4096, [[8192, 128], [512, KC], [1, tw]]),
                scr["xn"][:, t0:t0 + tw].rearrange("(kc p) t -> p kc t", p=128),
                waits=[rhs_war.get(hb)], slot=f"rta{hb}")
            rep = cx.F1[:, 0:512]
            rpt = bld.dma("d", rep[:, 0:tw],
                          bass.AP(rw_dram.tensor, e * 4096 + t0, [[0, 128], [1, tw]]),
                          waits=[rwst, rep_war], slot="rwrep")
            for fm in range(32):
                bk = fm % 4
                w0 = [w1_tok, b1t, rhs_tok, bank_war.get(("A", bk))]
                for kc in range(KC):
                    tok = bld.mm(cx.bank(bk, tw), w1_sl(kc, fm),
                                 cx.RT0[:, hb * 4096 + kc * 512: hb * 4096 + kc * 512 + tw],
                                 start=(kc == 0), stop=(kc == KC - 1),
                                 waits=w0 if kc == 0 else (), inc=(kc == KC - 1))
                rhs_war[hb] = tok
                hsl = ht_slice(fm, tw)
                if fm % 2 == 0:
                    evh = bld.op("s", lambda s, hsl=hsl, bk=bk, fm=fm, tw=tw:
                                 s.activation(hsl, cx.bank(bk, tw), AF.Relu,
                                              bias=b1c[:, fm:fm + 1]),
                                 waits=[tok, bank_war.get(("h",))])
                else:
                    evh = bld.op("v", lambda v, hsl=hsl, bk=bk, fm=fm, tw=tw:
                                 v.tensor_scalar(hsl, cx.bank(bk, tw),
                                                 b1c[:, fm:fm + 1], 0.0,
                                                 ALU.add, ALU.max),
                                 waits=[tok, bank_war.get(("h",))])
                bank_war[("A", bk)] = evh
                hready = bank_war.setdefault(("hr",), {})
                hready[fm] = evh
            for dm in range(KC):
                bk = 4 + dm % 4
                w0 = [w2_tok, b2t, bank_war.get(("B", bk))]
                hready = bank_war[("hr",)]
                for kc in range(32):
                    tok = bld.mm(cx.bank(bk, tw), w2_sl(kc, dm),
                                 ht_slice(kc, tw),
                                 start=(kc == 0), stop=(kc == 31),
                                 waits=(w0 + [hready[kc]]) if kc == 0
                                 else [hready[kc]],
                                 inc=(kc == 31))
                stg = cx.F1[:, 512 + (dm % 2) * 512: 512 + (dm % 2) * 512 + tw]
                wo_ = bld.op("v", lambda v, stg=stg, bk=bk, dm=dm, tw=tw, rep=rep:
                             v.scalar_tensor_tensor(stg, cx.bank(bk, tw),
                                                    b2c[:, dm:dm + 1], rep[:, 0:tw],
                                                    ALU.add, ALU.mult),
                             waits=[tok, rpt, bank_war.get(("stg", dm % 2))])
                bank_war[("B", bk)] = wo_
                acc_last[dm % 2] = bld.dma(
                    "g", acc_dram[dm * 128:(dm + 1) * 128, t0:t0 + tw],
                    stg, waits=[wo_] + acc_first_wait + acc_in,
                    slot=f"accd{dm % 2}", accum_op=ALU.add)
                bank_war[("stg", dm % 2)] = acc_last[dm % 2]
            rep_war = bank_war[("B", 4 + (KC - 1) % 4)]
            bank_war[("h",)] = tok
    bld.barrier()

    gcol = cx.SM[:, 72:80]
    bcol = cx.SM[:, 80:88]
    g1 = load_bias_cols(bld, cx, 72, wts["gamma"], "gbl")
    g2 = load_bias_cols(bld, cx, 80, wts["beta"], "gbl")
    emit_ln(bld, cx, acc_dram, xout_hbm, gb=(gcol, bcol, g2), res_hbm=x_hbm)


def emit_mix(bld, cx, xn_hbm, mixes, T):
    xnc, out_b, t1 = cx.H1, cx.H2, cx.F1
    a3 = st = None
    for c in range(KC):
        z = bld.op("g", lambda g: g.memset(xnc[:, 0:1], 0.0),
                   waits=[a3] if a3 is not None else ())
        ld = bld.dma("d", xnc[:, 1:T + 1], xn_hbm[c * 128:(c + 1) * 128, :],
                     waits=[a3] if a3 is not None else (), slot="mixld")
        for (mcol, omcol, dst) in mixes:
            a1 = bld.op("v", lambda v, mcol=mcol, c=c: v.tensor_scalar(
                t1[:, 0:T], xnc[:, 0:T], cx.SM[:, mcol + c:mcol + c + 1], None,
                ALU.mult), waits=[ld, z])
            a3 = bld.op("v", lambda v, omcol=omcol, c=c: v.scalar_tensor_tensor(
                out_b[:, 0:T], xnc[:, 1:T + 1], cx.SM[:, omcol + c:omcol + c + 1],
                t1[:, 0:T], ALU.mult, ALU.add), waits=[st] if st is not None else ())
            st = bld.dma("d", dst[c * 128:(c + 1) * 128, :], out_b[:, 0:T],
                         waits=[a3], slot="mixst")
    bld.barrier()


def emit_l2norm_pass(bld, cx, s_hbm, T):
    nc = cx.nc
    nrm = cx.dram("nrm_dram", [16, 4224], F32)
    ac, sq = cx.H1, cx.H2
    ones2 = cx.SMb[:, 3448:3450]
    z0 = bld.op("g", lambda g: g.memset(ones2, 0.0))
    z1 = bld.op("g", lambda g: g.memset(
        bass.AP(cx.SMb, 3448, [[cx.SMb.shape[1], 64], [1, 1]]), 1.0))
    z2 = bld.op("g", lambda g: g.memset(
        bass.AP(cx.SMb, 64 * cx.SMb.shape[1] + 3449, [[cx.SMb.shape[1], 64], [1, 1]]), 1.0))
    st2 = cx.SM[0:2, 1024:1536]
    mmt = evn = None
    stt_ = None
    for c in range(KC):
        ld = bld.dma("d", ac[:, 0:T], s_hbm[c * 128:(c + 1) * 128, :],
                     waits=[mmt], slot="nrmld")
        sqt = bld.op("s", lambda s: s.activation(sq[:, 0:T], ac[:, 0:T], AF.Square),
                     waits=[ld, mmt])
        for (t0, tw) in ttiles(T):
            mmt = bld.mm(cx.PS[0:2, 0:tw], ones2, sq[:, t0:t0 + tw],
                         start=True, stop=True, waits=[sqt, z0, z1, z2, evn],
                         inc=True)
            evn = bld.op("v", lambda v, tw=tw: v.tensor_copy(st2[:, 0:tw],
                                                             cx.PS[0:2, 0:tw]),
                         waits=[mmt, stt_])
            stt_ = bld.dma("d", nrm[2 * c:2 * c + 2, t0:t0 + tw], st2[:, 0:tw],
                           waits=[evn], slot="nrmrow")
    ldn = bld.dma("d", cx.F1[0:16, 0:T], nrm[0:16, 0:T], waits=[stt_, bld.tok("v")],
                  slot="nrmld2")
    sq1 = bld.op("s", lambda s: s.activation(cx.F1[0:16, 0:T], cx.F1[0:16, 0:T],
                                             AF.Sqrt), waits=[ldn])
    mx = bld.op("v", lambda v: v.tensor_scalar_max(cx.F1[0:16, 0:T],
                                                   cx.F1[0:16, 0:T], 1e-6),
                waits=[sq1])
    rc = bld.op("v", lambda v: v.reciprocal(cx.F1[0:16, 0:T], cx.F1[0:16, 0:T]))
    sr = bld.dma("d", nrm[0:16, 0:T], cx.F1[0:16, 0:T], waits=[rc], slot="nrmrow")
    mul = stb = None
    for c in range(KC):
        ld = bld.dma("d", ac[:, 0:T], s_hbm[c * 128:(c + 1) * 128, :],
                     waits=[stb], slot="nrmld")
        r1 = bld.dma("d", cx.F1[0:64, 0:T],
                     bass.AP(nrm.tensor, 2 * c * 4224, [[0, 64], [1, T]]),
                     waits=[sr, mul, rc], slot="nrmrep")
        r2 = bld.dma("d", cx.F1[64:128, 0:T],
                     bass.AP(nrm.tensor, (2 * c + 1) * 4224, [[0, 64], [1, T]]),
                     slot="nrmrep")
        mul = bld.op("v", lambda v: v.tensor_tensor(ac[:, 0:T], ac[:, 0:T],
                                                    cx.F1[:, 0:T], ALU.mult),
                     waits=[ld, r1, r2])
        stb = bld.dma("d", s_hbm[c * 128:(c + 1) * 128, :], ac[:, 0:T], waits=[mul],
                      slot="nrmst2")
    bld.barrier()


DBG_AB = False
AB_WINDOW = True


def emit_l0ab(bld, cx, x_hbm, xout_hbm, wts, scr, cconsts):
    T = cx.T
    C = 32
    nc = cx.nc
    GN_EPS = 0.00064
    emit_ln(bld, cx, x_hbm, scr["xn"])

    names = ("x_r", "x_w", "x_k", "x_v", "x_a", "x_b", "x_g")
    mtoks = []
    for j, n in enumerate(names):
        mtoks.append(load_bias_cols(bld, cx, 88 + 8 * j, wts[n], "mixl"))
    for j in range(7):
        bld.op("v", lambda v, j=j: v.tensor_scalar(
            cx.SM[:, 144 + 8 * j:152 + 8 * j], cx.SM[:, 88 + 8 * j:96 + 8 * j],
            -1.0, 1.0, ALU.mult, ALU.add), waits=mtoks)
    bld.barrier()
    sm = {"r": scr["xr"], "w": cx.dram("ab_w", [D, cx.T], BF16),
          "k": scr["xk"], "v": scr["kact"],
          "a": cx.dram("ab_a", [D, cx.T], BF16),
          "b": cx.dram("ab_b", [D, cx.T], BF16),
          "g": cx.dram("ab_g", [D, cx.T], BF16)}
    emit_mix(bld, cx, scr["xn"],
             [(88 + 8 * j, 144 + 8 * j, sm[k]) for j, k in
              enumerate(("r", "w", "k", "v", "a", "b", "g"))], T)

    s_r = cx.dram("ab_sr", [D, cx.T], BF16)
    s_wl = cx.dram("ab_swl", [D, cx.T], F32)
    s_k = cx.dram("ab_sk", [D, cx.T], BF16)
    s_v = cx.dram("ab_sv", [D, cx.T], BF16)
    s_a = cx.dram("ab_sa", [D, cx.T], BF16)
    s_b = cx.dram("ab_sb", [D, cx.T], BF16)
    s_g = cx.dram("ab_sg", [D, cx.T], BF16)
    s_o = cx.dram("ab_so", [D, cx.T], BF16)

    def proj(src, wT, dst, act, fdst32=False):
        w_sl, w_tok = emit_weight_load(bld, cx, 0, wT, D, D)
        st_war = {}

        def ep(mc, t0, tw, pslice, tok, hb):
            sl = mc % 2
            if fdst32:
                ob = cx.H2.bitcast(F32)[:, sl * 512: sl * 512 + tw]
            else:
                ob = cx.H1[:, sl * 512: sl * 512 + tw]
            if act == "tanh":
                e1 = bld.op("s", lambda s, ob=ob, pslice=pslice: s.activation(
                    ob, pslice, AF.Tanh), waits=[tok, st_war.get(sl)])
            elif act == "sig":
                e1 = bld.op("s", lambda s, ob=ob, pslice=pslice: s.activation(
                    ob, pslice, AF.Sigmoid), waits=[tok, st_war.get(sl)])
            elif act == "wlog":
                tb = cx.F1[:, (2 + sl) * 512: (2 + sl) * 512 + tw]
                bld.op("s", lambda s, tb=tb, pslice=pslice: s.activation(
                    tb, pslice, AF.Sigmoid, scale=-1.0),
                    waits=[tok, st_war.get(sl)])
                e1 = bld.op("s", lambda s, ob=ob, tb=tb: s.activation(
                    ob, tb, AF.Ln))
            else:
                e1 = bld.op("v", lambda v, ob=ob, pslice=pslice: v.tensor_copy(
                    ob, pslice), waits=[tok, st_war.get(sl)])
            st_war[sl] = bld.dma("d", dst[mc * 128:(mc + 1) * 128, t0:t0 + tw],
                                 ob, waits=[e1], slot=f"pst{sl}")
            return e1
        emit_proj(bld, cx, w_sl, w_tok, src, D, ep, cx.RT0)

    proj(sm["r"], wts["W_rT"], s_r, None)
    proj(sm["w"], wts["W_wT"], s_wl, "wlog", fdst32=True)
    proj(sm["k"], wts["W_kT"], s_k, "tanh")
    proj(sm["v"], wts["W_vT"], s_v, None)
    proj(sm["a"], wts["W_aT"], s_a, None)
    proj(sm["b"], wts["W_bT"], s_b, None)
    proj(sm["g"], wts["W_gT"], s_g, "sig")
    emit_l2norm_pass(bld, cx, s_a, T)
    emit_l2norm_pass(bld, cx, s_b, T)

    s_Ab = cx.dram("ab_Ab", [D, cx.T], F32)
    s_BB = cx.dram("ab_BB", [D, cx.T], F32)
    s_Kb = cx.dram("ab_Kb", [D, cx.T], F32)
    s_rb = cx.dram("ab_rb", [D, cx.T], F32)
    s_eLC = cx.dram("ab_eLC", [D, cx.T // C], F32)
    msk = cx.H2
    z0 = bld.op("g", lambda g: g.memset(msk[:, 0:T], 1.0))
    z1 = bld.op("g", lambda g: g.memset(
        bass.AP(cx.H2, 0, [[cx.H2.shape[1], 128], [C, T // C]]), 0.0))
    R1f = cx.RT1.bitcast(F32)
    R0f = cx.RT0.bitcast(F32)
    prev = {}
    for c in range(KC):
        ld = bld.dma("d", cx.F1[:, 0:T], s_wl[c * 128:(c + 1) * 128, :],
                     waits=[prev.get("L")], slot="dwl")
        sc_ = bld.op("v", lambda v: v.tensor_tensor_scan(
            cx.F1[:, 0:T], msk[:, 0:T], cx.F1[:, 0:T], 0.0, ALU.mult, ALU.add),
            waits=[ld, z0, z1])
        e1 = bld.op("s", lambda s: s.activation(R1f[:, 0:T], cx.F1[:, 0:T], AF.Exp),
                    waits=[sc_, prev.get("eL")])
        ecx = bld.op("v", lambda v: v.tensor_copy(
            cx.SM[:, 1728:1728 + T // C],
            bass.AP(R1f, C - 1, [[R1f.shape[1], 128], [C, T // C]])),
            waits=[e1, prev.get("elc")])
        st0 = bld.dma("d", s_eLC[c * 128:(c + 1) * 128, :],
                      cx.SM[:, 1728:1728 + T // C], waits=[ecx], slot="elcst")
        prev["elc"] = st0
        ldr = bld.dma("d", cx.H1[:, 0:T], s_r[c * 128:(c + 1) * 128, :],
                      waits=[prev.get("H1")], slot="dr")
        rb_ = bld.op("v", lambda v: v.tensor_tensor(R0f[:, 0:T], cx.H1[:, 0:T],
                                                    R1f[:, 0:T], ALU.mult),
                     waits=[ldr, e1, prev.get("R0")])
        st1 = bld.dma("d", s_rb[c * 128:(c + 1) * 128, :], R0f[:, 0:T],
                      waits=[rb_], slot="strb")
        ldw = bld.dma("d", R0f[:, 0:T], s_wl[c * 128:(c + 1) * 128, :],
                      waits=[st1], slot="dwl2")
        lp = bld.op("v", lambda v: v.tensor_sub(R0f[:, 0:T], cx.F1[:, 0:T],
                                                R0f[:, 0:T]), waits=[ldw])
        ep_ = bld.op("s", lambda s: s.activation(R0f[:, 0:T], R0f[:, 0:T], AF.Exp),
                     waits=[lp])
        lda = bld.dma("d", cx.H1[:, 0:T], s_a[c * 128:(c + 1) * 128, :],
                      waits=[rb_], slot="dr")
        ab_ = bld.op("v", lambda v: v.tensor_tensor(R0f[:, 0:T], cx.H1[:, 0:T],
                                                    R0f[:, 0:T], ALU.mult),
                     waits=[lda, ep_])
        st2 = bld.dma("d", s_Ab[c * 128:(c + 1) * 128, :], R0f[:, 0:T],
                      waits=[ab_], slot="strb")
        em = bld.op("s", lambda s: s.activation(R1f[:, 0:T], cx.F1[:, 0:T], AF.Exp,
                                                scale=-1.0), waits=[rb_, st0])
        ldb = bld.dma("d", cx.H1[:, 0:T], s_b[c * 128:(c + 1) * 128, :],
                      waits=[ab_], slot="dr")
        bb_ = bld.op("v", lambda v: v.tensor_tensor(R0f[:, 0:T], cx.H1[:, 0:T],
                                                    R1f[:, 0:T], ALU.mult),
                     waits=[ldb, em, st2])
        st3 = bld.dma("d", s_BB[c * 128:(c + 1) * 128, :], R0f[:, 0:T],
                      waits=[bb_], slot="strb")
        ldk = bld.dma("d", cx.H1[:, 0:T], s_k[c * 128:(c + 1) * 128, :],
                      waits=[bb_], slot="dr")
        kb_ = bld.op("v", lambda v: v.tensor_tensor(R0f[:, 0:T], cx.H1[:, 0:T],
                                                    R1f[:, 0:T], ALU.mult),
                     waits=[ldk, st3])
        st4 = bld.dma("d", s_Kb[c * 128:(c + 1) * 128, :], R0f[:, 0:T],
                      waits=[kb_], slot="strb")
        prev["L"] = em
        prev["eL"] = kb_
        prev["H1"] = kb_
        prev["R0"] = st4
    bld.barrier()

    idf = cx.SM[:, 1536:1664]
    idt = bld.dma("g", idf, wts["identf"], slot="idf")
    mus = cx.SM[0:64, 1664:1792]
    mut = bld.dma("g", mus, cconsts["mu2"], slot="mu2")
    gnw_t = load_bias_cols(bld, cx, 200, wts["gn_w"], "gnl")
    gnb_t = load_bias_cols(bld, cx, 208, wts["gn_b"], "gnl")
    epsg, epsg_t = cx.const_col(bld, GN_EPS)

    F1f = cx.F1
    H1 = cx.H1
    H2f = cx.H2.bitcast(F32)
    R1uf = cx.RT1.bitcast(F32)
    RT0f = cx.RT0.bitcast(F32)
    NB = T // 512
    NCC = 512 // C
    SCB = [160, 1128]
    sw = {}
    bw = {}
    ld_war = {}
    go2 = {}
    for blk in range(NB):
        b0 = blk * 512
        for pg in range(4):
            lds = {}
            zi = {}
            for ln in range(2):
                pr = pg * 2 + ln
                l = []
                for (srcs, col, key) in ((s_Ab, 0, "ab"), (s_BB, 1024, "bb"),
                                         (s_Kb, 2048, "kb"), (s_rb, 3072, "rb")):
                    l.append(bld.dma(
                        "d", F1f[:, col + ln * 512: col + ln * 512 + 512],
                        srcs[pr * 128:(pr + 1) * 128, b0:b0 + 512],
                        waits=[ld_war.get((key, ln))], slot=f"u{key}{ln}"))
                l.append(bld.dma(
                    "d", H2f[:, ln * 16:(ln + 1) * 16],
                    s_eLC[pr * 128:(pr + 1) * 128, blk * 16:(blk + 1) * 16],
                    waits=[ld_war.get(("elc", ln))], slot=f"uelc{ln}"))
                l.append(bld.dma(
                    "d", H1[:, ln * 512:(ln + 1) * 512],
                    s_v[pr * 128:(pr + 1) * 128, b0:b0 + 512],
                    waits=[ld_war.get(("v", ln))], slot=f"uv{ln}"))
                l.append(bld.dma(
                    "d", H1[:, 1024 + ln * 512: 1536 + ln * 512],
                    s_g[pr * 128:(pr + 1) * 128, b0:b0 + 512],
                    waits=[ld_war.get(("g", ln))], slot=f"ug{ln}"))
                lds[ln] = l
                if blk == 0:
                    zi[ln] = bld.op("g", lambda g, pr=pr: g.memset(
                        RT0f[:, pr * 64:(pr + 1) * 64], 0.0))
                    sw[("zr", pr)] = zi[ln]
            def unit_gen(cc, ln):
                    pr = pg * 2 + ln
                    u0 = ln * 512 + cc * C
                    SC = SCB[ln]
                    SPf = H2f if cc % 2 == 0 else R1uf
                    BK = ln * 4 * 512
                    Ab = F1f[:, u0:u0 + C]
                    BB = F1f[:, 1024 + u0:1024 + u0 + C]
                    Kb = F1f[:, 2048 + u0:2048 + u0 + C]
                    rb_t = F1f[:, 3072 + u0:3072 + u0 + C]
                    Vt = H1[:, u0:u0 + C]
                    Gt = H1[:, 1024 + u0:1024 + u0 + C]
                    outacc = H1[:, 2048 + ln * 512 + cc * C: 2048 + ln * 512 + cc * C + C]
                    eLC_col = H2f[:, ln * 16 + cc: ln * 16 + cc + 1]
                    Z = RT0f[:, pr * 64:(pr + 1) * 64]
                    cmc = SPf[0:64, SC:SC + 128]
                    CmT = cmc[:, 0:32]
                    CKT = cmc[:, 32:64]
                    GT = cmc[:, 64:96]
                    PT = cmc[:, 96:128]
                    Cm = SPf[0:64, SC + 128:SC + 160]
                    pw_s = SPf[0:64, SC + 160:SC + 192]
                    pwT_s = SPf[0:64, SC + 192:SC + 224]
                    UV = SPf[0:64, SC + 224:SC + 352]
                    V_tm = SPf[0:64, SC + 352:SC + 416]
                    Y_sb = SPf[0:64, SC + 416:SC + 480]
                    Ukm = SPf[:, SC + 480:SC + 512]
                    Qkm = SPf[:, SC + 512:SC + 544]
                    B_tm = SPf[0:64, SC + 544:SC + 608]
                    Kb_tm = SPf[0:64, SC + 608:SC + 672]
                    O_sb = SPf[0:64, SC + 672:SC + 736]
                    t64 = SPf[:, SC + 736:SC + 800]
                    gnc = SPf[0:64, SC + 800:SC + 808]
                    sqo = SPf[0:64, SC + 808:SC + 872]
                    gst = SPf[:, SC + 872:SC + 904]
                    Onf = SPf[0:64, SC + 904:SC + 968]
                    w8 = lds[ln] + [mut, idt]
                    mm1 = bld.mm(cx.PS[0:32, BK:BK + C], BB[0:64, :], Ab[0:64, :],
                                 start=True, stop=False,
                                 waits=[bw.get((0, ln))] + w8)
                    bld.mm(cx.PS[32:64, BK:BK + C], BB[64:128, :], Ab[64:128, :],
                           start=True, stop=False)
                    bld.mm(cx.PS[0:32, BK + 32:BK + 64], Kb[0:64, :], Ab[0:64, :],
                           start=False, stop=False)
                    bld.mm(cx.PS[32:64, BK + 32:BK + 64], Kb[64:128, :],
                           Ab[64:128, :], start=False, stop=False)
                    bld.mm(cx.PS[0:32, BK + 64:BK + 96], BB[0:64, :], rb_t[0:64, :],
                           start=False, stop=False)
                    bld.mm(cx.PS[32:64, BK + 64:BK + 96], BB[64:128, :],
                           rb_t[64:128, :], start=False, stop=False)
                    bld.mm(cx.PS[0:32, BK + 96:BK + 128], Kb[0:64, :], rb_t[0:64, :],
                           start=False, stop=True)
                    m2 = bld.mm(cx.PS[32:64, BK + 96:BK + 128], Kb[64:128, :],
                                rb_t[64:128, :], start=False, stop=True, inc=True)
                    ec = bld.op("v", lambda v, cmc=cmc, BK=BK: v.tensor_tensor(
                        cmc, cx.PS[0:64, BK:BK + 128], mus, ALU.mult),
                        waits=[m2])
                    bw[(0, ln)] = ec
                    yield
                    tr1 = bld.op("v", lambda v, Cm=Cm, CmT=CmT: v.transpose(Cm, CmT))
                    yield
                    tv1 = bld.mm(cx.PS[0:32, BK + 1024:BK + 1024 + 64], Vt[0:64, :],
                                 cx.ident[0:64, 0:64], start=True, stop=True,
                                 waits=[bw.get((2, ln)), bw.get((2, ln, "o"))])
                    tv2 = bld.mm(cx.PS[32:64, BK + 1024:BK + 1024 + 64],
                                 Vt[64:128, :], cx.ident[64:128, 64:128],
                                 start=True, stop=True, inc=True)
                    e3 = bld.op("v", lambda v, V_tm=V_tm, BK=BK: v.tensor_copy(
                        V_tm, cx.PS[0:64, BK + 1024:BK + 1024 + 64]), waits=[tv2])
                    bw[(2, ln)] = e3
                    yield
                    ta1 = bld.mm(cx.PS[0:32, BK + 512:BK + 512 + 64], Ab[0:64, :],
                                 idf[0:64, 0:64], start=True, stop=False,
                                 waits=[bw.get((1, ln))])
                    ta2 = bld.mm(cx.PS[32:64, BK + 512:BK + 512 + 64], Ab[64:128, :],
                                 idf[64:128, 64:128], start=True, stop=False)
                    m5a = bld.mm(cx.PS[0:32, BK + 512 + 64:BK + 512 + 128],
                                 CKT[0:32, :], V_tm[0:32, :], start=False,
                                 stop=True, waits=[ec, e3])
                    m5b = bld.mm(cx.PS[32:64, BK + 512 + 64:BK + 512 + 128],
                                 CKT[32:64, :], V_tm[32:64, :], start=False,
                                 stop=True, inc=True)
                    ev_uv = bld.op("v", lambda v, UV=UV, BK=BK: v.tensor_copy(
                        UV, cx.PS[0:64, BK + 512:BK + 512 + 128]),
                        waits=[m5b, tr1])
                    bw[(1, ln)] = ev_uv
                    yield
                    pw_cur, pwT_cur = Cm, CmT
                    trp = None
                    for it in range(5):
                        rg = BK + 512 + 128 + (it % 2) * 128
                        mna = bld.mm(cx.PS[0:32, rg:rg + 128], pwT_cur[0:32, :],
                                     UV[0:32, :], start=True, stop=True,
                                     waits=[ev_uv, trp, bw.get((1, ln, it % 2))])
                        mnb = bld.mm(cx.PS[32:64, rg:rg + 128], pwT_cur[32:64, :],
                                     UV[32:64, :], start=True, stop=True, inc=True)
                        ev_uv = bld.op("v", lambda v, UV=UV, rg=rg:
                                       v.scalar_tensor_tensor(
                                           UV, cx.PS[0:64, rg:rg + 128], 1.0, UV,
                                           ALU.mult, ALU.add), waits=[mnb])
                        bw[(1, ln, it % 2)] = ev_uv
                        if it < 4:
                            rp = BK + 512 + 384
                            mpa = bld.mm(cx.PS[0:32, rp:rp + 32], pwT_cur[0:32, :],
                                         pw_cur[0:32, :], start=True, stop=True,
                                         waits=[bw.get((1, ln, "p")), ev_uv])
                            mpb = bld.mm(cx.PS[32:64, rp:rp + 32],
                                         pwT_cur[32:64, :], pw_cur[32:64, :],
                                         start=True, stop=True, inc=True)
                            epw = bld.op("v", lambda v, pw_s=pw_s, rp=rp:
                                         v.tensor_copy(pw_s, cx.PS[0:64, rp:rp + 32]),
                                         waits=[mpb])
                            bw[(1, ln, "p")] = epw
                            trp = bld.op("v", lambda v, pwT_s=pwT_s, pw_s=pw_s:
                                         v.transpose(pwT_s, pw_s))
                            pw_cur, pwT_cur = pw_s, pwT_s
                        yield
                    bo_ = BK + 1536
                    bld.mm(cx.PS[0:32, bo_:bo_ + 64], PT[0:32, :], V_tm[0:32, :],
                           start=True, stop=False,
                           waits=[ec, bw.get((3, ln)), bw.get((3, ln, "z"))])
                    bld.mm(cx.PS[32:64, bo_:bo_ + 64], PT[32:64, :], V_tm[32:64, :],
                           start=True, stop=False)
                    bld.mm(cx.PS[0:32, bo_:bo_ + 64], GT[0:32, :], UV[0:32, 64:128],
                           start=False, stop=False, waits=[ev_uv])
                    bld.mm(cx.PS[32:64, bo_:bo_ + 64], GT[32:64, :],
                           UV[32:64, 64:128], start=False, stop=False)
                    bq = BK + 1024 + 64
                    bld.mm(cx.PS[0:64, bq:bq + 32], UV[0:32, 0:64], GT[0:32, :],
                           start=True, stop=False, waits=[ev_uv, bw.get((2, ln, "q"))])
                    bld.mm(cx.PS[64:128, bq:bq + 32], UV[32:64, 0:64], GT[32:64, :],
                           start=True, stop=False)
                    bld.mm(cx.PS[0:64, bq:bq + 32], idf[0:64, 0:64], rb_t[0:64, :],
                           start=False, stop=True)
                    mq = bld.mm(cx.PS[64:128, bq:bq + 32], idf[64:128, 64:128],
                                rb_t[64:128, :], start=False, stop=True, inc=True)
                    e4 = bld.op("v", lambda v, Qkm=Qkm, bq=bq: v.tensor_copy(
                        Qkm, cx.PS[:, bq:bq + 32]), waits=[mq])
                    bw[(2, ln, "q")] = e4
                    yield
                    mo1 = bld.mm(cx.PS[0:32, bo_:bo_ + 64], Qkm[0:64, :], Z[0:64, :],
                                 start=False, stop=True,
                                 waits=[e4, sw.get(("zr", pr))])
                    mo2 = bld.mm(cx.PS[32:64, bo_:bo_ + 64], Qkm[64:128, :],
                                 Z[64:128, :], start=False, stop=True, inc=True)
                    e5 = bld.op("v", lambda v, O_sb=O_sb, bo_=bo_: v.tensor_copy(
                        O_sb, cx.PS[0:64, bo_:bo_ + 64]), waits=[mo2])
                    bw[(3, ln)] = e5
                    yield
                    bt_ = BK + 1024 + 96
                    tu1 = bld.mm(cx.PS[0:64, bt_:bt_ + 32], UV[0:32, 0:64],
                                 idf[0:32, 0:32], start=True, stop=True,
                                 waits=[ev_uv, bw.get((2, ln, "t"))])
                    tu2 = bld.mm(cx.PS[64:128, bt_:bt_ + 32], UV[32:64, 0:64],
                                 idf[32:64, 32:64], start=True, stop=True)
                    tb1 = bld.mm(cx.PS[0:32, bt_ + 32:bt_ + 96], BB[0:64, :],
                                 idf[0:64, 0:64], start=True, stop=True)
                    tb2 = bld.mm(cx.PS[32:64, bt_ + 32:bt_ + 96], BB[64:128, :],
                                 idf[64:128, 64:128], start=True, stop=True)
                    tk1 = bld.mm(cx.PS[0:32, bt_ + 96:bt_ + 160], Kb[0:64, :],
                                 idf[0:64, 0:64], start=True, stop=True)
                    tk2 = bld.mm(cx.PS[32:64, bt_ + 96:bt_ + 160], Kb[64:128, :],
                                 idf[64:128, 64:128], start=True, stop=True,
                                 inc=True)
                    e6 = bld.op("v", lambda v, Ukm=Ukm, bt_=bt_: v.tensor_copy(
                        Ukm, cx.PS[:, bt_:bt_ + 32]), waits=[tk2])
                    e7 = bld.op("v", lambda v, B_tm=B_tm, bt_=bt_: v.tensor_copy(
                        B_tm, cx.PS[0:64, bt_ + 32:bt_ + 96]))
                    e8 = bld.op("v", lambda v, Kb_tm=Kb_tm, bt_=bt_: v.tensor_copy(
                        Kb_tm, cx.PS[0:64, bt_ + 96:bt_ + 160]))
                    bw[(2, ln, "t")] = e8
                    yield
                    by = BK + 1024 + 288
                    my1 = bld.mm(cx.PS[0:32, by:by + 64], Ukm[0:64, :], Z[0:64, :],
                                 start=True, stop=False,
                                 waits=[e8, bw.get((2, ln, "y")), sw.get(("zr", pr))])
                    my2 = bld.mm(cx.PS[32:64, by:by + 64], Ukm[64:128, :],
                                 Z[64:128, :], start=True, stop=False)
                    my3 = bld.mm(cx.PS[0:32, by:by + 64], idf[0:32, 0:32],
                                 UV[0:32, 64:128], start=False, stop=True)
                    my4 = bld.mm(cx.PS[32:64, by:by + 64], idf[32:64, 32:64],
                                 UV[32:64, 64:128], start=False, stop=True,
                                 inc=True)
                    e9 = bld.op("v", lambda v, Y_sb=Y_sb, by=by: v.tensor_copy(
                        Y_sb, cx.PS[0:64, by:by + 64]), waits=[my4])
                    bw[(2, ln, "y")] = e9
                    yield
                    bz = bo_ + 64
                    mz1 = bld.mm(cx.PS[0:64, bz:bz + 64], B_tm[0:32, :],
                                 Y_sb[0:32, :], start=True, stop=False,
                                 waits=[e7, e9, bw.get((3, ln, "z"))])
                    mz2 = bld.mm(cx.PS[64:128, bz:bz + 64], B_tm[32:64, :],
                                 Y_sb[32:64, :], start=True, stop=False)
                    mz3 = bld.mm(cx.PS[0:64, bz:bz + 64], Kb_tm[0:32, :],
                                 V_tm[0:32, :], start=False, stop=True, waits=[e8])
                    mz4 = bld.mm(cx.PS[64:128, bz:bz + 64], Kb_tm[32:64, :],
                                 V_tm[32:64, :], start=False, stop=True, inc=True)
                    e10 = bld.op("v", lambda v, t64=t64, bz=bz: v.tensor_copy(
                        t64, cx.PS[:, bz:bz + 64]), waits=[mz4])
                    bw[(3, ln, "z")] = e10
                    yield
                    z1_ = bld.op("v", lambda v, t64=t64, Z=Z: v.tensor_add(
                        t64, t64, Z), waits=[mo2, my2])
                    zr = bld.op("v", lambda v, Z=Z, t64=t64, eLC_col=eLC_col:
                                v.tensor_scalar(Z, t64, eLC_col, None, ALU.mult),
                                waits=[lds[ln][4]])
                    sw[("zr", pr)] = zr
                    bld.op("v", lambda v, gnc=gnc, O_sb=O_sb: v.reduce_sum(
                        gnc[:, 0:1], O_sb, axis=AX.X), waits=[e5])
                    bld.op("v", lambda v, gnc=gnc: v.tensor_scalar_mul(
                        gnc[:, 0:1], gnc[:, 0:1], 1.0 / 64))
                    g3_ = bld.op("s", lambda s, sqo=sqo, O_sb=O_sb, gnc=gnc:
                                 s.activation(sqo, O_sb, AF.Square,
                                              accum_out=gnc[:, 1:2]),
                                 waits=[e5, bw.get(("gn", ln))])
                    bld.op("v", lambda v, gnc=gnc: v.tensor_scalar_mul(
                        gnc[:, 1:2], gnc[:, 1:2], 1.0 / 64), waits=[g3_])
                    bld.op("v", lambda v, gnc=gnc: v.tensor_tensor(
                        gnc[:, 2:3], gnc[:, 0:1], gnc[:, 0:1], ALU.mult))
                    g6_ = bld.op("v", lambda v, gnc=gnc: v.tensor_sub(
                        gnc[:, 1:2], gnc[:, 1:2], gnc[:, 2:3]))
                    g7_ = bld.op("s", lambda s, gnc=gnc: s.activation(
                        gnc[:, 2:3], gnc[:, 1:2], AF.Ln, bias=epsg[0:64, 0:1]),
                        waits=[g6_, epsg_t])
                    g8_ = bld.op("s", lambda s, gnc=gnc: s.activation(
                        gnc[:, 2:3], gnc[:, 2:3], AF.Exp, scale=-0.5))
                    bw[("gn", ln)] = g8_
                    yield
                    g9_ = bld.op("v", lambda v, Onf=Onf, O_sb=O_sb, gnc=gnc:
                                 v.tensor_scalar(Onf, O_sb, gnc[:, 0:1],
                                                 gnc[:, 2:3], ALU.subtract,
                                                 ALU.mult),
                                 waits=[g8_, bw.get(("onb", ln))])
                    bon = BK + 1024 + 352
                    to1 = bld.mm(cx.PS[0:64, bon:bon + 32], Onf[0:32, :],
                                 idf[0:32, 0:32], start=True, stop=True,
                                 waits=[g9_, bw.get((2, ln, "o"))])
                    to2 = bld.mm(cx.PS[64:128, bon:bon + 32],
                                 Onf[32:64, :], idf[32:64, 32:64],
                                 start=True, stop=True, inc=True)
                    bw[("onb", ln)] = to2
                    go1 = bld.op("v", lambda v, gst=gst, bon=bon, pr=pr:
                                 v.tensor_scalar(gst, cx.PS[:, bon:bon + 32],
                                                 cx.SM[:, 200 + pr:201 + pr],
                                                 cx.SM[:, 208 + pr:209 + pr],
                                                 ALU.mult, ALU.add),
                                 waits=[to2, gnw_t, gnb_t, z1_])
                    bw[(2, ln, "o")] = go1
                    yield
                    go2[ln] = bld.op("v", lambda v, outacc=outacc, gst=gst, Gt=Gt:
                                     v.tensor_tensor(outacc, gst, Gt, ALU.mult),
                                     waits=[go2.get(ln), ld_war.get(("oa", ln))])
            if AB_WINDOW:
                KOFF = 11
                fl = {0: [], 1: []}
                nxt_cc = {0: 0, 1: 0}

                def pump(ln):
                    if (nxt_cc[ln] < NCC and len(fl[ln]) < 2
                            and (not fl[ln] or fl[ln][-1][1] >= KOFF)):
                        fl[ln].append([unit_gen(nxt_cc[ln], ln), 0])
                        nxt_cc[ln] += 1

                while True:
                    pump(0)
                    pump(1)
                    order = []
                    for depth in (0, 1):
                        for ln in (0, 1):
                            if depth < len(fl[ln]):
                                order.append((ln, fl[ln][depth]))
                    if not order:
                        break
                    fin = []
                    for ln, ent in order:
                        try:
                            next(ent[0])
                            ent[1] += 1
                        except StopIteration:
                            fin.append((ln, ent))
                    for ln, ent in fin:
                        fl[ln].remove(ent)
            else:
                for cc in range(NCC):
                    gens = [unit_gen(cc, 0), unit_gen(cc, 1)]
                    while gens:
                        nxt = []
                        for g in gens:
                            try:
                                next(g)
                                nxt.append(g)
                            except StopIteration:
                                pass
                        gens = nxt
            for ln in range(2):
                pr = pg * 2 + ln
                oa = bld.dma("d", s_o[pr * 128:(pr + 1) * 128, b0:b0 + 512],
                             H1[:, 2048 + ln * 512: 2560 + ln * 512],
                             waits=[go2[ln]], slot=f"oast{ln}")
                ld_war[("oa", ln)] = oa
                for key in ("ab", "bb", "kb", "rb"):
                    ld_war[(key, ln)] = bld.tok("t")
                ld_war[("elc", ln)] = sw[("zr", pr)]
                ld_war[("v", ln)] = bld.tok("t")
                ld_war[("g", ln)] = go2[ln]
    bld.barrier()

    wo_sl, wo_tok = emit_weight_load(bld, cx, 0, wts["W_oT"], D, D)
    bo_tok = load_bias_cols(bld, cx, 232, wts["b_o"], "bol")
    ep = make_res_epilogue(bld, cx, x_hbm, xout_hbm, 232, bo_tok)
    emit_proj(bld, cx, wo_sl, wo_tok, s_o, D, ep, cx.RT0)



def build_program(T=4096, chain=("l1ffn",)):
    nc = bass.Bass("TRN2", target_bir_lowering=False)
    bld = Builder(nc)
    cx = Ctx(nc, T)

    x_in = nc.dram_tensor("x_in", [D, T], F32, kind="ExternalInput").ap()
    y_out = nc.dram_tensor("y_out", [D, T], F32, kind="ExternalOutput").ap()
    ones_in = nc.dram_tensor("c_one", [1, 128], F32, kind="ExternalInput").ap()
    ident_in = nc.dram_tensor("c_ident", [128, 128], F32, kind="ExternalInput").ap()
    emit_consts(bld, cx, ones_in, ident_in)

    scr = {}
    for nm, dt in (("xn", BF16), ("xr", BF16), ("xk", BF16), ("kact", BF16),
                   ("b1", BF16), ("xa", F32), ("xb2", F32)):
        scr[nm] = nc.dram_tensor("scr_" + nm, [D, T], dt, kind="Internal").ap()
    cconsts = {}
    if any(s == "l1aa" for s in chain):
        cconsts["onehotT"] = nc.dram_tensor("c_onehotT", [32, 8192], F32,
                                            kind="ExternalInput").ap()
    if any(s == "l1ab" for s in chain):
        cconsts["cos"] = nc.dram_tensor("c_cos", [512, T], F32, kind="ExternalInput").ap()
        cconsts["sin"] = nc.dram_tensor("c_sin", [512, T], F32, kind="ExternalInput").ap()
        cconsts["mask"] = nc.dram_tensor("c_mask", [1, 8192], F32, kind="ExternalInput").ap()

    wcache = {}
    def win(name, shape):
        if name not in wcache:
            wcache[name] = nc.dram_tensor(name, list(shape), F32,
                                          kind="ExternalInput").ap()
        return wcache[name]

    cur = x_in
    ping = scr["xa"]
    pong = scr["xb2"]
    for i, sl in enumerate(chain):
        dst = y_out if i == len(chain) - 1 else ping
        if sl in ("l1ffn", "l1ffn2"):
            wts = {
                "WrT": win("l1ffn_WrT", (D, D)),
                "WkT": win("l1ffn_WkT", (D, D)),
                "WvT": win("l1ffn_WvT", (D, D)),
                "mu_r": win("l1ffn_mu_r", (D, 1)),
                "mu_k": win("l1ffn_mu_k", (D, 1)),
            }
            emit_l1ffn(bld, cx, cur, dst, wts, scr)
        elif sl == "l0aa":
            wts = {
                "WiT": win("l0aa_WiT", (D, D)),
                "WgT": win("l0aa_WgT", (D, D)),
                "WoT": win("l0aa_WoT", (D, D)),
                "bi": win("l0aa_bi", (D, 1)),
                "bg": win("l0aa_bg", (D, 1)),
                "bo": win("l0aa_bo", (D, 1)),
            }
            emit_l0aa(bld, cx, cur, dst, wts, scr)
        elif sl == "l1ab":
            wts = {n: win("l1ab_" + n, (D, D) if n.startswith("W") else (D, 1))
                   for n in ("WqT", "WkT", "WvT", "WoT", "bq", "bk", "bv", "bo")}
            emit_l1ab(bld, cx, cur, dst, wts, scr, cconsts)
        elif sl == "l1aa":
            wts = {n: win("l1aa_" + n, (D, D)) for n in ("Wq", "Wk", "Wv", "Wo")}
            wts["rel_emb"] = win("l1aa_rel_emb", (32, 16))
            emit_l1aa(bld, cx, cur, dst, wts, scr, cconsts)
        elif sl in ("l0ffn", "l0ffn2"):
            w1t = win("l0ffn_W1T", (8 * D, 4096))
            w2t = win("l0ffn_W2T", (8 * 4096, D))
            b1 = win("l0ffn_b1", (8 * 4096, 1))
            b2 = win("l0ffn_b2", (8 * D, 1))
            wts = {
                "WrT": win("l0ffn_WrT", (D, 8)),
                "br": win("l0ffn_br", (8, 1)),
                "W1T": [w1t[e * D:(e + 1) * D, :] for e in range(8)],
                "W2T": [w2t[e * 4096:(e + 1) * 4096, :] for e in range(8)],
                "b1": [b1[e * 4096:(e + 1) * 4096, :] for e in range(8)],
                "b2": [b2[e * D:(e + 1) * D, :] for e in range(8)],
                "gamma": win("l0ffn_gamma", (D, 1)),
                "beta": win("l0ffn_beta", (D, 1)),
                "identf": win("c_identf", (128, 128)),
            }
            emit_l0ffn(bld, cx, cur, dst, wts, scr)
        elif sl == "l0ab":
            wts = {}
            for n in ("W_r", "W_w", "W_k", "W_v", "W_a", "W_b", "W_g", "W_o"):
                wts[n + "T"] = win("l0ab_" + n + "T", (D, D))
            for n in ("x_r", "x_w", "x_k", "x_v", "x_a", "x_b", "x_g",
                      "b_o", "gn_w", "gn_b"):
                wts[n] = win("l0ab_" + n, (D, 1))
            wts["identf"] = win("c_identf", (128, 128))
            ab_consts = {"mu2": win("c_mu2", (64, 128))}
            emit_l0ab(bld, cx, cur, dst, wts, scr, ab_consts)
        else:
            raise ValueError(sl)
        cur = dst
        ping, pong = pong, ping

    bld.finalize()
    return nc



def _rp_bucket_np(rel, num_buckets=32, max_distance=128):
    nb = num_buckets // 2
    n = -rel
    ret = (n < 0).astype(np.int64) * nb
    n = np.abs(n)
    max_exact = nb // 2
    is_small = n < max_exact
    vl = max_exact + (np.log(np.maximum(n, 1).astype(np.float64) / max_exact)
                      / math.log(max_distance / max_exact) * (nb - max_exact)).astype(np.int64)
    vl = np.minimum(vl, nb - 1)
    return ret + np.where(is_small, n, vl)


def prep_core_inputs(d, T, chain, core):
    x = d["x"][core]
    m = {
        "x_in": np.ascontiguousarray(x.T).astype(np.float32),
        "c_one": np.ones((1, 128), np.float32),
        "c_ident": np.eye(128, dtype=np.float32),
    }
    for sl in chain:
        if sl in ("l1ffn", "l1ffn2"):
            m["l1ffn_WrT"] = np.ascontiguousarray(d["l1ffn_W_r"].T)
            m["l1ffn_WkT"] = np.ascontiguousarray(d["l1ffn_W_k"].T)
            m["l1ffn_WvT"] = np.ascontiguousarray(d["l1ffn_W_v"].T)
            m["l1ffn_mu_r"] = d["l1ffn_mu_r"].reshape(D, 1)
            m["l1ffn_mu_k"] = d["l1ffn_mu_k"].reshape(D, 1)
        elif sl == "l1ab":
            m["l1ab_WqT"] = np.ascontiguousarray(d["l1ab_Wq"].T)
            m["l1ab_WkT"] = np.ascontiguousarray(d["l1ab_Wk"].T)
            m["l1ab_WvT"] = np.ascontiguousarray(d["l1ab_Wv"].T)
            m["l1ab_WoT"] = np.ascontiguousarray(d["l1ab_Wo"].T)
            for n in ("bq", "bk", "bv", "bo"):
                m[f"l1ab_{n}"] = d[f"l1ab_{n}"].reshape(D, 1)
            half = 512
            inv = 1.0 / (10000.0 ** (np.arange(half, dtype=np.float64) / half))
            ang = np.arange(T, dtype=np.float64)[:, None] * inv[None, :]
            m["c_cos"] = np.ascontiguousarray(np.cos(ang).T.astype(np.float32))
            m["c_sin"] = np.ascontiguousarray(np.sin(ang).T.astype(np.float32))
            mrow = np.zeros((1, 8192), np.float32)
            j = np.arange(8192) - 4096
            mrow[0, (j >= -512) & (j <= 0)] = 1.0
            m["c_mask"] = mrow
        elif sl == "l1aa":
            for n in ("Wq", "Wk", "Wv", "Wo"):
                m[f"l1aa_{n}"] = np.ascontiguousarray(d[f"l1aa_{n}"])
            m["l1aa_rel_emb"] = d["l1aa_rel_emb"]
            idxs = np.arange(8192) - 4095
            bk = _rp_bucket_np(idxs)
            oh = np.zeros((32, 8192), np.float32)
            oh[bk, np.arange(8192)] = 1.0
            m["c_onehotT"] = oh
        elif sl in ("l0ffn", "l0ffn2"):
            m["l0ffn_WrT"] = np.ascontiguousarray(d["l0ffn_Wr"].T)
            m["l0ffn_br"] = d["l0ffn_br"].reshape(8, 1)
            m["l0ffn_W1T"] = np.ascontiguousarray(
                d["l0ffn_W1"].transpose(0, 2, 1).reshape(8 * D, 4096))
            m["l0ffn_W2T"] = np.ascontiguousarray(
                d["l0ffn_W2"].transpose(0, 2, 1).reshape(8 * 4096, D))
            m["l0ffn_b1"] = d["l0ffn_b1"].reshape(8 * 4096, 1)
            m["l0ffn_b2"] = d["l0ffn_b2"].reshape(8 * D, 1)
            m["l0ffn_gamma"] = d["l0ffn_gamma"].reshape(D, 1)
            m["l0ffn_beta"] = d["l0ffn_beta"].reshape(D, 1)
            m["c_identf"] = np.eye(128, dtype=np.float32)
        elif sl == "l0ab":
            for n in ("W_r", "W_w", "W_k", "W_v", "W_a", "W_b", "W_g", "W_o"):
                m["l0ab_" + n + "T"] = np.ascontiguousarray(d["l0ab_" + n].T)
            for n in ("x_r", "x_w", "x_k", "x_v", "x_a", "x_b", "x_g",
                      "b_o", "gn_w", "gn_b"):
                m["l0ab_" + n] = d["l0ab_" + n].reshape(D, 1)
            m["c_identf"] = np.eye(128, dtype=np.float32)
            st_ = np.triu(np.ones((32, 32), np.float32), 1)
            inc_ = np.triu(np.ones((32, 32), np.float32), 0)
            blkS = np.concatenate([st_, st_], axis=0)
            blkI = np.concatenate([inc_, inc_], axis=0)
            m["c_mu2"] = np.concatenate([blkS, blkS, blkI, blkI], axis=1)
        elif sl == "l0aa":
            m["l0aa_WiT"] = np.ascontiguousarray(d["l0aa_Wi"].T)
            m["l0aa_WgT"] = np.ascontiguousarray(d["l0aa_Wg"].T)
            m["l0aa_WoT"] = np.ascontiguousarray(d["l0aa_Wo"].T)
            m["l0aa_bi"] = d["l0aa_bi"].reshape(D, 1)
            m["l0aa_bg"] = d["l0aa_bg"].reshape(D, 1)
            m["l0aa_bo"] = d["l0aa_bo"].reshape(D, 1)
    return m


FULL_CHAIN = ("l0aa", "l0ffn", "l0ab", "l0ffn2", "l1aa", "l1ffn", "l1ab",
              "l1ffn2")

_H2D = {
    "L0_attn_a_": "l0aa_", "L0_attn_b_": "l0ab_", "L0_ffn_": "l0ffn_",
    "L1_attn_a_": "l1aa_", "L1_attn_b_": "l1ab_", "L1_ffn_": "l1ffn_",
}


def kernel(**inputs):
    d = {}
    for k, v in inputs.items():
        for hp, dp in _H2D.items():
            if k.startswith(hp):
                d[dp + k[len(hp):]] = np.asarray(v)
                break
    d["x"] = np.asarray(inputs["L0_attn_a_x"])
    T = d["x"].shape[1]
    chain = FULL_CHAIN
    nc = build_program(T=T, chain=chain)
    base = prep_core_inputs(d, T, chain, 0)
    in_maps = []
    for c in range(NCORES):
        m = dict(base)
        m["x_in"] = np.ascontiguousarray(d["x"][c].T).astype(np.float32)
        in_maps.append(m)
    res = run_bass_kernel_spmd(nc, in_maps, core_ids=list(range(NCORES)))
    out = np.stack([np.asarray(r["y_out"]).T for r in res.results], axis=0)
    return out.astype(np.float32)


# version 09
# speedup vs baseline: 1.0355x; 1.0355x over previous
import math
from contextlib import ExitStack

import numpy as np
import concourse.bass as bass
import concourse.mybir as mybir
from concourse.bass_utils import run_bass_kernel_spmd

F32 = mybir.dt.float32
BF16 = mybir.dt.bfloat16
AF = mybir.ActivationFunctionType
ALU = mybir.AluOpType
AX = mybir.AxisListType

D = 1024
KC = D // 128
NCORES = 8
LN_EPS = 1e-5


class Tok:
    __slots__ = ("sem", "val")
    def __init__(self, sem, val):
        self.sem = sem
        self.val = val


class Builder:
    ENGINES = ("t", "v", "s", "g", "d")

    def __init__(self, nc):
        self.nc = nc
        self.es = ExitStack()
        self.q = {e: [] for e in self.ENGINES}
        self.sem = {}
        self.cnt = {e: 0 for e in ("t", "v", "s", "g")}
        self.slots = {}
        for e in ("v", "s", "g", "t"):
            self.sem[e] = self.es.enter_context(nc.semaphore(f"seq_{e}"))
        self._waited = {e: {} for e in self.ENGINES}
        self._bump = {}

    def _emit_wait(self, e, sem, val, key):
        if val <= 0:
            return
        w = self._waited[e]
        if w.get(key, -1) >= val:
            return
        w[key] = val
        self.q[e].append(lambda eng, sem=sem, val=val: eng.wait_ge(sem, val))

    def wait(self, e, tok):
        if tok is None:
            return
        val = tok.val
        bump = self._bump.get(id(tok.sem))
        if bump is not None:
            val = bump[1] * 16
        self._emit_wait(e, tok.sem, val, id(tok.sem))

    def waits(self, e, toks):
        best = {}
        for tok in toks:
            if tok is None:
                continue
            k = id(tok.sem)
            if k not in best or best[k].val < tok.val:
                best[k] = tok
        for tok in best.values():
            self.wait(e, tok)

    def op(self, e, fn, waits=(), inc=True):
        self.waits(e, waits)
        if e != "t":
            self._emit_wait(e, self.sem[e], self.cnt[e], f"self_{e}")
        if inc:
            self.cnt[e] += 1
            sem = self.sem[e]
            self.q[e].append(lambda eng, fn=fn, sem=sem: fn(eng).then_inc(sem, 1))
            return Tok(sem, self.cnt[e])
        self.q[e].append(lambda eng, fn=fn: fn(eng))
        return Tok(self.sem[e], self.cnt[e])

    def mm(self, psum, lhsT, rhs, start, stop, waits=(), inc=False):
        return self.op(
            "t", lambda t, a=psum, b=lhsT, c=rhs, s1=start, s2=stop:
                t.matmul(a, b, c, start=s1, stop=s2),
            waits=waits, inc=inc)

    SLOT_ALIAS = {}
    for _n in ("gnl", "bia", "mixl", "b1l", "b2l", "gbl", "bol", "mu",
               "bqkv", "toepst"):
        SLOT_ALIAS[_n] = "cfg"
    for _n in ("c0", "c1", "c2", "idf", "mu2", "re", "mrep", "oht"):
        SLOT_ALIAS[_n] = "cfgg"
    for _i in range(4):
        SLOT_ALIAS[f"lnx{_i}"] = f"a{_i}"
        SLOT_ALIAS[f"yst{_i}"] = f"a{_i}"
        SLOT_ALIAS[f"rox{_i}"] = f"b{_i}"
        SLOT_ALIAS[f"kst{_i}"] = f"b{_i}"
    for _i in range(2):
        SLOT_ALIAS[f"qst{_i}"] = f"c{_i}"
        SLOT_ALIAS[f"pst{_i}"] = f"c{_i}"
        SLOT_ALIAS[f"oast{_i}"] = f"c{_i}"
        SLOT_ALIAS[f"hst{_i}"] = f"c{_i}"
    for _n in ("lnst", "mixst", "ctxst"):
        SLOT_ALIAS[_n] = "st5"
    for _n in ("lnrow", "arow", "nrmrow", "rwst"):
        SLOT_ALIAS[_n] = "row"
    for _n in ("lnb", "arep", "nrmrep", "rwrep"):
        SLOT_ALIAS[_n] = "repl"
    for _n in ("mixld", "lnxa", "nrmld", "dwl"):
        SLOT_ALIAS[_n] = "ld0"
    for _n in ("nrmld2", "dwl2"):
        SLOT_ALIAS[_n] = "ld1"
    SLOT_ALIAS["kh"] = "kt0"
    SLOT_ALIAS["vh"] = "vt0"

    def slot_sem(self, slot):
        slot = self.SLOT_ALIAS.get(slot, slot)
        if slot not in self.slots:
            self.slots[slot] = [self.es.enter_context(self.nc.semaphore(f"sl_{slot}")), 0]
            if slot in ("cfg", "cfgg"):
                self._bump[id(self.slots[slot][0])] = self.slots[slot]
        return self.slots[slot]

    def dma(self, qn, out, in_, waits=(), slot=None, **kw):
        assert slot is not None
        self.waits(qn, waits)
        ss = self.slot_sem(slot)
        ss[1] += 1
        self.q[qn].append(
            lambda eng, out=out, in_=in_, sem=ss[0], kw=kw:
                eng.dma_start(out, in_, **kw).then_inc(sem, 16))
        return Tok(ss[0], ss[1] * 16)

    def tok(self, e):
        return Tok(self.sem[e], self.cnt[e])

    def barrier(self):
        toks = [Tok(self.sem[e], self.cnt[e]) for e in ("t", "v", "s", "g")]
        toks += [Tok(ss[0], ss[1] * 16) for ss in self.slots.values()]
        for e in self.ENGINES:
            for tk in toks:
                self.wait(e, tk)

    def finalize(self):
        nc = self.nc
        with nc.Block() as blk:
            @blk.gpsimd
            def _(eng):
                for fn in self.q["g"]:
                    fn(eng)
            @blk.tensor
            def _(eng):
                for fn in self.q["t"]:
                    fn(eng)
            @blk.vector
            def _(eng):
                for fn in self.q["v"]:
                    fn(eng)
            @blk.scalar
            def _(eng):
                for fn in self.q["s"]:
                    fn(eng)
            @blk.sync
            def _(eng):
                for fn in self.q["d"]:
                    fn(eng)
        self.es.close()


class Ctx:
    def __init__(self, nc, T):
        self.nc = nc
        self.T = T
        self.W = nc.alloc_sbuf_tensor("W", [128, 65536], BF16)
        self.Wf = self.W.bitcast(F32)
        self.RT0 = nc.alloc_sbuf_tensor("RT0", [128, 8192], BF16)
        self.RT1 = nc.alloc_sbuf_tensor("RT1", [128, 8192], BF16)
        self.F1 = nc.alloc_sbuf_tensor("F1", [128, 4096], F32)
        self.H1 = nc.alloc_sbuf_tensor("H1", [128, 4224], BF16)
        self.H2 = nc.alloc_sbuf_tensor("H2", [128, 4224], BF16)
        self.SM = nc.alloc_sbuf_tensor("SM", [128, 2048], F32)
        self.SMb = self.SM.bitcast(BF16)
        self.PS = nc.alloc_psum_tensor("PS", [128, 4096], F32)
        self.PSb = self.PS.bitcast(BF16)
        self.rows_dram = nc.dram_tensor("rows_dram", [8, 4096], F32,
                                        kind="Internal").ap()
        self.ones_col = None
        self.ones_row = None
        self._cc = {}

    def const_col(self, bld, value):
        if value not in self._cc:
            i = len(self._cc)
            assert i < 60
            sl = self.SM[:, 1900 + i: 1901 + i]
            tok = bld.op("g", lambda g, sl=sl, value=value: g.memset(sl, value))
            self._cc[value] = (sl, tok)
        return self._cc[value]

    def bank(self, b, w=512):
        return self.PS[:, b * 512: b * 512 + w]

    def dram(self, name, shape, dtype):
        if not hasattr(self, "_dr"):
            self._dr = {}
        if name not in self._dr:
            self._dr[name] = self.nc.dram_tensor(name, list(shape), dtype,
                                                 kind="Internal").ap()
        return self._dr[name]


def ttiles(T, n=512):
    return [(i, min(n, T - i)) for i in range(0, T, n)]



def emit_consts(bld, cx, ones_hbm, ident_hbm):
    oc = cx.SMb[:, 4094:4095]
    orow = cx.SMb[0:1, 3968:4096]
    ident = cx.SMb[:, 3840:3968]
    t0 = bld.dma("g", oc, bass.AP(ones_hbm.tensor, 0, [[0, 128], [1, 1]]), slot="c0")
    bld.dma("g", orow, ones_hbm, waits=[t0], slot="c1")
    bld.dma("g", ident, ident_hbm, slot="c2")
    cx.ones_col = oc
    cx.ones_row = orow
    cx.ident = ident
    bld.barrier()


def emit_weight_load(bld, cx, wcol, hbm_ap, din, dout):
    nkc = din // 128
    W = cx.W
    tok = None
    for k0 in range(0, nkc, 8):
        kn = min(8, nkc - k0)
        tok = bld.dma(
            "g", bass.AP(W, wcol + k0 * dout, [[W.shape[1], 128], [dout, kn], [1, dout]]),
            hbm_ap[k0 * 128:(k0 + kn) * 128, :].rearrange("(kc p) j -> p kc j", p=128),
            slot=f"w{wcol}")
    def sl(kc, mc, mw=128):
        return W[:, wcol + kc * dout + mc * mw: wcol + kc * dout + mc * mw + mw]
    return sl, tok


def emit_ln(bld, cx, x_hbm, xn_hbm, gb=None, res_hbm=None):
    T = cx.T
    emit_ln._radd = None
    Wf = cx.Wf
    MB = Wf[:, 12288:16384]
    RB = Wf[:, 16384:20480]
    stat_s = Wf[0:1, 12288:16384]
    stat_q = Wf[0:1, 16384:20480]
    r_mean = Wf[0:1, 20480:24576]
    r_rstd = Wf[0:1, 24576:28672]
    xc, xb, sq = cx.F1, cx.H1, cx.H2
    tts = ttiles(T)

    slot_cast = {}
    slot_mm = {}
    evac = {}
    for ti, (t0, tw) in enumerate(tts):
        bs, bq = 2 * (ti % 2), 2 * (ti % 2) + 1
        mmtok = None
        for c in range(KC):
            sl = c % 4
            ld = bld.dma("d", xc[:, sl * 512: sl * 512 + tw],
                         x_hbm[c * 128:(c + 1) * 128, t0:t0 + tw],
                         waits=[slot_cast.get(sl)], slot=f"lnx{sl}")
            cast = bld.op("s", lambda s, sl=sl, tw=tw: s.copy(
                xb[:, sl * 512: sl * 512 + tw], xc[:, sl * 512: sl * 512 + tw]),
                waits=[ld, slot_mm.get(sl)])
            sqt = bld.op("s", lambda s, sl=sl, tw=tw: s.activation(
                sq[:, sl * 512: sl * 512 + tw], xb[:, sl * 512: sl * 512 + tw],
                AF.Square))
            slot_cast[sl] = sqt
            w0 = [cast] + ([evac.get(ti % 2)] if c == 0 and evac.get(ti % 2) else [])
            bld.mm(cx.PS[0:1, bs * 512: bs * 512 + tw], cx.ones_col,
                   xb[:, sl * 512: sl * 512 + tw],
                   start=(c == 0), stop=(c == KC - 1), waits=w0)
            mmtok = bld.mm(cx.PS[0:1, bq * 512: bq * 512 + tw], cx.ones_col,
                           sq[:, sl * 512: sl * 512 + tw],
                           start=(c == 0), stop=(c == KC - 1), waits=[sqt],
                           inc=True)
            slot_mm[sl] = mmtok
        e1 = bld.op("v", lambda v, t0=t0, tw=tw, bs=bs: v.tensor_copy(
            stat_s[:, t0:t0 + tw], cx.PS[0:1, bs * 512: bs * 512 + tw]),
            waits=[mmtok])
        e2 = bld.op("v", lambda v, t0=t0, tw=tw, bq=bq: v.tensor_copy(
            stat_q[:, t0:t0 + tw], cx.PS[0:1, bq * 512: bq * 512 + tw]))
        evac[ti % 2] = e2
    bld.op("v", lambda v: v.tensor_scalar_mul(r_mean[:, 0:T], stat_s[:, 0:T], 1.0 / D))
    bld.op("v", lambda v: v.tensor_scalar_mul(stat_q[:, 0:T], stat_q[:, 0:T], 1.0 / D))
    bld.op("v", lambda v: v.tensor_mul(stat_s[:, 0:T], r_mean[:, 0:T], r_mean[:, 0:T]))
    vv = bld.op("v", lambda v: v.tensor_sub(stat_q[:, 0:T], stat_q[:, 0:T], stat_s[:, 0:T]))
    epsc, epst = cx.const_col(bld, LN_EPS)
    lg = bld.op("s", lambda s: s.activation(r_rstd[:, 0:T], stat_q[:, 0:T], AF.Ln,
                                            bias=epsc[0:1, 0:1]), waits=[vv, epst])
    rst = bld.op("s", lambda s: s.activation(r_rstd[:, 0:T], r_rstd[:, 0:T], AF.Exp,
                                             scale=-0.5))
    d1 = bld.dma("d", cx.rows_dram[0:1, 0:T], r_mean[:, 0:T], waits=[bld.tok("v")], slot="lnrow")
    d2 = bld.dma("d", cx.rows_dram[1:2, 0:T], r_rstd[:, 0:T], waits=[rst], slot="lnrow")
    rd = cx.rows_dram.tensor
    b1 = bld.dma("d", MB[:, 0:T], bass.AP(rd, 0, [[0, 128], [1, T]]), waits=[d1, d2], slot="lnb")
    b2 = bld.dma("d", RB[:, 0:T], bass.AP(rd, 4096, [[0, 128], [1, T]]), slot="lnb")
    a1 = a2 = st = rl = None
    H2f = cx.H2.bitcast(F32)
    for c in range(KC):
        ld = bld.dma("d", xc[:, 0:T], x_hbm[c * 128:(c + 1) * 128, :],
                     waits=[a1 if a1 is not None else bld.tok("s")], slot="lnxa")
        a1 = bld.op("v", lambda v: v.tensor_sub(xc[:, 0:T], xc[:, 0:T], MB[:, 0:T]),
                    waits=[ld, b1, b2])
        if gb is None:
            a2 = bld.op("v", lambda v: v.tensor_tensor(
                xb[:, 0:T], xc[:, 0:T], RB[:, 0:T], ALU.mult),
                waits=[st] if st is not None else ())
            st = bld.dma("d", xn_hbm[c * 128:(c + 1) * 128, :], xb[:, 0:T],
                         waits=[a2], slot="lnst")
        else:
            gcol, bcol, gtok = gb
            a15 = bld.op("v", lambda v: v.tensor_tensor(
                xc[:, 0:T], xc[:, 0:T], RB[:, 0:T], ALU.mult))
            for hh in range(2):
                h0 = hh * (T // 2)
                hw = T // 2
                zt = H2f[:, 0:hw]
                a2 = bld.op("v", lambda v, h0=h0, hw=hw, zt=zt, c=c: v.tensor_scalar(
                    zt, xc[:, h0:h0 + hw], gcol[:, c:c + 1], bcol[:, c:c + 1],
                    ALU.mult, ALU.add), waits=[gtok, st, a15])
                if res_hbm is not None:
                    rl = bld.dma("d", cx.H1.bitcast(F32)[:, 0:hw],
                                 res_hbm[c * 128:(c + 1) * 128, h0:h0 + hw],
                                 waits=[getattr(emit_ln, "_radd", None)], slot="lnres")
                    a2 = bld.op("v", lambda v, zt=zt, hw=hw: v.tensor_add(
                        zt, zt, cx.H1.bitcast(F32)[:, 0:hw]), waits=[rl])
                    emit_ln._radd = a2
                st = bld.dma("d", xn_hbm[c * 128:(c + 1) * 128, h0:h0 + hw], zt,
                             waits=[a2], slot="lnst")
    bld.barrier()


def emit_proj(bld, cx, w_sl, w_tok, rhs_hbm, dout, epilogue, rt, extra_streams=()):
    T = cx.T
    nmc = dout // 128
    nb = 8 if nmc >= 8 else max(2, nmc)
    bank_free = [None] * nb
    tts = ttiles(T)
    half = [rt[:, 0:4096], rt[:, 4096:8192]]
    rhs_toks = [None, None]
    for ti, (t0, tw) in enumerate(tts):
        hb = ti % 2
        war = [bank_free[i] for i in range(nb)] if ti >= 2 else []
        rhs_toks[hb] = bld.dma(
            "d", bass.AP(rt, hb * 4096, [[8192, 128], [512, KC], [1, tw]]),
            rhs_hbm[:, t0:t0 + tw].rearrange("(kc p) t -> p kc t", p=128),
            waits=[w for w in war if w is not None], slot=f"rta{hb}")
        extras = []
        for (eh,) in extra_streams:
            extras.append(bld.dma(
                "d", bass.AP(cx.RT1, hb * 4096, [[8192, 128], [512, KC], [1, tw]]),
                eh[:, t0:t0 + tw].rearrange("(kc p) t -> p kc t", p=128),
                waits=[w for w in war if w is not None], slot=f"rtb{hb}"))
        for mc in range(nmc):
            bi = mc % nb
            w0 = [w_tok, rhs_toks[hb]] + extras
            if bank_free[bi] is not None:
                w0.append(bank_free[bi])
            pslice = cx.bank(bi, tw)
            for kc in range(KC):
                tok = bld.mm(pslice, w_sl(kc, mc),
                             rt[:, hb * 4096 + kc * 512: hb * 4096 + kc * 512 + tw],
                             start=(kc == 0), stop=(kc == KC - 1),
                             waits=w0 if kc == 0 else (), inc=(kc == KC - 1))
            bank_free[bi] = epilogue(mc, t0, tw, pslice, tok, hb)
    bld.barrier()


def emit_l1ffn(bld, cx, x_hbm, xout_hbm, wts, scr):
    T = cx.T
    emit_ln(bld, cx, x_hbm, scr["xn"])

    mu_r, mu_k = cx.SM[:, 0:8], cx.SM[:, 8:16]
    mtoks = []
    for c in range(KC):
        mtoks.append(bld.dma("d", mu_r[:, c:c + 1], wts["mu_r"][c * 128:(c + 1) * 128, 0:1], slot="mu"))
        mtoks.append(bld.dma("d", mu_k[:, c:c + 1], wts["mu_k"][c * 128:(c + 1) * 128, 0:1], slot="mu"))
    xnc, out_b, t1 = cx.H1, cx.H2, cx.F1
    a3 = st = None
    for c in range(KC):
        z = bld.op("g", lambda g: g.memset(xnc[:, 0:1], 0.0),
                   waits=[a3] if a3 is not None else ())
        ld = bld.dma("d", xnc[:, 1:T + 1], scr["xn"][c * 128:(c + 1) * 128, :],
                     waits=[a3] if a3 is not None else (), slot="mixld")
        for mu, dst in ((mu_r, scr["xr"]), (mu_k, scr["xk"])):
            a1 = bld.op("v", lambda v, mu=mu, c=c: v.tensor_scalar(
                t1[:, 0:T], xnc[:, 0:T], mu[:, c:c + 1], None, ALU.mult),
                waits=[ld, z] + mtoks)
            a2 = bld.op("v", lambda v: v.tensor_sub(t1[:, 0:T], xnc[:, 0:T], t1[:, 0:T]))
            a3 = bld.op("v", lambda v, mu=mu, c=c: v.scalar_tensor_tensor(
                out_b[:, 0:T], xnc[:, 1:T + 1], mu[:, c:c + 1], t1[:, 0:T],
                ALU.mult, ALU.add), waits=[st] if st is not None else ())
            st = bld.dma("d", dst[c * 128:(c + 1) * 128, :], out_b[:, 0:T], waits=[a3], slot="mixst")
    bld.barrier()

    wk_sl, wk_tok = emit_weight_load(bld, cx, 0, wts["WkT"], D, D)
    kstage = cx.H1
    kst_tok = {}

    def ep_k(mc, t0, tw, pslice, tok, hb):
        slot = (mc % 4)
        sl = kstage[:, slot * 1024: slot * 1024 + tw]
        sl2 = kstage[:, slot * 1024 + 512: slot * 1024 + 512 + tw]
        war = [kst_tok[slot]] if slot in kst_tok else []
        a1 = bld.op("v", lambda v: v.tensor_scalar_max(sl, pslice, 0.0),
                    waits=[tok] + war)
        a2 = bld.op("v", lambda v: v.tensor_tensor(sl2, sl, sl, ALU.mult))
        kst_tok[slot] = bld.dma("d", scr["kact"][mc * 128:(mc + 1) * 128, t0:t0 + tw],
                                sl2, waits=[a2], slot=f"kst{slot}")
        return a1
    emit_proj(bld, cx, wk_sl, wk_tok, scr["xk"], D, ep_k, cx.RT0)

    wr_sl, wr_tok = emit_weight_load(bld, cx, 8192, wts["WrT"], D, D)
    wv_sl, wv_tok = emit_weight_load(bld, cx, 16384, wts["WvT"], D, D)
    H2f = cx.H2.bitcast(F32)
    tts = ttiles(T)
    bank_free = [None] * 8
    rhs_toks = [None, None]
    ka_toks = [None, None]
    prev = {}
    for ti, (t0, tw) in enumerate(tts):
        hb = ti % 2
        war = [b for b in bank_free if b is not None] if ti >= 2 else []
        rhs_toks[hb] = bld.dma(
            "d", bass.AP(cx.RT0, hb * 4096, [[8192, 128], [512, KC], [1, tw]]),
            scr["xr"][:, t0:t0 + tw].rearrange("(kc p) t -> p kc t", p=128), waits=war,
            slot=f"rta{hb}")
        ka_toks[hb] = bld.dma(
            "d", bass.AP(cx.RT1, hb * 4096, [[8192, 128], [512, KC], [1, tw]]),
            scr["kact"][:, t0:t0 + tw].rearrange("(kc p) t -> p kc t", p=128), waits=war,
            slot=f"rtb{hb}")
        for mc in range(KC):
            br, bo = (2 * mc) % 8, (2 * mc + 1) % 8
            sx = (2 * ti + mc) % 4
            sr = mc % 2
            xl = bld.dma("d", cx.F1[:, sx * 1024: sx * 1024 + tw],
                         x_hbm[mc * 128:(mc + 1) * 128, t0:t0 + tw],
                         waits=[prev.get(("x", sx))], slot=f"rox{sx}")
            w0 = [wr_tok, rhs_toks[hb]]
            if bank_free[br] is not None:
                w0.append(bank_free[br])
            for kc in range(KC):
                rtk = bld.mm(cx.bank(br, tw), wr_sl(kc, mc),
                             cx.RT0[:, hb * 4096 + kc * 512: hb * 4096 + kc * 512 + tw],
                             start=(kc == 0), stop=(kc == KC - 1),
                             waits=w0 if kc == 0 else (), inc=(kc == KC - 1))
            w1 = [wv_tok, ka_toks[hb]]
            if bank_free[bo] is not None:
                w1.append(bank_free[bo])
            for kc in range(KC):
                otk = bld.mm(cx.bank(bo, tw), wv_sl(kc, mc),
                             cx.RT1[:, hb * 4096 + kc * 512: hb * 4096 + kc * 512 + tw],
                             start=(kc == 0), stop=(kc == KC - 1),
                             waits=w1 if kc == 0 else (), inc=(kc == KC - 1))
            rbuf = H2f[:, 1024 + sr * 512: 1024 + sr * 512 + tw]
            stage = H2f[:, sr * 512: sr * 512 + tw]
            s1 = bld.op("s", lambda s, rbuf=rbuf, br=br, tw=tw: s.activation(
                rbuf, cx.bank(br, tw), AF.Sigmoid),
                waits=[rtk, prev.get(("r", sr))])
            m1 = bld.op("v", lambda v, stage=stage, rbuf=rbuf, bo=bo, tw=tw:
                        v.tensor_tensor(stage, rbuf, cx.bank(bo, tw), ALU.mult),
                        waits=[otk, s1, prev.get(("st", sr))])
            m2 = bld.op("v", lambda v, stage=stage, sx=sx, tw=tw: v.tensor_add(
                stage, stage, cx.F1[:, sx * 1024: sx * 1024 + tw]), waits=[xl])
            stok = bld.dma("d", xout_hbm[mc * 128:(mc + 1) * 128, t0:t0 + tw],
                           stage, waits=[m2], slot=f"yst{sr}")
            prev[("r", sr)] = m1
            prev[("st", sr)] = stok
            prev[("x", sx)] = m2
            bank_free[br] = m1
            bank_free[bo] = m1
    bld.barrier()


def load_bias_cols(bld, cx, smcol, hbm_ap, slot):
    toks = []
    for c in range(KC):
        toks.append(bld.dma("d", cx.SM[:, smcol + c: smcol + c + 1],
                            hbm_ap[c * 128:(c + 1) * 128, 0:1], slot=slot))
    return toks[-1]


def emit_l0aa(bld, cx, x_hbm, xout_hbm, wts, scr):
    T = cx.T
    emit_ln(bld, cx, x_hbm, scr["xn"])

    wi_sl, wi_tok = emit_weight_load(bld, cx, 0, wts["WiT"], D, D)
    wg_sl, wg_tok = emit_weight_load(bld, cx, 8192, wts["WgT"], D, D)
    bi_tok = load_bias_cols(bld, cx, 0, wts["bi"], "bia")
    bg_tok = load_bias_cols(bld, cx, 8, wts["bg"], "bia")
    H2f = cx.H2.bitcast(F32)
    tts = ttiles(T)
    rhs_war = {}
    hstores = {}
    scan_cst = {}
    evac_done = {}
    sc_prev = None
    sc_prev_w = 512
    for mc in range(KC):
        for ti, (t0, tw) in enumerate(tts):
            hb = ti % 2
            bu, bg_ = 2 * hb, 2 * hb + 1
            rhs_tok = bld.dma(
                "d", bass.AP(cx.RT0, hb * 4096, [[8192, 128], [512, KC], [1, tw]]),
                scr["xn"][:, t0:t0 + tw].rearrange("(kc p) t -> p kc t", p=128),
                waits=[rhs_war.get(hb)], slot=f"rta{hb}")
            w0 = [wi_tok, wg_tok, bi_tok, bg_tok, rhs_tok] + list(evac_done.get(hb, ()))
            for kc in range(KC):
                bld.mm(cx.bank(bu, tw), wi_sl(kc, mc),
                       cx.RT0[:, hb * 4096 + kc * 512: hb * 4096 + kc * 512 + tw],
                       start=(kc == 0), stop=(kc == KC - 1),
                       waits=w0 if kc == 0 else ())
                gtk = bld.mm(cx.bank(bg_, tw), wg_sl(kc, mc),
                             cx.RT0[:, hb * 4096 + kc * 512: hb * 4096 + kc * 512 + tw],
                             start=(kc == 0), stop=(kc == KC - 1),
                             inc=(kc == KC - 1))
            rhs_war[hb] = gtk
            gt = H2f[:, 512:512 + tw]
            ut = H2f[:, 0:tw]
            bt = H2f[:, 1024:1024 + tw]
            sg = bld.op("s", lambda s, gt=gt, bg_=bg_, mc=mc, tw=tw: s.activation(
                gt, cx.bank(bg_, tw), AF.Sigmoid, bias=cx.SM[:, 8 + mc: 9 + mc]),
                waits=[gtk, sc_prev])
            uu = bld.op("v", lambda v, ut=ut, bu=bu, mc=mc, tw=tw: v.tensor_scalar_add(
                ut, cx.bank(bu, tw), cx.SM[:, 0 + mc: 1 + mc]), waits=[gtk])
            m1 = bld.op("v", lambda v, bt=bt, gt=gt, ut=ut: v.tensor_tensor(
                bt, gt, ut, ALU.mult), waits=[sg])
            m2 = bld.op("v", lambda v, bt=bt, ut=ut: v.tensor_sub(bt, ut, bt))
            evac_done[hb] = (sg, uu)
            hsl = cx.F1[:, hb * 512: hb * 512 + tw]
            if sc_prev is None:
                sc = bld.op("v", lambda v, hsl=hsl, gt=gt, bt=bt: v.tensor_tensor_scan(
                    hsl, gt, bt, 0.0, ALU.mult, ALU.add),
                    waits=[scan_cst.get(hb)])
            else:
                carry = cx.F1[:, (1 - hb) * 512 + sc_prev_w - 1: (1 - hb) * 512 + sc_prev_w]
                sc = bld.op("v", lambda v, hsl=hsl, gt=gt, bt=bt, carry=carry:
                            v.tensor_tensor_scan(hsl, gt, bt, carry, ALU.mult, ALU.add),
                            waits=[scan_cst.get(hb)])
            sc_prev, sc_prev_w = sc, tw
            hbuf = cx.H1[:, hb * 512: hb * 512 + tw]
            cstt = bld.op("s", lambda s, hbuf=hbuf, hsl=hsl: s.copy(hbuf, hsl),
                          waits=[sc, hstores.get(hb)])
            scan_cst[hb] = cstt
            hstores[hb] = bld.dma(
                "d", scr["b1"][mc * 128:(mc + 1) * 128, t0:t0 + tw], hbuf,
                waits=[cstt], slot=f"hst{hb}")
    bld.barrier()

    wo_sl, wo_tok = emit_weight_load(bld, cx, 16384, wts["WoT"], D, D)
    bo_tok = load_bias_cols(bld, cx, 16, wts["bo"], "bia")
    prev = {}

    def ep_o(mc, t0, tw, pslice, tok, hb):
        sx = (mc % 2) + 2 * hb
        xl = bld.dma("d", cx.F1[:, sx * 1024: sx * 1024 + tw],
                     x_hbm[mc * 128:(mc + 1) * 128, t0:t0 + tw],
                     waits=[prev.get(("x", sx))], slot=f"rox{sx}")
        stage = cx.H2.bitcast(F32)[:, sx * 512: sx * 512 + tw]
        a1 = bld.op("v", lambda v, stage=stage, tw=tw: v.tensor_scalar_add(
            stage, pslice, cx.SM[:, 16 + mc: 17 + mc]),
            waits=[tok, bo_tok, prev.get(("st", sx))])
        a2 = bld.op("v", lambda v, stage=stage, sx=sx, tw=tw: v.tensor_add(
            stage, stage, cx.F1[:, sx * 1024: sx * 1024 + tw]), waits=[xl])
        stok = bld.dma("d", xout_hbm[mc * 128:(mc + 1) * 128, t0:t0 + tw],
                       stage, waits=[a2], slot=f"yst{sx}")
        prev[("x", sx)] = a2
        prev[("st", sx)] = stok
        return a1
    emit_proj(bld, cx, wo_sl, wo_tok, scr["b1"], D, ep_o, cx.RT0)


def make_res_epilogue(bld, cx, x_hbm, xout_hbm, bias_smcol=None, bias_tok=None):
    prev = {}

    def ep(mc, t0, tw, pslice, tok, hb):
        sx = (mc % 2) + 2 * hb
        xl = bld.dma("d", cx.F1[:, sx * 1024: sx * 1024 + tw],
                     x_hbm[mc * 128:(mc + 1) * 128, t0:t0 + tw],
                     waits=[prev.get(("x", sx))], slot=f"rox{sx}")
        stage = cx.H2.bitcast(F32)[:, sx * 512: sx * 512 + tw]
        if bias_smcol is not None:
            a1 = bld.op("v", lambda v, stage=stage, mc=mc: v.tensor_scalar_add(
                stage, pslice, cx.SM[:, bias_smcol + mc: bias_smcol + mc + 1]),
                waits=[tok, bias_tok, prev.get(("st", sx))])
        else:
            a1 = bld.op("v", lambda v, stage=stage: v.tensor_copy(stage, pslice),
                        waits=[tok, prev.get(("st", sx))])
        a2 = bld.op("v", lambda v, stage=stage, sx=sx, tw=tw: v.tensor_add(
            stage, stage, cx.F1[:, sx * 1024: sx * 1024 + tw]), waits=[xl])
        stok = bld.dma("d", xout_hbm[mc * 128:(mc + 1) * 128, t0:t0 + tw],
                       stage, waits=[a2], slot=f"yst{sx}")
        prev[("x", sx)] = a2
        prev[("st", sx)] = stok
        return a1
    return ep


def emit_qkv_pass(bld, cx, rhs_hbm, wT_hbm, b_hbm, dst_hbm, rope_sign, cs):
    T = cx.T
    w_sl, w_tok = emit_weight_load(bld, cx, 0, wT_hbm, D, D)
    b_tok = load_bias_cols(bld, cx, 24, b_hbm, "bqkv") if b_hbm is not None else None
    RTf = cx.RT1.bitcast(F32)
    H2f = cx.H2.bitcast(F32)
    tts = ttiles(T)
    bank_free = [None] * 8
    rhs_war = {}
    cs_war = None
    st_war = {}
    rope = rope_sign is not None
    for ti, (t0, tw) in enumerate(tts):
        hb = ti % 2
        rhs_tok = bld.dma(
            "d", bass.AP(cx.RT0, hb * 4096, [[8192, 128], [512, KC], [1, tw]]),
            rhs_hbm[:, t0:t0 + tw].rearrange("(kc p) t -> p kc t", p=128),
            waits=[rhs_war.get(hb)], slot=f"rta{hb}")
        ctoks = []
        if rope:
            ctoks.append(bld.dma(
                "d", bass.AP(RTf, 0, [[RTf.shape[1], 128], [512, 4], [1, tw]]),
                cs[0][:, t0:t0 + tw].rearrange("(j p) t -> p j t", p=128),
                waits=[cs_war], slot="cs0"))
            ctoks.append(bld.dma(
                "d", bass.AP(RTf, 2048, [[RTf.shape[1], 128], [512, 4], [1, tw]]),
                cs[1][:, t0:t0 + tw].rearrange("(j p) t -> p j t", p=128),
                waits=[cs_war], slot="cs1"))
        mm_last = None
        for mc in range(KC):
            w0 = [w_tok, b_tok, rhs_tok, bank_free[mc]]
            for kc in range(KC):
                tok = bld.mm(cx.bank(mc, tw), w_sl(kc, mc),
                             cx.RT0[:, hb * 4096 + kc * 512: hb * 4096 + kc * 512 + tw],
                             start=(kc == 0), stop=(kc == KC - 1),
                             waits=w0 if kc == 0 else (), inc=(kc == KC - 1))
            mm_last = tok
            rhs_war[hb] = tok
            if not rope:
                sl = mc % 2
                if b_hbm is not None:
                    ev = bld.op("v", lambda v, sl=sl, mc=mc, tw=tw: v.tensor_scalar(
                        cx.H1[:, sl * 512: sl * 512 + tw], cx.bank(mc, tw),
                        cx.SM[:, 24 + mc: 25 + mc], None, ALU.add),
                        waits=[tok, st_war.get(sl)])
                else:
                    ev = bld.op("v", lambda v, sl=sl, mc=mc, tw=tw: v.tensor_copy(
                        cx.H1[:, sl * 512: sl * 512 + tw], cx.bank(mc, tw)),
                        waits=[tok, st_war.get(sl)])
                st_war[sl] = bld.dma(
                    "d", dst_hbm[mc * 128:(mc + 1) * 128, t0:t0 + tw],
                    cx.H1[:, sl * 512: sl * 512 + tw], waits=[ev], slot=f"qst{sl}")
                bank_free[mc] = ev
            else:
                ev = bld.op("v", lambda v, mc=mc, tw=tw: v.tensor_scalar(
                    cx.F1[:, mc * 512: mc * 512 + tw], cx.bank(mc, tw),
                    cx.SM[:, 24 + mc: 25 + mc], None, ALU.add), waits=[tok])
                bank_free[mc] = ev
        if rope:
            for mc in range(KC):
                j4 = mc % 4
                par = mc + 4 if mc < 4 else mc - 4
                cos_t = RTf[:, j4 * 512: j4 * 512 + tw]
                sin_t = RTf[:, 2048 + j4 * 512: 2048 + j4 * 512 + tw]
                tmp1 = H2f[:, 0:tw]
                tmp2 = H2f[:, 512:512 + tw]
                sl = mc % 2
                m1 = bld.op("v", lambda v, tmp1=tmp1, mc=mc, cos_t=cos_t, tw=tw:
                            v.tensor_tensor(tmp1, cx.F1[:, mc * 512: mc * 512 + tw],
                                            cos_t, ALU.mult), waits=ctoks)
                m2 = bld.op("v", lambda v, tmp2=tmp2, par=par, sin_t=sin_t, tw=tw:
                            v.tensor_tensor(tmp2, cx.F1[:, par * 512: par * 512 + tw],
                                            sin_t, ALU.mult))
                if mc < 4:
                    m3 = bld.op("v", lambda v, sl=sl, tmp1=tmp1, tmp2=tmp2, tw=tw:
                                v.tensor_sub(cx.H1[:, sl * 512: sl * 512 + tw],
                                             tmp1, tmp2), waits=[st_war.get(sl)])
                else:
                    m3 = bld.op("v", lambda v, sl=sl, tmp1=tmp1, tmp2=tmp2, tw=tw:
                                v.tensor_add(cx.H1[:, sl * 512: sl * 512 + tw],
                                             tmp1, tmp2), waits=[st_war.get(sl)])
                st_war[sl] = bld.dma(
                    "d", dst_hbm[mc * 128:(mc + 1) * 128, t0:t0 + tw],
                    cx.H1[:, sl * 512: sl * 512 + tw], waits=[m3], slot=f"qst{sl}")
            cs_war = m3
    bld.barrier()


def emit_l1ab(bld, cx, x_hbm, xout_hbm, wts, scr, consts):
    T = cx.T
    WIN = 512
    emit_ln(bld, cx, x_hbm, scr["xn"])
    emit_qkv_pass(bld, cx, scr["xn"], wts["WqT"], wts["bq"], scr["xr"], +1,
                  (consts["cos"], consts["sin"]))
    emit_qkv_pass(bld, cx, scr["xn"], wts["WkT"], wts["bk"], scr["xk"], +1,
                  (consts["cos"], consts["sin"]))
    emit_qkv_pass(bld, cx, scr["xn"], wts["WvT"], wts["bv"], scr["kact"], None, None)

    mrow = consts["mask"]
    mtoks = []
    for idx in range(8):
        d0 = idx * 128 - 512
        off = 4096 + d0 - 511
        mtoks.append(bld.dma(
            "g", cx.RT0[:, idx * 1024:(idx + 1) * 1024],
            bass.AP(mrow.tensor, off, [[1, 128], [1, 1024]]), slot="mrep"))
    mtok = mtoks[-1]
    sc40, sc40t = cx.const_col(bld, -40.0)

    H2 = cx.H2
    nqb = T // WIN
    onesv = bld.op("g", lambda g: g.memset(
        bass.AP(cx.H2, 1024 + 64, [[cx.H2.shape[1], 128], [65, 16]]), 1.0))
    set_war = {}
    bank_war = {}
    for h in range(16):
        for qb in range(nqb):
            q0 = qb * WIN
            kv0 = max(0, q0 - WIN)
            kw = q0 + WIN - kv0
            nkc = kw // 128
            st_ = qb % 2
            base = st_ * 1536
            Qt = cx.H1[0:64, base + 1024: base + 1536]
            Kc = cx.H1[0:64, base: base + 1024]
            Vc = cx.H1[64:128, base: base + 1024]
            VT = None
            vt0 = 1024 + st_ * 520
            qtok = bld.dma("d", Qt[:, 0:WIN], scr["xr"][h * 64:(h + 1) * 64, q0:q0 + WIN],
                           waits=[set_war.get(("q", st_))], slot=f"qt{st_}")
            ktok = bld.dma("d", Kc[:, 0:kw], scr["xk"][h * 64:(h + 1) * 64, kv0:kv0 + kw],
                           waits=[set_war.get(("q", st_))], slot=f"kt{st_}")
            vtok = bld.dma("d", Vc[:, 0:kw], scr["kact"][h * 64:(h + 1) * 64, kv0:kv0 + kw],
                           waits=[set_war.get(("v", st_))], slot=f"vt{st_}")
            evt = None
            for j in range(nkc):
                pb = cx.PSb[:, (6 + j % 2) * 1024: (6 + j % 2) * 1024 + 64]
                trw = [vtok, bank_war.get(("tr", j % 2))]
                tr = bld.op(
                    "t", lambda t, pb=pb, j=j, Vc=Vc: t.transpose(
                        pb, Vc[:, j * 128:(j + 1) * 128], cx.ident[64:128, 64:128]),
                    waits=trw, inc=True)
                ev = bld.op("v", lambda v, pb=pb, j=j, vt0=vt0: v.tensor_copy(
                    H2[:, vt0 + j * 65: vt0 + j * 65 + 64], pb),
                    waits=[tr, set_war.get(("vt", st_)), onesv])
                bank_war[("tr", j % 2)] = ev
                evt = ev
            lastpv = None
            par = qb % 2
            pvc = 2048 + par * 512
            pend = []

            def pv_emit(ent, par=par, pvc=pvc, evt=evt, nkc=nkc, vt0=vt0):
                tkc, mkt, E2m = ent
                w = [mkt, evt, bank_war.get(("pv", par))] if tkc == 0 else [mkt]
                pv = bld.mm(cx.PS[0:65, pvc:pvc + WIN],
                            H2[:, vt0 + tkc * 65: vt0 + tkc * 65 + 65], E2m,
                            start=(tkc == 0), stop=(tkc == nkc - 1),
                            waits=w, inc=True)
                bank_war[("E2", tkc % 4)] = pv
                return pv

            for tkc in range(nkc):
                sE = tkc % 4
                idx = (kv0 + tkc * 128 - q0 + 512) // 128
                Et = cx.RT1[:, sE * 512: sE * 512 + WIN]
                E2 = cx.RT1[:, 2048 + sE * 512: 2048 + sE * 512 + WIN]
                smm = bld.mm(cx.bank(sE, WIN), Kc[:, tkc * 128:(tkc + 1) * 128],
                             Qt[:, 0:WIN], start=True, stop=True,
                             waits=[qtok, ktok, bank_war.get(("S", sE))], inc=True)
                ex = bld.op("s", lambda s, Et=Et, sE=sE: s.activation(
                    Et, cx.bank(sE, WIN), AF.Exp, bias=sc40, scale=0.125),
                    waits=[smm, sc40t, bank_war.get(("E", sE))])
                bank_war[("S", sE)] = ex
                mask_ap = bass.AP(cx.RT0, idx * 1024 + 511, [[8192, 128], [-1, WIN]])
                mk = bld.op("v", lambda v, E2=E2, Et=Et, mask_ap=mask_ap:
                            v.tensor_tensor(E2, Et, mask_ap, ALU.mult),
                            waits=[ex, mtok, bank_war.get(("E2", sE))])
                bank_war[("E", sE)] = mk
                pend.append((tkc, mk, E2))
                if len(pend) > 2:
                    lastpv = pv_emit(pend.pop(0))
            qt_war = bld.tok("t")
            while pend:
                lastpv = pv_emit(pend.pop(0))
            set_war[("q", st_)] = qt_war
            set_war[("vt", st_)] = lastpv
            set_war[("v", st_)] = bld.tok("t")
            srow = cx.SM[64:65, 1024:1024 + WIN]
            rc = bld.op("v", lambda v, srow=srow, pvc=pvc: v.reciprocal(
                srow, cx.PS[64:65, pvc:pvc + WIN]),
                waits=[lastpv, bank_war.get(("srow",))])
            rd = bld.dma("d", cx.rows_dram[2:3, 0:WIN], srow,
                         waits=[rc, bank_war.get(("rb",))], slot="arow")
            bank_war[("srow",)] = rd
            rep = cx.F1[0:64, 0:WIN]
            rb = bld.dma("d", rep, bass.AP(cx.rows_dram.tensor, 2 * 4096, [[0, 64], [1, WIN]]),
                         waits=[rd, bank_war.get(("rep",))], slot="arep")
            bank_war[("rb",)] = rb
            stg = H2[0:64, 3104: 3104 + WIN]
            cn = bld.op("v", lambda v, stg=stg, rep=rep, pvc=pvc: v.tensor_tensor(
                stg, cx.PS[0:64, pvc:pvc + WIN], rep, ALU.mult),
                waits=[rb, bank_war.get(("stg",))])
            bank_war[("pv", par)] = cn
            bank_war[("rep",)] = cn
            bank_war[("stg",)] = bld.dma(
                "d", scr["b1"][h * 64:(h + 1) * 64, q0:q0 + WIN], stg,
                waits=[cn], slot="ctxst")
    bld.barrier()

    wo_sl, wo_tok = emit_weight_load(bld, cx, 0, wts["WoT"], D, D)
    bo_tok = load_bias_cols(bld, cx, 24, wts["bo"], "bqkv")
    ep = make_res_epilogue(bld, cx, x_hbm, xout_hbm, 24, bo_tok)
    emit_proj(bld, cx, wo_sl, wo_tok, scr["b1"], D, ep, cx.RT1)


def emit_l1aa(bld, cx, x_hbm, xout_hbm, wts, scr, cconsts):
    T = cx.T
    emit_ln(bld, cx, x_hbm, scr["xn"])
    emit_qkv_pass(bld, cx, scr["xn"], wts["Wq"], None, scr["xr"], None, None)
    emit_qkv_pass(bld, cx, scr["xn"], wts["Wk"], None, scr["xk"], None, None)
    emit_qkv_pass(bld, cx, scr["xn"], wts["Wv"], None, scr["kact"], None, None)

    toep_dram = cx.dram("toep_dram", [16, 8192], BF16)
    re_sb = cx.SMb[0:32, 2560:2576]
    rtok = bld.dma("g", re_sb, wts["rel_emb"], slot="re")
    oh = cconsts["onehotT"]
    oht = cx.H1[0:32, 0:512]
    erow = cx.SMb[0:16, 2048:2560]
    mmt = etok = stok = None
    for i in range(16):
        otok = bld.dma("g", oht, oh[:, i * 512:(i + 1) * 512],
                       waits=[mmt], slot="oht")
        mmt = bld.mm(cx.PS[0:16, 0:512], re_sb, oht, start=True, stop=True,
                     waits=[rtok, otok, etok], inc=True)
        etok = bld.op("s", lambda s: s.activation(erow, cx.PS[0:16, 0:512], AF.Exp),
                      waits=[mmt, stok])
        stok = bld.dma("d", toep_dram[:, i * 512:(i + 1) * 512], erow,
                       waits=[etok], slot="toepst")
    bld.barrier()

    sc40, sc40t = cx.const_col(bld, -40.0)
    H2 = cx.H2
    ntq = T // 512
    nkc = T // 128
    U = 8064
    set_war = {}
    bank_war = {}
    onesv = bld.op("g", lambda g: g.memset(
        bass.AP(cx.H2, 1024 + 64, [[cx.H2.shape[1], 128], [65, nkc]]), 1.0))
    F1b = cx.F1.bitcast(BF16)
    for h in range(16):
        hs = h % 2
        KV = cx.H1 if hs == 0 else F1b
        Kh = KV[0:64, 0:T]
        Vh = KV[64:128, 0:T]
        ktok = bld.dma("d", Kh, scr["xk"][h * 64:(h + 1) * 64, :],
                       waits=[set_war.get(("kh", hs))], slot=f"kt{hs}")
        vtok = bld.dma("d", Vh, scr["kact"][h * 64:(h + 1) * 64, :],
                       waits=[set_war.get(("kh", hs))], slot=f"vt{hs}")
        reptok = bld.dma("g", cx.RT1[:, 0:U],
                         bass.AP(toep_dram.tensor, h * 8192, [[1, 128], [1, U]]),
                         waits=[set_war.get("rep")], slot="reph")
        evt = None
        for j in range(nkc):
            pb = cx.PSb[:, (6 + j % 2) * 1024: (6 + j % 2) * 1024 + 64]
            tr = bld.op("t", lambda t, pb=pb, j=j, Kh=Kh, Vh=Vh: t.transpose(
                pb, Vh[:, j * 128:(j + 1) * 128], cx.ident[64:128, 64:128]),
                waits=[vtok, bank_war.get(("tr", j % 2))], inc=True)
            evt = bld.op("v", lambda v, pb=pb, j=j: v.tensor_copy(
                H2[:, 1024 + j * 65: 1024 + j * 65 + 64], pb),
                waits=[tr, set_war.get("vt"), onesv])
            bank_war[("tr", j % 2)] = evt
        for tqt in range(ntq):
            q0 = tqt * 512
            par = tqt % 2
            pvc = 2048 + par * 512
            Qt = H2[0:64, 3136:3648]
            qtok = bld.dma("d", Qt, scr["xr"][h * 64:(h + 1) * 64, q0:q0 + 512],
                           waits=[set_war.get("qt")], slot="qt0")
            lastpv = None
            pend = []

            def pv_emit(ent, par=par, pvc=pvc, evt=evt, nkc=nkc):
                tkc, mkt, E2m = ent
                w = [mkt, evt, bank_war.get(("pv", par))] if tkc == 0 else [mkt]
                pv = bld.mm(cx.PS[0:65, pvc:pvc + 512],
                            H2[:, 1024 + tkc * 65: 1024 + tkc * 65 + 65], E2m,
                            start=(tkc == 0), stop=(tkc == nkc - 1),
                            waits=w, inc=True)
                bank_war[("E2", tkc % 4)] = pv
                return pv

            for tkc in range(nkc):
                sE = tkc % 4
                Et = cx.RT0[:, sE * 512: sE * 512 + 512]
                E2 = cx.RT0[:, 2048 + sE * 512: 2048 + sE * 512 + 512]
                smm = bld.mm(cx.bank(sE, 512), Kh[:, tkc * 128:(tkc + 1) * 128],
                             Qt, start=True, stop=True,
                             waits=[qtok, ktok, bank_war.get(("S", sE))], inc=True)
                ex = bld.op("s", lambda s, Et=Et, sE=sE: s.activation(
                    Et, cx.bank(sE, 512), AF.Exp, bias=sc40),
                    waits=[smm, sc40t, bank_war.get(("E", sE))])
                bank_war[("S", sE)] = ex
                C = 4095 + tkc * 128 - q0
                eap = bass.AP(cx.RT1, C, [[8192, 128], [-1, 512]])
                mk = bld.op("v", lambda v, E2=E2, Et=Et, eap=eap:
                            v.tensor_tensor(E2, Et, eap, ALU.mult),
                            waits=[ex, reptok, bank_war.get(("E2", sE))])
                bank_war[("E", sE)] = mk
                pend.append((tkc, mk, E2))
                if len(pend) > 2:
                    lastpv = pv_emit(pend.pop(0))
            qt_war = bld.tok("t")
            while pend:
                lastpv = pv_emit(pend.pop(0))
            set_war["qt"] = qt_war
            srow = cx.SM[64:65, 1024:1536]
            rc = bld.op("v", lambda v, srow=srow, pvc=pvc: v.reciprocal(
                srow, cx.PS[64:65, pvc:pvc + 512]),
                waits=[lastpv, bank_war.get(("srow",))])
            rd = bld.dma("d", cx.rows_dram[2:3, 0:512], srow,
                         waits=[rc, bank_war.get(("rb",))], slot="arow")
            bank_war[("srow",)] = rd
            rep = cx.H2.bitcast(F32)[0:64, 0:512]
            rb = bld.dma("d", rep,
                         bass.AP(cx.rows_dram.tensor, 2 * 4096, [[0, 64], [1, 512]]),
                         waits=[rd, bank_war.get(("rep",))], slot="arep")
            bank_war[("rb",)] = rb
            stg = H2[0:64, 3648:4160]
            cn = bld.op("v", lambda v, stg=stg, rep=rep, pvc=pvc: v.tensor_tensor(
                stg, cx.PS[0:64, pvc:pvc + 512], rep, ALU.mult),
                waits=[rb, bank_war.get(("stg",))])
            bank_war[("pv", par)] = cn
            bank_war[("rep",)] = cn
            bank_war[("stg",)] = bld.dma(
                "d", scr["b1"][h * 64:(h + 1) * 64, q0:q0 + 512], stg,
                waits=[cn], slot="ctxst")
        set_war[("kh", hs)] = bld.tok("t")
        set_war["vt"] = bank_war.get(("pv", (ntq - 1) % 2))
        set_war["rep"] = bld.tok("v")
    bld.barrier()

    wo_sl, wo_tok = emit_weight_load(bld, cx, 0, wts["Wo"], D, D)
    ep = make_res_epilogue(bld, cx, x_hbm, xout_hbm)
    emit_proj(bld, cx, wo_sl, wo_tok, scr["b1"], D, ep, cx.RT1)


def emit_l0ffn(bld, cx, x_hbm, xout_hbm, wts, scr):
    T = cx.T
    nc = cx.nc
    emit_ln(bld, cx, x_hbm, scr["xn"])
    acc_dram = cx.dram("acc_dram", [D, cx.T], F32)
    rw_dram = cx.dram("rw_dram", [8, 4096], F32)
    tts = ttiles(T)

    wr_sl, wr_tok = emit_weight_load(bld, cx, 0, wts["WrT"], D, 8)
    brc = cx.SM[0:8, 72:73]
    brt = bld.dma("d", brc, wts["br"][0:8, 0:1], slot="bia")
    idf = cx.SM[:, 1536:1664]
    idt = bld.dma("g", idf, wts["identf"], slot="idf")
    lgrow = cx.F1[0:8, 0:T]
    ev = None
    for ti, (t0, tw) in enumerate(tts):
        hb = ti % 2
        rhs_tok = bld.dma(
            "d", bass.AP(cx.RT0, hb * 4096, [[8192, 128], [512, KC], [1, tw]]),
            scr["xn"][:, t0:t0 + tw].rearrange("(kc p) t -> p kc t", p=128),
            waits=[Tok(bld.sem["t"], bld.cnt["t"]) if ti >= 2 else None],
            slot=f"rta{hb}")
        for kc in range(KC):
            mmt = bld.mm(cx.PS[0:8, 0:tw], wr_sl(kc, 0, 8),
                         cx.RT0[:, hb * 4096 + kc * 512: hb * 4096 + kc * 512 + tw],
                         start=(kc == 0), stop=(kc == KC - 1),
                         waits=[wr_tok, rhs_tok, ev] if kc == 0 else (),
                         inc=(kc == KC - 1))
        ev = bld.op("v", lambda v, t0=t0, tw=tw: v.tensor_scalar(
            lgrow[:, t0:t0 + tw], cx.PS[0:8, 0:tw], brc, None, ALU.add),
            waits=[mmt, brt])
    H2f = cx.H2.bitcast(F32)
    lg = H2f[:, 0:8]
    m1 = H2f[:, 8:9]
    m2 = H2f[:, 9:10]
    e1 = H2f[:, 10:18]
    e2 = H2f[:, 18:26]
    w1 = H2f[:, 26:27]
    t5 = H2f[:, 27:35]
    rwt = H2f[:, 35:43]
    prev_tb = prev_ev2 = None
    for i in range(T // 128):
        tf = bld.op("t", lambda t, i=i: t.transpose(
            cx.PS[0:128, 512:520], lgrow[:, i * 128:(i + 1) * 128], idf[0:8, 0:8]),
            waits=[ev, idt, prev_ev2], inc=True)
        evl = bld.op("v", lambda v: v.tensor_copy(lg, cx.PS[0:128, 512:520]),
                     waits=[tf, prev_tb])
        bld.op("v", lambda v: v.reduce_max(m1, lg, axis=AX.X))
        bld.op("v", lambda v: v.tensor_scalar(e1, lg, m1, None, ALU.is_equal))
        bld.op("v", lambda v: v.scalar_tensor_tensor(e2, e1, -1e30, lg, ALU.mult, ALU.add))
        bld.op("v", lambda v: v.reduce_max(m2, e2, axis=AX.X))
        bld.op("v", lambda v: v.tensor_scalar(e2, e2, m2, None, ALU.is_equal))
        vv = bld.op("v", lambda v: v.tensor_sub(w1, m1, m2))
        sg = bld.op("s", lambda s: s.activation(w1, w1, AF.Sigmoid), waits=[vv])
        bld.op("v", lambda v: v.tensor_scalar(rwt, e1, w1, None, ALU.mult), waits=[sg])
        bld.op("v", lambda v: v.tensor_scalar(t5, e2, w1, None, ALU.mult))
        bld.op("v", lambda v: v.tensor_sub(t5, e2, t5))
        ev2 = bld.op("v", lambda v: v.tensor_add(rwt, rwt, t5))
        prev_ev2 = ev2
        tb = bld.op("t", lambda t, i=i: t.transpose(
            cx.PS[0:8, 520:648], rwt, idf), waits=[ev2], inc=True)
        prev_tb = bld.op("v", lambda v, i=i: v.tensor_copy(
            lgrow[:, i * 128:(i + 1) * 128], cx.PS[0:8, 520:648]), waits=[tb])
    rwst = bld.dma("d", rw_dram[0:8, 0:T], lgrow, waits=[prev_tb], slot="rwst")
    acc_in = []
    for c in range(KC):
        acc_in.append(bld.dma("g", acc_dram[c * 128:(c + 1) * 128, :],
                              scr["xn"][c * 128:(c + 1) * 128, :], slot="acci"))
    bld.barrier()

    def ht_slice(c, tw):
        if c < 16:
            return cx.RT1[:, c * 512: c * 512 + tw]
        if c < 24:
            return cx.H1[:, (c - 16) * 512: (c - 16) * 512 + tw]
        return cx.H2[:, (c - 24) * 512: (c - 24) * 512 + tw]

    acc_last = [None, None]
    rhs_war = {}
    bank_war = {}
    rep_war = None
    for e in range(8):
        w1_sl, w1_tok = emit_weight_load(bld, cx, 0, wts["W1T"][e], D, 4096)
        w2_sl, w2_tok = emit_weight_load(bld, cx, 32768, wts["W2T"][e], 4096, 1024)
        b1c = cx.SM[:, 32:64]
        b2c = cx.SM[:, 64:72]
        b1t = bld.dma("d", b1c, wts["b1"][e].rearrange("(f p) one -> p (f one)", p=128),
                      waits=[bld.tok("v"), bld.tok("s")], slot="b1l",
                      allow_slow_non_contiguous=True)
        b2t = bld.dma("d", b2c, wts["b2"][e].rearrange("(f p) one -> p (f one)", p=128),
                      slot="b2l", allow_slow_non_contiguous=True)
        acc_first_wait = list(acc_last) if isinstance(acc_last, list) else [acc_last]
        for ti, (t0, tw) in enumerate(tts):
            hb = ti % 2
            rhs_tok = bld.dma(
                "d", bass.AP(cx.RT0, hb * 4096, [[8192, 128], [512, KC], [1, tw]]),
                scr["xn"][:, t0:t0 + tw].rearrange("(kc p) t -> p kc t", p=128),
                waits=[rhs_war.get(hb)], slot=f"rta{hb}")
            rep = cx.F1[:, 0:512]
            rpt = bld.dma("d", rep[:, 0:tw],
                          bass.AP(rw_dram.tensor, e * 4096 + t0, [[0, 128], [1, tw]]),
                          waits=[rwst, rep_war], slot="rwrep")
            for fm in range(32):
                bk = fm % 4
                w0 = [w1_tok, b1t, rhs_tok, bank_war.get(("A", bk))]
                for kc in range(KC):
                    tok = bld.mm(cx.bank(bk, tw), w1_sl(kc, fm),
                                 cx.RT0[:, hb * 4096 + kc * 512: hb * 4096 + kc * 512 + tw],
                                 start=(kc == 0), stop=(kc == KC - 1),
                                 waits=w0 if kc == 0 else (), inc=(kc == KC - 1))
                rhs_war[hb] = tok
                hsl = ht_slice(fm, tw)
                if fm % 2 == 0:
                    evh = bld.op("s", lambda s, hsl=hsl, bk=bk, fm=fm, tw=tw:
                                 s.activation(hsl, cx.bank(bk, tw), AF.Relu,
                                              bias=b1c[:, fm:fm + 1]),
                                 waits=[tok, bank_war.get(("h",))])
                else:
                    evh = bld.op("v", lambda v, hsl=hsl, bk=bk, fm=fm, tw=tw:
                                 v.tensor_scalar(hsl, cx.bank(bk, tw),
                                                 b1c[:, fm:fm + 1], 0.0,
                                                 ALU.add, ALU.max),
                                 waits=[tok, bank_war.get(("h",))])
                bank_war[("A", bk)] = evh
                hready = bank_war.setdefault(("hr",), {})
                hready[fm] = evh
            for dm in range(KC):
                bk = 4 + dm % 4
                w0 = [w2_tok, b2t, bank_war.get(("B", bk))]
                hready = bank_war[("hr",)]
                for kc in range(32):
                    tok = bld.mm(cx.bank(bk, tw), w2_sl(kc, dm),
                                 ht_slice(kc, tw),
                                 start=(kc == 0), stop=(kc == 31),
                                 waits=(w0 + [hready[kc]]) if kc == 0
                                 else [hready[kc]],
                                 inc=(kc == 31))
                stg = cx.F1[:, 512 + (dm % 2) * 512: 512 + (dm % 2) * 512 + tw]
                wo_ = bld.op("v", lambda v, stg=stg, bk=bk, dm=dm, tw=tw, rep=rep:
                             v.scalar_tensor_tensor(stg, cx.bank(bk, tw),
                                                    b2c[:, dm:dm + 1], rep[:, 0:tw],
                                                    ALU.add, ALU.mult),
                             waits=[tok, rpt, bank_war.get(("stg", dm % 2))])
                bank_war[("B", bk)] = wo_
                acc_last[dm % 2] = bld.dma(
                    "g", acc_dram[dm * 128:(dm + 1) * 128, t0:t0 + tw],
                    stg, waits=[wo_] + acc_first_wait + acc_in,
                    slot=f"accd{dm % 2}", accum_op=ALU.add)
                bank_war[("stg", dm % 2)] = acc_last[dm % 2]
            rep_war = bank_war[("B", 4 + (KC - 1) % 4)]
            bank_war[("h",)] = tok
    bld.barrier()

    gcol = cx.SM[:, 72:80]
    bcol = cx.SM[:, 80:88]
    g1 = load_bias_cols(bld, cx, 72, wts["gamma"], "gbl")
    g2 = load_bias_cols(bld, cx, 80, wts["beta"], "gbl")
    emit_ln(bld, cx, acc_dram, xout_hbm, gb=(gcol, bcol, g2), res_hbm=x_hbm)


def emit_mix(bld, cx, xn_hbm, mixes, T):
    xnc, out_b, t1 = cx.H1, cx.H2, cx.F1
    a3 = st = None
    for c in range(KC):
        z = bld.op("g", lambda g: g.memset(xnc[:, 0:1], 0.0),
                   waits=[a3] if a3 is not None else ())
        ld = bld.dma("d", xnc[:, 1:T + 1], xn_hbm[c * 128:(c + 1) * 128, :],
                     waits=[a3] if a3 is not None else (), slot="mixld")
        for (mcol, omcol, dst) in mixes:
            a1 = bld.op("v", lambda v, mcol=mcol, c=c: v.tensor_scalar(
                t1[:, 0:T], xnc[:, 0:T], cx.SM[:, mcol + c:mcol + c + 1], None,
                ALU.mult), waits=[ld, z])
            a3 = bld.op("v", lambda v, omcol=omcol, c=c: v.scalar_tensor_tensor(
                out_b[:, 0:T], xnc[:, 1:T + 1], cx.SM[:, omcol + c:omcol + c + 1],
                t1[:, 0:T], ALU.mult, ALU.add), waits=[st] if st is not None else ())
            st = bld.dma("d", dst[c * 128:(c + 1) * 128, :], out_b[:, 0:T],
                         waits=[a3], slot="mixst")
    bld.barrier()


def emit_l2norm_pass(bld, cx, s_hbm, T):
    nc = cx.nc
    nrm = cx.dram("nrm_dram", [16, 4224], F32)
    ac, sq = cx.H1, cx.H2
    ones2 = cx.SMb[:, 3448:3450]
    z0 = bld.op("g", lambda g: g.memset(ones2, 0.0))
    z1 = bld.op("g", lambda g: g.memset(
        bass.AP(cx.SMb, 3448, [[cx.SMb.shape[1], 64], [1, 1]]), 1.0))
    z2 = bld.op("g", lambda g: g.memset(
        bass.AP(cx.SMb, 64 * cx.SMb.shape[1] + 3449, [[cx.SMb.shape[1], 64], [1, 1]]), 1.0))
    st2 = cx.SM[0:2, 1024:1536]
    mmt = evn = None
    stt_ = None
    for c in range(KC):
        ld = bld.dma("d", ac[:, 0:T], s_hbm[c * 128:(c + 1) * 128, :],
                     waits=[mmt], slot="nrmld")
        sqt = bld.op("s", lambda s: s.activation(sq[:, 0:T], ac[:, 0:T], AF.Square),
                     waits=[ld, mmt])
        for (t0, tw) in ttiles(T):
            mmt = bld.mm(cx.PS[0:2, 0:tw], ones2, sq[:, t0:t0 + tw],
                         start=True, stop=True, waits=[sqt, z0, z1, z2, evn],
                         inc=True)
            evn = bld.op("v", lambda v, tw=tw: v.tensor_copy(st2[:, 0:tw],
                                                             cx.PS[0:2, 0:tw]),
                         waits=[mmt, stt_])
            stt_ = bld.dma("d", nrm[2 * c:2 * c + 2, t0:t0 + tw], st2[:, 0:tw],
                           waits=[evn], slot="nrmrow")
    ldn = bld.dma("d", cx.F1[0:16, 0:T], nrm[0:16, 0:T], waits=[stt_, bld.tok("v")],
                  slot="nrmld2")
    sq1 = bld.op("s", lambda s: s.activation(cx.F1[0:16, 0:T], cx.F1[0:16, 0:T],
                                             AF.Sqrt), waits=[ldn])
    mx = bld.op("v", lambda v: v.tensor_scalar_max(cx.F1[0:16, 0:T],
                                                   cx.F1[0:16, 0:T], 1e-6),
                waits=[sq1])
    rc = bld.op("v", lambda v: v.reciprocal(cx.F1[0:16, 0:T], cx.F1[0:16, 0:T]))
    sr = bld.dma("d", nrm[0:16, 0:T], cx.F1[0:16, 0:T], waits=[rc], slot="nrmrow")
    mul = stb = None
    for c in range(KC):
        ld = bld.dma("d", ac[:, 0:T], s_hbm[c * 128:(c + 1) * 128, :],
                     waits=[stb], slot="nrmld")
        r1 = bld.dma("d", cx.F1[0:64, 0:T],
                     bass.AP(nrm.tensor, 2 * c * 4224, [[0, 64], [1, T]]),
                     waits=[sr, mul, rc], slot="nrmrep")
        r2 = bld.dma("d", cx.F1[64:128, 0:T],
                     bass.AP(nrm.tensor, (2 * c + 1) * 4224, [[0, 64], [1, T]]),
                     slot="nrmrep")
        mul = bld.op("v", lambda v: v.tensor_tensor(ac[:, 0:T], ac[:, 0:T],
                                                    cx.F1[:, 0:T], ALU.mult),
                     waits=[ld, r1, r2])
        stb = bld.dma("d", s_hbm[c * 128:(c + 1) * 128, :], ac[:, 0:T], waits=[mul],
                      slot="nrmst2")
    bld.barrier()


DBG_AB = False
AB_WINDOW = True


def emit_l0ab(bld, cx, x_hbm, xout_hbm, wts, scr, cconsts):
    T = cx.T
    C = 32
    nc = cx.nc
    GN_EPS = 0.00064
    emit_ln(bld, cx, x_hbm, scr["xn"])

    names = ("x_r", "x_w", "x_k", "x_v", "x_a", "x_b", "x_g")
    mtoks = []
    for j, n in enumerate(names):
        mtoks.append(load_bias_cols(bld, cx, 88 + 8 * j, wts[n], "mixl"))
    for j in range(7):
        bld.op("v", lambda v, j=j: v.tensor_scalar(
            cx.SM[:, 144 + 8 * j:152 + 8 * j], cx.SM[:, 88 + 8 * j:96 + 8 * j],
            -1.0, 1.0, ALU.mult, ALU.add), waits=mtoks)
    bld.barrier()
    sm = {"r": scr["xr"], "w": cx.dram("ab_w", [D, cx.T], BF16),
          "k": scr["xk"], "v": scr["kact"],
          "a": cx.dram("ab_a", [D, cx.T], BF16),
          "b": cx.dram("ab_b", [D, cx.T], BF16),
          "g": cx.dram("ab_g", [D, cx.T], BF16)}
    emit_mix(bld, cx, scr["xn"],
             [(88 + 8 * j, 144 + 8 * j, sm[k]) for j, k in
              enumerate(("r", "w", "k", "v", "a", "b", "g"))], T)

    s_r = cx.dram("ab_sr", [D, cx.T], BF16)
    s_wl = cx.dram("ab_swl", [D, cx.T], F32)
    s_k = cx.dram("ab_sk", [D, cx.T], BF16)
    s_v = cx.dram("ab_sv", [D, cx.T], BF16)
    s_a = cx.dram("ab_sa", [D, cx.T], BF16)
    s_b = cx.dram("ab_sb", [D, cx.T], BF16)
    s_g = cx.dram("ab_sg", [D, cx.T], BF16)
    s_o = cx.dram("ab_so", [D, cx.T], BF16)

    def proj(src, wT, dst, act, fdst32=False):
        w_sl, w_tok = emit_weight_load(bld, cx, 0, wT, D, D)
        st_war = {}

        def ep(mc, t0, tw, pslice, tok, hb):
            sl = mc % 2
            if fdst32:
                ob = cx.H2.bitcast(F32)[:, sl * 512: sl * 512 + tw]
            else:
                ob = cx.H1[:, sl * 512: sl * 512 + tw]
            if act == "tanh":
                e1 = bld.op("s", lambda s, ob=ob, pslice=pslice: s.activation(
                    ob, pslice, AF.Tanh), waits=[tok, st_war.get(sl)])
            elif act == "sig":
                e1 = bld.op("s", lambda s, ob=ob, pslice=pslice: s.activation(
                    ob, pslice, AF.Sigmoid), waits=[tok, st_war.get(sl)])
            elif act == "wlog":
                tb = cx.F1[:, (2 + sl) * 512: (2 + sl) * 512 + tw]
                bld.op("s", lambda s, tb=tb, pslice=pslice: s.activation(
                    tb, pslice, AF.Sigmoid, scale=-1.0),
                    waits=[tok, st_war.get(sl)])
                e1 = bld.op("s", lambda s, ob=ob, tb=tb: s.activation(
                    ob, tb, AF.Ln))
            else:
                e1 = bld.op("v", lambda v, ob=ob, pslice=pslice: v.tensor_copy(
                    ob, pslice), waits=[tok, st_war.get(sl)])
            st_war[sl] = bld.dma("d", dst[mc * 128:(mc + 1) * 128, t0:t0 + tw],
                                 ob, waits=[e1], slot=f"pst{sl}")
            return e1
        emit_proj(bld, cx, w_sl, w_tok, src, D, ep, cx.RT0)

    proj(sm["r"], wts["W_rT"], s_r, None)
    proj(sm["w"], wts["W_wT"], s_wl, "wlog", fdst32=True)
    proj(sm["k"], wts["W_kT"], s_k, "tanh")
    proj(sm["v"], wts["W_vT"], s_v, None)
    proj(sm["a"], wts["W_aT"], s_a, None)
    proj(sm["b"], wts["W_bT"], s_b, None)
    proj(sm["g"], wts["W_gT"], s_g, "sig")
    emit_l2norm_pass(bld, cx, s_a, T)
    emit_l2norm_pass(bld, cx, s_b, T)

    s_Ab = cx.dram("ab_Ab", [D, cx.T], F32)
    s_BB = cx.dram("ab_BB", [D, cx.T], F32)
    s_Kb = cx.dram("ab_Kb", [D, cx.T], F32)
    s_rb = cx.dram("ab_rb", [D, cx.T], F32)
    s_eLC = cx.dram("ab_eLC", [D, cx.T // C], F32)
    msk = cx.H2
    z0 = bld.op("g", lambda g: g.memset(msk[:, 0:T], 1.0))
    z1 = bld.op("g", lambda g: g.memset(
        bass.AP(cx.H2, 0, [[cx.H2.shape[1], 128], [C, T // C]]), 0.0))
    R1f = cx.RT1.bitcast(F32)
    R0f = cx.RT0.bitcast(F32)
    prev = {}
    for c in range(KC):
        ld = bld.dma("d", cx.F1[:, 0:T], s_wl[c * 128:(c + 1) * 128, :],
                     waits=[prev.get("L")], slot="dwl")
        sc_ = bld.op("v", lambda v: v.tensor_tensor_scan(
            cx.F1[:, 0:T], msk[:, 0:T], cx.F1[:, 0:T], 0.0, ALU.mult, ALU.add),
            waits=[ld, z0, z1])
        e1 = bld.op("s", lambda s: s.activation(R1f[:, 0:T], cx.F1[:, 0:T], AF.Exp),
                    waits=[sc_, prev.get("eL")])
        ecx = bld.op("v", lambda v: v.tensor_copy(
            cx.SM[:, 1728:1728 + T // C],
            bass.AP(R1f, C - 1, [[R1f.shape[1], 128], [C, T // C]])),
            waits=[e1, prev.get("elc")])
        st0 = bld.dma("d", s_eLC[c * 128:(c + 1) * 128, :],
                      cx.SM[:, 1728:1728 + T // C], waits=[ecx], slot="elcst")
        prev["elc"] = st0
        ldr = bld.dma("d", cx.H1[:, 0:T], s_r[c * 128:(c + 1) * 128, :],
                      waits=[prev.get("H1")], slot="dr")
        rb_ = bld.op("v", lambda v: v.tensor_tensor(R0f[:, 0:T], cx.H1[:, 0:T],
                                                    R1f[:, 0:T], ALU.mult),
                     waits=[ldr, e1, prev.get("R0")])
        st1 = bld.dma("d", s_rb[c * 128:(c + 1) * 128, :], R0f[:, 0:T],
                      waits=[rb_], slot="strb")
        ldw = bld.dma("d", R0f[:, 0:T], s_wl[c * 128:(c + 1) * 128, :],
                      waits=[st1], slot="dwl2")
        lp = bld.op("v", lambda v: v.tensor_sub(R0f[:, 0:T], cx.F1[:, 0:T],
                                                R0f[:, 0:T]), waits=[ldw])
        ep_ = bld.op("s", lambda s: s.activation(R0f[:, 0:T], R0f[:, 0:T], AF.Exp),
                     waits=[lp])
        lda = bld.dma("d", cx.H1[:, 0:T], s_a[c * 128:(c + 1) * 128, :],
                      waits=[rb_], slot="dr")
        ab_ = bld.op("v", lambda v: v.tensor_tensor(R0f[:, 0:T], cx.H1[:, 0:T],
                                                    R0f[:, 0:T], ALU.mult),
                     waits=[lda, ep_])
        st2 = bld.dma("d", s_Ab[c * 128:(c + 1) * 128, :], R0f[:, 0:T],
                      waits=[ab_], slot="strb")
        em = bld.op("s", lambda s: s.activation(R1f[:, 0:T], cx.F1[:, 0:T], AF.Exp,
                                                scale=-1.0), waits=[rb_, st0])
        ldb = bld.dma("d", cx.H1[:, 0:T], s_b[c * 128:(c + 1) * 128, :],
                      waits=[ab_], slot="dr")
        bb_ = bld.op("v", lambda v: v.tensor_tensor(R0f[:, 0:T], cx.H1[:, 0:T],
                                                    R1f[:, 0:T], ALU.mult),
                     waits=[ldb, em, st2])
        st3 = bld.dma("d", s_BB[c * 128:(c + 1) * 128, :], R0f[:, 0:T],
                      waits=[bb_], slot="strb")
        ldk = bld.dma("d", cx.H1[:, 0:T], s_k[c * 128:(c + 1) * 128, :],
                      waits=[bb_], slot="dr")
        kb_ = bld.op("v", lambda v: v.tensor_tensor(R0f[:, 0:T], cx.H1[:, 0:T],
                                                    R1f[:, 0:T], ALU.mult),
                     waits=[ldk, st3])
        st4 = bld.dma("d", s_Kb[c * 128:(c + 1) * 128, :], R0f[:, 0:T],
                      waits=[kb_], slot="strb")
        prev["L"] = em
        prev["eL"] = kb_
        prev["H1"] = kb_
        prev["R0"] = st4
    bld.barrier()

    idf = cx.SM[:, 1536:1664]
    idt = bld.dma("g", idf, wts["identf"], slot="idf")
    mus = cx.SM[0:64, 1664:1792]
    mut = bld.dma("g", mus, cconsts["mu2"], slot="mu2")
    gnw_t = load_bias_cols(bld, cx, 200, wts["gn_w"], "gnl")
    gnb_t = load_bias_cols(bld, cx, 208, wts["gn_b"], "gnl")
    epsg, epsg_t = cx.const_col(bld, GN_EPS)

    F1f = cx.F1
    H1 = cx.H1
    H2f = cx.H2.bitcast(F32)
    R1uf = cx.RT1.bitcast(F32)
    RT0f = cx.RT0.bitcast(F32)
    NB = T // 512
    NCC = 512 // C
    SCB = [160, 1128]
    sw = {}
    bw = {}
    ld_war = {}
    go2 = {}
    for blk in range(NB):
        b0 = blk * 512
        for pg in range(4):
            lds = {}
            zi = {}
            for ln in range(2):
                pr = pg * 2 + ln
                l = []
                for (srcs, col, key) in ((s_Ab, 0, "ab"), (s_BB, 1024, "bb"),
                                         (s_Kb, 2048, "kb"), (s_rb, 3072, "rb")):
                    l.append(bld.dma(
                        "d", F1f[:, col + ln * 512: col + ln * 512 + 512],
                        srcs[pr * 128:(pr + 1) * 128, b0:b0 + 512],
                        waits=[ld_war.get((key, ln))], slot=f"u{key}{ln}"))
                l.append(bld.dma(
                    "d", H2f[:, ln * 16:(ln + 1) * 16],
                    s_eLC[pr * 128:(pr + 1) * 128, blk * 16:(blk + 1) * 16],
                    waits=[ld_war.get(("elc", ln))], slot=f"uelc{ln}"))
                l.append(bld.dma(
                    "d", H1[:, ln * 512:(ln + 1) * 512],
                    s_v[pr * 128:(pr + 1) * 128, b0:b0 + 512],
                    waits=[ld_war.get(("v", ln))], slot=f"uv{ln}"))
                l.append(bld.dma(
                    "d", H1[:, 1024 + ln * 512: 1536 + ln * 512],
                    s_g[pr * 128:(pr + 1) * 128, b0:b0 + 512],
                    waits=[ld_war.get(("g", ln))], slot=f"ug{ln}"))
                lds[ln] = l
                if blk == 0:
                    zi[ln] = bld.op("g", lambda g, pr=pr: g.memset(
                        RT0f[:, pr * 64:(pr + 1) * 64], 0.0))
                    sw[("zr", pr)] = zi[ln]
            def unit_gen(cc, ln):
                    pr = pg * 2 + ln
                    u0 = ln * 512 + cc * C
                    SC = SCB[ln]
                    SPf = H2f if cc % 2 == 0 else R1uf
                    BK = ln * 4 * 512
                    Ab = F1f[:, u0:u0 + C]
                    BB = F1f[:, 1024 + u0:1024 + u0 + C]
                    Kb = F1f[:, 2048 + u0:2048 + u0 + C]
                    rb_t = F1f[:, 3072 + u0:3072 + u0 + C]
                    Vt = H1[:, u0:u0 + C]
                    Gt = H1[:, 1024 + u0:1024 + u0 + C]
                    outacc = H1[:, 2048 + ln * 512 + cc * C: 2048 + ln * 512 + cc * C + C]
                    eLC_col = H2f[:, ln * 16 + cc: ln * 16 + cc + 1]
                    Z = RT0f[:, pr * 64:(pr + 1) * 64]
                    cmc = SPf[0:64, SC:SC + 128]
                    CmT = cmc[:, 0:32]
                    CKT = cmc[:, 32:64]
                    GT = cmc[:, 64:96]
                    PT = cmc[:, 96:128]
                    Cm = SPf[0:64, SC + 128:SC + 160]
                    pw_s = SPf[0:64, SC + 160:SC + 192]
                    pwT_s = SPf[0:64, SC + 192:SC + 224]
                    UV = SPf[0:64, SC + 224:SC + 352]
                    V_tm = SPf[0:64, SC + 352:SC + 416]
                    Y_sb = SPf[0:64, SC + 416:SC + 480]
                    Ukm = SPf[:, SC + 480:SC + 512]
                    Qkm = SPf[:, SC + 512:SC + 544]
                    B_tm = SPf[0:64, SC + 544:SC + 608]
                    Kb_tm = SPf[0:64, SC + 608:SC + 672]
                    O_sb = SPf[0:64, SC + 672:SC + 736]
                    t64 = SPf[:, SC + 736:SC + 800]
                    gnc = SPf[0:64, SC + 800:SC + 808]
                    sqo = SPf[0:64, SC + 808:SC + 872]
                    gst = SPf[:, SC + 872:SC + 904]
                    Onf = SPf[0:64, SC + 904:SC + 968]
                    w8 = lds[ln] + [mut, idt]
                    mm1 = bld.mm(cx.PS[0:32, BK:BK + C], BB[0:64, :], Ab[0:64, :],
                                 start=True, stop=False,
                                 waits=[bw.get((0, ln))] + w8)
                    bld.mm(cx.PS[32:64, BK:BK + C], BB[64:128, :], Ab[64:128, :],
                           start=True, stop=False)
                    bld.mm(cx.PS[0:32, BK + 32:BK + 64], Kb[0:64, :], Ab[0:64, :],
                           start=False, stop=False)
                    bld.mm(cx.PS[32:64, BK + 32:BK + 64], Kb[64:128, :],
                           Ab[64:128, :], start=False, stop=False)
                    bld.mm(cx.PS[0:32, BK + 64:BK + 96], BB[0:64, :], rb_t[0:64, :],
                           start=False, stop=False)
                    bld.mm(cx.PS[32:64, BK + 64:BK + 96], BB[64:128, :],
                           rb_t[64:128, :], start=False, stop=False)
                    bld.mm(cx.PS[0:32, BK + 96:BK + 128], Kb[0:64, :], rb_t[0:64, :],
                           start=False, stop=True)
                    m2 = bld.mm(cx.PS[32:64, BK + 96:BK + 128], Kb[64:128, :],
                                rb_t[64:128, :], start=False, stop=True, inc=True)
                    ec = bld.op("v", lambda v, cmc=cmc, BK=BK: v.tensor_tensor(
                        cmc, cx.PS[0:64, BK:BK + 128], mus, ALU.mult),
                        waits=[m2])
                    bw[(0, ln)] = ec
                    yield
                    tr1 = bld.op("v", lambda v, Cm=Cm, CmT=CmT: v.transpose(Cm, CmT))
                    yield
                    tv1 = bld.mm(cx.PS[0:32, BK + 1024:BK + 1024 + 64], Vt[0:64, :],
                                 cx.ident[0:64, 0:64], start=True, stop=True,
                                 waits=[bw.get((2, ln)), bw.get((2, ln, "o"))])
                    tv2 = bld.mm(cx.PS[32:64, BK + 1024:BK + 1024 + 64],
                                 Vt[64:128, :], cx.ident[64:128, 64:128],
                                 start=True, stop=True, inc=True)
                    e3 = bld.op("s", lambda s, V_tm=V_tm, BK=BK: s.activation(
                        V_tm, cx.PS[0:64, BK + 1024:BK + 1024 + 64], AF.Copy),
                        waits=[tv2])
                    bw[(2, ln)] = e3
                    yield
                    ta1 = bld.mm(cx.PS[0:32, BK + 512:BK + 512 + 64], Ab[0:64, :],
                                 idf[0:64, 0:64], start=True, stop=False,
                                 waits=[bw.get((1, ln))])
                    ta2 = bld.mm(cx.PS[32:64, BK + 512:BK + 512 + 64], Ab[64:128, :],
                                 idf[64:128, 64:128], start=True, stop=False)
                    m5a = bld.mm(cx.PS[0:32, BK + 512 + 64:BK + 512 + 128],
                                 CKT[0:32, :], V_tm[0:32, :], start=False,
                                 stop=True, waits=[ec, e3])
                    m5b = bld.mm(cx.PS[32:64, BK + 512 + 64:BK + 512 + 128],
                                 CKT[32:64, :], V_tm[32:64, :], start=False,
                                 stop=True, inc=True)
                    ev_uv = bld.op("v", lambda v, UV=UV, BK=BK: v.tensor_copy(
                        UV, cx.PS[0:64, BK + 512:BK + 512 + 128]),
                        waits=[m5b, tr1])
                    bw[(1, ln)] = ev_uv
                    yield
                    pw_cur, pwT_cur = Cm, CmT
                    trp = None
                    for it in range(5):
                        rg = BK + 512 + 128 + (it % 2) * 128
                        mna = bld.mm(cx.PS[0:32, rg:rg + 128], pwT_cur[0:32, :],
                                     UV[0:32, :], start=True, stop=True,
                                     waits=[ev_uv, trp, bw.get((1, ln, it % 2))])
                        mnb = bld.mm(cx.PS[32:64, rg:rg + 128], pwT_cur[32:64, :],
                                     UV[32:64, :], start=True, stop=True, inc=True)
                        ev_uv = bld.op("v", lambda v, UV=UV, rg=rg:
                                       v.scalar_tensor_tensor(
                                           UV, cx.PS[0:64, rg:rg + 128], 1.0, UV,
                                           ALU.mult, ALU.add), waits=[mnb])
                        bw[(1, ln, it % 2)] = ev_uv
                        if it < 4:
                            rp = BK + 512 + 384
                            mpa = bld.mm(cx.PS[0:32, rp:rp + 32], pwT_cur[0:32, :],
                                         pw_cur[0:32, :], start=True, stop=True,
                                         waits=[bw.get((1, ln, "p")), ev_uv])
                            mpb = bld.mm(cx.PS[32:64, rp:rp + 32],
                                         pwT_cur[32:64, :], pw_cur[32:64, :],
                                         start=True, stop=True, inc=True)
                            epw = bld.op("v", lambda v, pw_s=pw_s, rp=rp:
                                         v.tensor_copy(pw_s, cx.PS[0:64, rp:rp + 32]),
                                         waits=[mpb])
                            bw[(1, ln, "p")] = epw
                            trp = bld.op("v", lambda v, pwT_s=pwT_s, pw_s=pw_s:
                                         v.transpose(pwT_s, pw_s))
                            pw_cur, pwT_cur = pw_s, pwT_s
                        yield
                    bo_ = BK + 1536
                    bld.mm(cx.PS[0:32, bo_:bo_ + 64], PT[0:32, :], V_tm[0:32, :],
                           start=True, stop=False,
                           waits=[ec, bw.get((3, ln)), bw.get((3, ln, "z"))])
                    bld.mm(cx.PS[32:64, bo_:bo_ + 64], PT[32:64, :], V_tm[32:64, :],
                           start=True, stop=False)
                    bld.mm(cx.PS[0:32, bo_:bo_ + 64], GT[0:32, :], UV[0:32, 64:128],
                           start=False, stop=False, waits=[ev_uv])
                    bld.mm(cx.PS[32:64, bo_:bo_ + 64], GT[32:64, :],
                           UV[32:64, 64:128], start=False, stop=False)
                    bq = BK + 1024 + 64
                    bld.mm(cx.PS[0:64, bq:bq + 32], UV[0:32, 0:64], GT[0:32, :],
                           start=True, stop=False, waits=[ev_uv, bw.get((2, ln, "q"))])
                    bld.mm(cx.PS[64:128, bq:bq + 32], UV[32:64, 0:64], GT[32:64, :],
                           start=True, stop=False)
                    bld.mm(cx.PS[0:64, bq:bq + 32], idf[0:64, 0:64], rb_t[0:64, :],
                           start=False, stop=True)
                    mq = bld.mm(cx.PS[64:128, bq:bq + 32], idf[64:128, 64:128],
                                rb_t[64:128, :], start=False, stop=True, inc=True)
                    e4 = bld.op("s", lambda s, Qkm=Qkm, bq=bq: s.activation(
                        Qkm, cx.PS[:, bq:bq + 32], AF.Copy), waits=[mq])
                    bw[(2, ln, "q")] = e4
                    yield
                    mo1 = bld.mm(cx.PS[0:32, bo_:bo_ + 64], Qkm[0:64, :], Z[0:64, :],
                                 start=False, stop=True,
                                 waits=[e4, sw.get(("zr", pr))])
                    mo2 = bld.mm(cx.PS[32:64, bo_:bo_ + 64], Qkm[64:128, :],
                                 Z[64:128, :], start=False, stop=True, inc=True)
                    e5 = bld.op("s", lambda s, O_sb=O_sb, bo_=bo_: s.activation(
                        O_sb, cx.PS[0:64, bo_:bo_ + 64], AF.Copy), waits=[mo2])
                    bw[(3, ln)] = e5
                    yield
                    bt_ = BK + 1024 + 96
                    tu1 = bld.mm(cx.PS[0:64, bt_:bt_ + 32], UV[0:32, 0:64],
                                 idf[0:32, 0:32], start=True, stop=True,
                                 waits=[ev_uv, bw.get((2, ln, "t"))])
                    tu2 = bld.mm(cx.PS[64:128, bt_:bt_ + 32], UV[32:64, 0:64],
                                 idf[32:64, 32:64], start=True, stop=True)
                    tb1 = bld.mm(cx.PS[0:32, bt_ + 32:bt_ + 96], BB[0:64, :],
                                 idf[0:64, 0:64], start=True, stop=True)
                    tb2 = bld.mm(cx.PS[32:64, bt_ + 32:bt_ + 96], BB[64:128, :],
                                 idf[64:128, 64:128], start=True, stop=True)
                    tk1 = bld.mm(cx.PS[0:32, bt_ + 96:bt_ + 160], Kb[0:64, :],
                                 idf[0:64, 0:64], start=True, stop=True)
                    tk2 = bld.mm(cx.PS[32:64, bt_ + 96:bt_ + 160], Kb[64:128, :],
                                 idf[64:128, 64:128], start=True, stop=True,
                                 inc=True)
                    e6 = bld.op("s", lambda s, Ukm=Ukm, bt_=bt_: s.activation(
                        Ukm, cx.PS[:, bt_:bt_ + 32], AF.Copy), waits=[tk2])
                    e7 = bld.op("s", lambda s, B_tm=B_tm, bt_=bt_: s.activation(
                        B_tm, cx.PS[0:64, bt_ + 32:bt_ + 96], AF.Copy))
                    e8 = bld.op("s", lambda s, Kb_tm=Kb_tm, bt_=bt_: s.activation(
                        Kb_tm, cx.PS[0:64, bt_ + 96:bt_ + 160], AF.Copy))
                    bw[(2, ln, "t")] = e8
                    yield
                    by = BK + 1024 + 288
                    my1 = bld.mm(cx.PS[0:32, by:by + 64], Ukm[0:64, :], Z[0:64, :],
                                 start=True, stop=False,
                                 waits=[e8, bw.get((2, ln, "y")), sw.get(("zr", pr))])
                    my2 = bld.mm(cx.PS[32:64, by:by + 64], Ukm[64:128, :],
                                 Z[64:128, :], start=True, stop=False)
                    my3 = bld.mm(cx.PS[0:32, by:by + 64], idf[0:32, 0:32],
                                 UV[0:32, 64:128], start=False, stop=True)
                    my4 = bld.mm(cx.PS[32:64, by:by + 64], idf[32:64, 32:64],
                                 UV[32:64, 64:128], start=False, stop=True,
                                 inc=True)
                    e9 = bld.op("s", lambda s, Y_sb=Y_sb, by=by: s.activation(
                        Y_sb, cx.PS[0:64, by:by + 64], AF.Copy), waits=[my4])
                    bw[(2, ln, "y")] = e9
                    yield
                    bz = bo_ + 64
                    mz1 = bld.mm(cx.PS[0:64, bz:bz + 64], B_tm[0:32, :],
                                 Y_sb[0:32, :], start=True, stop=False,
                                 waits=[e7, e9, bw.get((3, ln, "z"))])
                    mz2 = bld.mm(cx.PS[64:128, bz:bz + 64], B_tm[32:64, :],
                                 Y_sb[32:64, :], start=True, stop=False)
                    mz3 = bld.mm(cx.PS[0:64, bz:bz + 64], Kb_tm[0:32, :],
                                 V_tm[0:32, :], start=False, stop=True, waits=[e8])
                    mz4 = bld.mm(cx.PS[64:128, bz:bz + 64], Kb_tm[32:64, :],
                                 V_tm[32:64, :], start=False, stop=True, inc=True)
                    e10 = bld.op("s", lambda s, t64=t64, bz=bz: s.activation(
                        t64, cx.PS[:, bz:bz + 64], AF.Copy), waits=[mz4])
                    bw[(3, ln, "z")] = e10
                    yield
                    z1_ = bld.op("v", lambda v, t64=t64, Z=Z: v.tensor_add(
                        t64, t64, Z), waits=[mo2, my2, e10])
                    zr = bld.op("v", lambda v, Z=Z, t64=t64, eLC_col=eLC_col:
                                v.tensor_scalar(Z, t64, eLC_col, None, ALU.mult),
                                waits=[lds[ln][4]])
                    sw[("zr", pr)] = zr
                    bld.op("v", lambda v, gnc=gnc, O_sb=O_sb: v.reduce_sum(
                        gnc[:, 0:1], O_sb, axis=AX.X), waits=[e5])
                    bld.op("v", lambda v, gnc=gnc: v.tensor_scalar_mul(
                        gnc[:, 0:1], gnc[:, 0:1], 1.0 / 64))
                    g3_ = bld.op("s", lambda s, sqo=sqo, O_sb=O_sb, gnc=gnc:
                                 s.activation(sqo, O_sb, AF.Square,
                                              accum_out=gnc[:, 1:2]),
                                 waits=[e5, bw.get(("gn", ln))])
                    bld.op("v", lambda v, gnc=gnc: v.tensor_scalar_mul(
                        gnc[:, 1:2], gnc[:, 1:2], 1.0 / 64), waits=[g3_])
                    bld.op("v", lambda v, gnc=gnc: v.tensor_tensor(
                        gnc[:, 2:3], gnc[:, 0:1], gnc[:, 0:1], ALU.mult))
                    g6_ = bld.op("v", lambda v, gnc=gnc: v.tensor_sub(
                        gnc[:, 1:2], gnc[:, 1:2], gnc[:, 2:3]))
                    g7_ = bld.op("s", lambda s, gnc=gnc: s.activation(
                        gnc[:, 2:3], gnc[:, 1:2], AF.Ln, bias=epsg[0:64, 0:1]),
                        waits=[g6_, epsg_t])
                    g8_ = bld.op("s", lambda s, gnc=gnc: s.activation(
                        gnc[:, 2:3], gnc[:, 2:3], AF.Exp, scale=-0.5))
                    bw[("gn", ln)] = g8_
                    yield
                    g9_ = bld.op("v", lambda v, Onf=Onf, O_sb=O_sb, gnc=gnc:
                                 v.tensor_scalar(Onf, O_sb, gnc[:, 0:1],
                                                 gnc[:, 2:3], ALU.subtract,
                                                 ALU.mult),
                                 waits=[g8_, bw.get(("onb", ln))])
                    bon = BK + 1024 + 352
                    to1 = bld.mm(cx.PS[0:64, bon:bon + 32], Onf[0:32, :],
                                 idf[0:32, 0:32], start=True, stop=True,
                                 waits=[g9_, bw.get((2, ln, "o"))])
                    to2 = bld.mm(cx.PS[64:128, bon:bon + 32],
                                 Onf[32:64, :], idf[32:64, 32:64],
                                 start=True, stop=True, inc=True)
                    bw[("onb", ln)] = to2
                    go1 = bld.op("v", lambda v, gst=gst, bon=bon, pr=pr:
                                 v.tensor_scalar(gst, cx.PS[:, bon:bon + 32],
                                                 cx.SM[:, 200 + pr:201 + pr],
                                                 cx.SM[:, 208 + pr:209 + pr],
                                                 ALU.mult, ALU.add),
                                 waits=[to2, gnw_t, gnb_t, z1_])
                    bw[(2, ln, "o")] = go1
                    yield
                    go2[ln] = bld.op("v", lambda v, outacc=outacc, gst=gst, Gt=Gt:
                                     v.tensor_tensor(outacc, gst, Gt, ALU.mult),
                                     waits=[go2.get(ln), ld_war.get(("oa", ln))])
            if AB_WINDOW:
                KOFF = 11
                fl = {0: [], 1: []}
                nxt_cc = {0: 0, 1: 0}

                def pump(ln):
                    if (nxt_cc[ln] < NCC and len(fl[ln]) < 2
                            and (not fl[ln] or fl[ln][-1][1] >= KOFF)):
                        fl[ln].append([unit_gen(nxt_cc[ln], ln), 0])
                        nxt_cc[ln] += 1

                while True:
                    pump(0)
                    pump(1)
                    order = []
                    for depth in (0, 1):
                        for ln in (0, 1):
                            if depth < len(fl[ln]):
                                order.append((ln, fl[ln][depth]))
                    if not order:
                        break
                    fin = []
                    for ln, ent in order:
                        try:
                            next(ent[0])
                            ent[1] += 1
                        except StopIteration:
                            fin.append((ln, ent))
                    for ln, ent in fin:
                        fl[ln].remove(ent)
            else:
                for cc in range(NCC):
                    gens = [unit_gen(cc, 0), unit_gen(cc, 1)]
                    while gens:
                        nxt = []
                        for g in gens:
                            try:
                                next(g)
                                nxt.append(g)
                            except StopIteration:
                                pass
                        gens = nxt
            for ln in range(2):
                pr = pg * 2 + ln
                oa = bld.dma("d", s_o[pr * 128:(pr + 1) * 128, b0:b0 + 512],
                             H1[:, 2048 + ln * 512: 2560 + ln * 512],
                             waits=[go2[ln]], slot=f"oast{ln}")
                ld_war[("oa", ln)] = oa
                for key in ("ab", "bb", "kb", "rb"):
                    ld_war[(key, ln)] = bld.tok("t")
                ld_war[("elc", ln)] = sw[("zr", pr)]
                ld_war[("v", ln)] = bld.tok("t")
                ld_war[("g", ln)] = go2[ln]
    bld.barrier()

    wo_sl, wo_tok = emit_weight_load(bld, cx, 0, wts["W_oT"], D, D)
    bo_tok = load_bias_cols(bld, cx, 232, wts["b_o"], "bol")
    ep = make_res_epilogue(bld, cx, x_hbm, xout_hbm, 232, bo_tok)
    emit_proj(bld, cx, wo_sl, wo_tok, s_o, D, ep, cx.RT0)



def build_program(T=4096, chain=("l1ffn",)):
    nc = bass.Bass("TRN2", target_bir_lowering=False)
    bld = Builder(nc)
    cx = Ctx(nc, T)

    x_in = nc.dram_tensor("x_in", [D, T], F32, kind="ExternalInput").ap()
    y_out = nc.dram_tensor("y_out", [D, T], F32, kind="ExternalOutput").ap()
    ones_in = nc.dram_tensor("c_one", [1, 128], F32, kind="ExternalInput").ap()
    ident_in = nc.dram_tensor("c_ident", [128, 128], F32, kind="ExternalInput").ap()
    emit_consts(bld, cx, ones_in, ident_in)

    scr = {}
    for nm, dt in (("xn", BF16), ("xr", BF16), ("xk", BF16), ("kact", BF16),
                   ("b1", BF16), ("xa", F32), ("xb2", F32)):
        scr[nm] = nc.dram_tensor("scr_" + nm, [D, T], dt, kind="Internal").ap()
    cconsts = {}
    if any(s == "l1aa" for s in chain):
        cconsts["onehotT"] = nc.dram_tensor("c_onehotT", [32, 8192], F32,
                                            kind="ExternalInput").ap()
    if any(s == "l1ab" for s in chain):
        cconsts["cos"] = nc.dram_tensor("c_cos", [512, T], F32, kind="ExternalInput").ap()
        cconsts["sin"] = nc.dram_tensor("c_sin", [512, T], F32, kind="ExternalInput").ap()
        cconsts["mask"] = nc.dram_tensor("c_mask", [1, 8192], F32, kind="ExternalInput").ap()

    wcache = {}
    def win(name, shape):
        if name not in wcache:
            wcache[name] = nc.dram_tensor(name, list(shape), F32,
                                          kind="ExternalInput").ap()
        return wcache[name]

    cur = x_in
    ping = scr["xa"]
    pong = scr["xb2"]
    for i, sl in enumerate(chain):
        dst = y_out if i == len(chain) - 1 else ping
        if sl in ("l1ffn", "l1ffn2"):
            wts = {
                "WrT": win("l1ffn_WrT", (D, D)),
                "WkT": win("l1ffn_WkT", (D, D)),
                "WvT": win("l1ffn_WvT", (D, D)),
                "mu_r": win("l1ffn_mu_r", (D, 1)),
                "mu_k": win("l1ffn_mu_k", (D, 1)),
            }
            emit_l1ffn(bld, cx, cur, dst, wts, scr)
        elif sl == "l0aa":
            wts = {
                "WiT": win("l0aa_WiT", (D, D)),
                "WgT": win("l0aa_WgT", (D, D)),
                "WoT": win("l0aa_WoT", (D, D)),
                "bi": win("l0aa_bi", (D, 1)),
                "bg": win("l0aa_bg", (D, 1)),
                "bo": win("l0aa_bo", (D, 1)),
            }
            emit_l0aa(bld, cx, cur, dst, wts, scr)
        elif sl == "l1ab":
            wts = {n: win("l1ab_" + n, (D, D) if n.startswith("W") else (D, 1))
                   for n in ("WqT", "WkT", "WvT", "WoT", "bq", "bk", "bv", "bo")}
            emit_l1ab(bld, cx, cur, dst, wts, scr, cconsts)
        elif sl == "l1aa":
            wts = {n: win("l1aa_" + n, (D, D)) for n in ("Wq", "Wk", "Wv", "Wo")}
            wts["rel_emb"] = win("l1aa_rel_emb", (32, 16))
            emit_l1aa(bld, cx, cur, dst, wts, scr, cconsts)
        elif sl in ("l0ffn", "l0ffn2"):
            w1t = win("l0ffn_W1T", (8 * D, 4096))
            w2t = win("l0ffn_W2T", (8 * 4096, D))
            b1 = win("l0ffn_b1", (8 * 4096, 1))
            b2 = win("l0ffn_b2", (8 * D, 1))
            wts = {
                "WrT": win("l0ffn_WrT", (D, 8)),
                "br": win("l0ffn_br", (8, 1)),
                "W1T": [w1t[e * D:(e + 1) * D, :] for e in range(8)],
                "W2T": [w2t[e * 4096:(e + 1) * 4096, :] for e in range(8)],
                "b1": [b1[e * 4096:(e + 1) * 4096, :] for e in range(8)],
                "b2": [b2[e * D:(e + 1) * D, :] for e in range(8)],
                "gamma": win("l0ffn_gamma", (D, 1)),
                "beta": win("l0ffn_beta", (D, 1)),
                "identf": win("c_identf", (128, 128)),
            }
            emit_l0ffn(bld, cx, cur, dst, wts, scr)
        elif sl == "l0ab":
            wts = {}
            for n in ("W_r", "W_w", "W_k", "W_v", "W_a", "W_b", "W_g", "W_o"):
                wts[n + "T"] = win("l0ab_" + n + "T", (D, D))
            for n in ("x_r", "x_w", "x_k", "x_v", "x_a", "x_b", "x_g",
                      "b_o", "gn_w", "gn_b"):
                wts[n] = win("l0ab_" + n, (D, 1))
            wts["identf"] = win("c_identf", (128, 128))
            ab_consts = {"mu2": win("c_mu2", (64, 128))}
            emit_l0ab(bld, cx, cur, dst, wts, scr, ab_consts)
        else:
            raise ValueError(sl)
        cur = dst
        ping, pong = pong, ping

    bld.finalize()
    return nc



def _rp_bucket_np(rel, num_buckets=32, max_distance=128):
    nb = num_buckets // 2
    n = -rel
    ret = (n < 0).astype(np.int64) * nb
    n = np.abs(n)
    max_exact = nb // 2
    is_small = n < max_exact
    vl = max_exact + (np.log(np.maximum(n, 1).astype(np.float64) / max_exact)
                      / math.log(max_distance / max_exact) * (nb - max_exact)).astype(np.int64)
    vl = np.minimum(vl, nb - 1)
    return ret + np.where(is_small, n, vl)


def prep_core_inputs(d, T, chain, core):
    x = d["x"][core]
    m = {
        "x_in": np.ascontiguousarray(x.T).astype(np.float32),
        "c_one": np.ones((1, 128), np.float32),
        "c_ident": np.eye(128, dtype=np.float32),
    }
    for sl in chain:
        if sl in ("l1ffn", "l1ffn2"):
            m["l1ffn_WrT"] = np.ascontiguousarray(d["l1ffn_W_r"].T)
            m["l1ffn_WkT"] = np.ascontiguousarray(d["l1ffn_W_k"].T)
            m["l1ffn_WvT"] = np.ascontiguousarray(d["l1ffn_W_v"].T)
            m["l1ffn_mu_r"] = d["l1ffn_mu_r"].reshape(D, 1)
            m["l1ffn_mu_k"] = d["l1ffn_mu_k"].reshape(D, 1)
        elif sl == "l1ab":
            m["l1ab_WqT"] = np.ascontiguousarray(d["l1ab_Wq"].T)
            m["l1ab_WkT"] = np.ascontiguousarray(d["l1ab_Wk"].T)
            m["l1ab_WvT"] = np.ascontiguousarray(d["l1ab_Wv"].T)
            m["l1ab_WoT"] = np.ascontiguousarray(d["l1ab_Wo"].T)
            for n in ("bq", "bk", "bv", "bo"):
                m[f"l1ab_{n}"] = d[f"l1ab_{n}"].reshape(D, 1)
            half = 512
            inv = 1.0 / (10000.0 ** (np.arange(half, dtype=np.float64) / half))
            ang = np.arange(T, dtype=np.float64)[:, None] * inv[None, :]
            m["c_cos"] = np.ascontiguousarray(np.cos(ang).T.astype(np.float32))
            m["c_sin"] = np.ascontiguousarray(np.sin(ang).T.astype(np.float32))
            mrow = np.zeros((1, 8192), np.float32)
            j = np.arange(8192) - 4096
            mrow[0, (j >= -512) & (j <= 0)] = 1.0
            m["c_mask"] = mrow
        elif sl == "l1aa":
            for n in ("Wq", "Wk", "Wv", "Wo"):
                m[f"l1aa_{n}"] = np.ascontiguousarray(d[f"l1aa_{n}"])
            m["l1aa_rel_emb"] = d["l1aa_rel_emb"]
            idxs = np.arange(8192) - 4095
            bk = _rp_bucket_np(idxs)
            oh = np.zeros((32, 8192), np.float32)
            oh[bk, np.arange(8192)] = 1.0
            m["c_onehotT"] = oh
        elif sl in ("l0ffn", "l0ffn2"):
            m["l0ffn_WrT"] = np.ascontiguousarray(d["l0ffn_Wr"].T)
            m["l0ffn_br"] = d["l0ffn_br"].reshape(8, 1)
            m["l0ffn_W1T"] = np.ascontiguousarray(
                d["l0ffn_W1"].transpose(0, 2, 1).reshape(8 * D, 4096))
            m["l0ffn_W2T"] = np.ascontiguousarray(
                d["l0ffn_W2"].transpose(0, 2, 1).reshape(8 * 4096, D))
            m["l0ffn_b1"] = d["l0ffn_b1"].reshape(8 * 4096, 1)
            m["l0ffn_b2"] = d["l0ffn_b2"].reshape(8 * D, 1)
            m["l0ffn_gamma"] = d["l0ffn_gamma"].reshape(D, 1)
            m["l0ffn_beta"] = d["l0ffn_beta"].reshape(D, 1)
            m["c_identf"] = np.eye(128, dtype=np.float32)
        elif sl == "l0ab":
            for n in ("W_r", "W_w", "W_k", "W_v", "W_a", "W_b", "W_g", "W_o"):
                m["l0ab_" + n + "T"] = np.ascontiguousarray(d["l0ab_" + n].T)
            for n in ("x_r", "x_w", "x_k", "x_v", "x_a", "x_b", "x_g",
                      "b_o", "gn_w", "gn_b"):
                m["l0ab_" + n] = d["l0ab_" + n].reshape(D, 1)
            m["c_identf"] = np.eye(128, dtype=np.float32)
            st_ = np.triu(np.ones((32, 32), np.float32), 1)
            inc_ = np.triu(np.ones((32, 32), np.float32), 0)
            blkS = np.concatenate([st_, st_], axis=0)
            blkI = np.concatenate([inc_, inc_], axis=0)
            m["c_mu2"] = np.concatenate([blkS, blkS, blkI, blkI], axis=1)
        elif sl == "l0aa":
            m["l0aa_WiT"] = np.ascontiguousarray(d["l0aa_Wi"].T)
            m["l0aa_WgT"] = np.ascontiguousarray(d["l0aa_Wg"].T)
            m["l0aa_WoT"] = np.ascontiguousarray(d["l0aa_Wo"].T)
            m["l0aa_bi"] = d["l0aa_bi"].reshape(D, 1)
            m["l0aa_bg"] = d["l0aa_bg"].reshape(D, 1)
            m["l0aa_bo"] = d["l0aa_bo"].reshape(D, 1)
    return m


FULL_CHAIN = ("l0aa", "l0ffn", "l0ab", "l0ffn2", "l1aa", "l1ffn", "l1ab",
              "l1ffn2")

_H2D = {
    "L0_attn_a_": "l0aa_", "L0_attn_b_": "l0ab_", "L0_ffn_": "l0ffn_",
    "L1_attn_a_": "l1aa_", "L1_attn_b_": "l1ab_", "L1_ffn_": "l1ffn_",
}


def kernel(**inputs):
    d = {}
    for k, v in inputs.items():
        for hp, dp in _H2D.items():
            if k.startswith(hp):
                d[dp + k[len(hp):]] = np.asarray(v)
                break
    d["x"] = np.asarray(inputs["L0_attn_a_x"])
    T = d["x"].shape[1]
    chain = FULL_CHAIN
    nc = build_program(T=T, chain=chain)
    base = prep_core_inputs(d, T, chain, 0)
    in_maps = []
    for c in range(NCORES):
        m = dict(base)
        m["x_in"] = np.ascontiguousarray(d["x"][c].T).astype(np.float32)
        in_maps.append(m)
    res = run_bass_kernel_spmd(nc, in_maps, core_ids=list(range(NCORES)))
    out = np.stack([np.asarray(r["y_out"]).T for r in res.results], axis=0)
    return out.astype(np.float32)
